# version 02
# baseline (speedup 1.0000x reference)
import math
import numpy as np
import ml_dtypes
from contextlib import ExitStack

import concourse.bass as bass
import concourse.mybir as mybir
from concourse.bass_utils import run_bass_kernel_spmd

F32 = mybir.dt.float32
BF16 = mybir.dt.bfloat16
AF = mybir.ActivationFunctionType
AX = mybir.AxisListType
OP = mybir.AluOpType

S = 2048
H = 2048
NH = 16
HD = 128
NE = 32
DI = 768
NCHUNK = 2
TW = S // NCHUNK
EPS = 1e-5
ISQ = 1.0 / math.sqrt(HD)
NHT = H // 128
NIT = DI // 128
NG = 4
GH = NH // NG
EG = 4
NEG = -1e9
import os as _os
XBAR = int(_os.environ.get('KBAR', '0'))


class Plan:

    ENGINES = ("tensor", "vector", "scalar", "gpsimd", "sync")

    def __init__(self, nc, ctx):
        self.nc = nc
        self.ctx = ctx
        self.q = {e: [] for e in self.ENGINES}
        self.ctr = {}
        self.sem = {}
        self.step = {}
        self.lastw = {}
        for name in ("pe", "dve", "act", "pool"):
            self._mk(name, 1)

    def _mk(self, name, step):
        self.sem[name] = self.ctx.enter_context(self.nc.semaphore(name))
        self.ctr[name] = 0
        self.step[name] = step

    def qctr(self, name):
        if name not in self.ctr:
            self._mk(name, 16)
        return name

    def wait(self, engine, name, val=None):
        val = self.ctr[name] if val is None else val
        if val <= 0 or self.lastw.get((engine, name), -1) >= val:
            return
        self.lastw[(engine, name)] = val
        sem = self.sem[name]
        self.q[engine].append(lambda eng, s=sem, v=val: eng.wait_ge(s, v))

    def op(self, engine, method, *args, inc=None, **kwargs):
        if inc is None:
            self.q[engine].append(
                lambda eng, m=method, a=args, kw=kwargs: getattr(eng, m)(*a, **kw))
            return None
        self.ctr[inc] += self.step[inc]
        sem, k = self.sem[inc], self.step[inc]
        self.q[engine].append(
            lambda eng, m=method, a=args, kw=kwargs, sem=sem, k=k:
            getattr(eng, m)(*a, **kw).then_inc(sem, k))
        return self.ctr[inc]

    def drain_inc(self, engine, inc):
        return self.op(engine, "drain", inc=inc)

    def dma(self, engine, qname, outap, inap):
        qname = self.qctr(qname)
        self.wait(engine, qname)
        return self.op(engine, "dma_start", inc=qname, out=outap, in_=inap)

    def tok(self, name):
        return self.ctr[name]

    def barrier(self, engines=None):
        engines = engines or self.ENGINES
        for e in engines:
            for c_ in list(self.ctr):
                self.wait(e, c_)


def ceil_div(a, b):
    return (a + b - 1) // b


_uid = [0]


def _u():
    _uid[0] += 1
    return _uid[0]


def build(n_chunks=NCHUNK, n_groups=NG, n_experts=NE, debug_dumps=()):
    nc = bass.Bass("TRN2", target_bir_lowering=False, debug=False)

    x_d = nc.dram_tensor("x", [S, H], F32, kind="ExternalInput").ap()
    wq_d = nc.dram_tensor("wq", [H, H], F32, kind="ExternalInput").ap()
    wk_d = nc.dram_tensor("wk", [H, H], F32, kind="ExternalInput").ap()
    wv_d = nc.dram_tensor("wv", [H, H], F32, kind="ExternalInput").ap()
    wo_d = nc.dram_tensor("wo", [H, H], F32, kind="ExternalInput").ap()
    gw_d = nc.dram_tensor("gw", [NE, H], F32, kind="ExternalInput").ap()
    wg_d = nc.dram_tensor("wgate", [NE, H, DI], F32, kind="ExternalInput").ap()
    wu_d = nc.dram_tensor("wup", [NE, H, DI], F32, kind="ExternalInput").ap()
    wd_d = nc.dram_tensor("wdown", [NE, DI, H], F32, kind="ExternalInput").ap()
    cq_d = nc.dram_tensor("ropec", [HD, S], BF16, kind="ExternalInput").ap()
    sq_d = nc.dram_tensor("ropes", [HD, S], BF16, kind="ExternalInput").ap()
    dmask_d = nc.dram_tensor("dmask", [128, 128], F32, kind="ExternalInput").ap()
    idf_d = nc.dram_tensor("idf", [128, 128], F32, kind="ExternalInput").ap()
    idb_d = nc.dram_tensor("idb", [128, 128], BF16, kind="ExternalInput").ap()
    ones_d = nc.dram_tensor("ones", [128, 128], F32, kind="ExternalInput").ap()
    out_d = nc.dram_tensor("out", [S, H], F32, kind="ExternalOutput").ap()
    dbg = {}
    for name in debug_dumps:
        if name:
            dbg[name] = nc.dram_tensor(name, [128, NHT * TW], F32,
                                       kind="ExternalOutput").ap()

    xbf_d = nc.dram_tensor("xbf_i", [H, S], BF16, kind="Internal").ap()
    wt_d = nc.dram_tensor("wt_i", [4, NHT, NHT, 128, 128], BF16, kind="Internal").ap()

    ctx = ExitStack()
    sb = lambda n, s, d: ctx.enter_context(nc.sbuf_tensor(n, s, d))
    p = Plan(nc, ctx)

    acc = sb("acc", [128, NHT * TW], F32)
    idf = sb("idf_t", [128, 128], F32)
    idb = sb("idb_t", [128, 128], BF16)
    ones = sb("ones_t", [128, 128], F32)
    dmask = sb("dmask_t", [128, 128], F32)
    gwT = sb("gwT", [128, NHT * NE], F32)

    psA = ctx.enter_context(nc.psum_tensor("psA", [128, 2048], F32))
    psB = ctx.enter_context(nc.psum_tensor("psB", [128, 1024], F32))
    psT = ctx.enter_context(nc.psum_tensor("psT", [128, 1024], BF16))
    psC = ctx.enter_context(nc.psum_tensor("psC", [128, 512], F32))

    xv = x_d.rearrange("(tt pp) h -> tt pp h", pp=128)
    ov = out_d.rearrange("(tt pp) h -> tt pp h", pp=128)
    xbv = xbf_d.rearrange("(hh pp) t -> hh pp t", pp=128)

    p.dma("sync", "qc", idf[:], idf_d)
    p.dma("sync", "qc", idb[:], idb_d)
    p.dma("sync", "qc", ones[:], ones_d)
    t_cst = p.dma("sync", "qc", dmask[:], dmask_d)

    def acc_slice(hh, lo, w):
        return acc[:, hh * TW + lo: hh * TW + lo + w]

    with ExitStack() as ps_:
        psb_ = lambda n, s, d: ps_.enter_context(nc.sbuf_tensor(n, s, d))
        xt = [psb_(f"pr_xt{i}", [128, H], F32) for i in range(2)]
        xb = [psb_(f"pr_xb{i}", [128, H], BF16) for i in range(2)]
        stg = [psb_(f"pr_st{i}", [128, NHT * 128], BF16) for i in range(2)]
        gwsb = psb_("pr_gw", [NE, H], F32)
        p.wait("tensor", "qc", t_cst)

        t_gw = p.dma("sync", "qc", gwsb[:], gw_d)
        p.wait("tensor", "qc", t_gw)
        for kk in range(NHT):
            t_tr = p.op("tensor", "matmul",
                        psB[:, (kk % 2) * 512:(kk % 2) * 512 + 128],
                        gwsb[:, kk * 128:(kk + 1) * 128], idf[0:NE, :],
                        start=True, stop=True, inc="pe")
            p.wait("vector", "pe", t_tr)
            t_ev = p.op("vector", "tensor_copy",
                        out=gwT[:, kk * NE:(kk + 1) * NE],
                        in_=psB[:, (kk % 2) * 512:(kk % 2) * 512 + NE], inc="dve")
            p.wait("tensor", "dve", t_ev)

        def prep_one(idx, src_rows_f32, dst_view):
            par = idx % 2
            p.wait("sync", "dve", p.tok("dve"))
            p.wait("scalar", p.qctr(f"qst{par}"), p.tok(p.qctr(f"qst{par}")))
            t_load = p.dma("sync", f"qxt{par}", xt[par][:], src_rows_f32)
            p.wait("vector", f"qxt{par}", t_load)
            p.wait("vector", "pe", p.tok("pe"))
            t_cast = p.op("vector", "tensor_copy", out=xb[par][:], in_=xt[par][:],
                          inc="dve")
            p.wait("tensor", "dve", t_cast)
            t_ev = 0
            for hh in range(NHT):
                half = hh % 2
                if t_ev:
                    p.wait("tensor", "act", t_ev)
                t_tr = p.op("tensor", "transpose",
                            psT[:, half * 512:half * 512 + 128],
                            xb[par][:, hh * 128:(hh + 1) * 128], idb[:], inc="pe")
                p.wait("scalar", "pe", t_tr)
                t_ev = p.op("scalar", "copy",
                            out=stg[par][:, hh * 128:(hh + 1) * 128],
                            in_=psT[:, half * 512:half * 512 + 128], inc="act")
            p.wait("gpsimd", "act", t_ev)
            p.dma("gpsimd", f"qst{par}", dst_view.rearrange("hh pp cc -> pp hh cc"),
                  stg[par].rearrange("pp (hh cc) -> pp hh cc", hh=NHT))

        for tt in range(NHT):
            prep_one(tt, xv[tt], xbv[:, :, tt * 128:(tt + 1) * 128])
        for wi, wsrc in enumerate([wq_d, wk_d, wv_d, wo_d]):
            wvv = wsrc.rearrange("(ot pp) h -> ot pp h", pp=128)
            for ot in range(NHT):
                prep_one(wi * NHT + ot, wvv[ot], wt_d[wi, ot])
        p.barrier()

    def dump_acc(tag):
        if tag not in dbg:
            return
        with ExitStack() as ds:
            stg2 = ds.enter_context(
                nc.sbuf_tensor(f"dump{_u()}", [128, NHT * TW], F32))
            p.barrier(("vector",))
            t = p.op("vector", "tensor_copy", out=stg2[:], in_=acc[:], inc="dve")
            p.wait("sync", "dve", t)
            t2 = p.dma("sync", "qdp", dbg[tag], stg2[:])
            p.wait("sync", "qdp", t2)
            p.barrier()

    def layernorm():
        u = _u()
        with ExitStack() as ls:
            lsb = lambda n, s, d: ls.enter_context(nc.sbuf_tensor(n, s, d))
            sqs = [lsb(f"ln{u}_sq{i}", [128, TW], F32) for i in range(2)]
            rows = lsb(f"ln{u}_rows", [1, 3 * TW], F32)
            mb = lsb(f"ln{u}_mb", [128, TW], F32)
            rb = lsb(f"ln{u}_rb", [128, TW], F32)
            p.barrier()
            tmm = 0
            for hh in range(NHT):
                par = hh % 2
                p.wait("scalar", "pe", p.tok("pe"))
                t_sq = p.op("scalar", "activation", out=sqs[par][:],
                            in_=acc_slice(hh, 0, TW), func=AF.Square, inc="act")
                for n in range(TW // 512):
                    p.op("tensor", "matmul",
                         psA[0:1, n * 512:(n + 1) * 512],
                         ones[:, 0:1], acc_slice(hh, n * 512, 512),
                         start=(hh == 0), stop=(hh == NHT - 1), inc="pe")
                p.wait("tensor", "act", t_sq)
                for n in range(TW // 512):
                    tmm = p.op("tensor", "matmul",
                               psA[0:1, TW + n * 512: TW + (n + 1) * 512],
                               ones[:, 0:1], sqs[par][:, n * 512:(n + 1) * 512],
                               start=(hh == 0), stop=(hh == NHT - 1), inc="pe")
            p.wait("vector", "pe", tmm)
            p.op("vector", "tensor_scalar_mul", rows[0:1, 0:TW],
                 psA[0:1, 0:TW], 1.0 / H)
            p.op("vector", "tensor_scalar_mul", rows[0:1, TW:2 * TW],
                 psA[0:1, TW:2 * TW], 1.0 / H)
            p.op("vector", "drain")
            p.op("vector", "tensor_mul", rows[0:1, 2 * TW:3 * TW],
                 rows[0:1, 0:TW], rows[0:1, 0:TW])
            p.op("vector", "drain")
            p.op("vector", "tensor_sub", rows[0:1, TW:2 * TW],
                 rows[0:1, TW:2 * TW], rows[0:1, 2 * TW:3 * TW])
            p.op("vector", "drain")
            p.op("vector", "tensor_scalar_add", rows[0:1, TW:2 * TW],
                 rows[0:1, TW:2 * TW], EPS)
            t = p.drain_inc("vector", "dve")
            p.wait("scalar", "dve", t)
            t = p.op("scalar", "activation", out=rows[0:1, TW:2 * TW],
                     in_=rows[0:1, TW:2 * TW], func=AF.Sqrt, inc="act")
            p.wait("vector", "act", t)
            p.op("vector", "reciprocal", rows[0:1, TW:2 * TW],
                 rows[0:1, TW:2 * TW])
            t = p.drain_inc("vector", "dve")
            p.wait("tensor", "dve", t)
            for n in range(TW // 512):
                t1_ = p.op("tensor", "matmul", psB[:, 0:512], ones[0:1, :],
                           rows[0:1, n * 512:(n + 1) * 512],
                           start=True, stop=True, inc="pe")
                p.wait("vector", "pe", t1_)
                p.op("vector", "tensor_copy", out=mb[:, n * 512:(n + 1) * 512],
                     in_=psB[:, 0:512], inc="dve")
                t2_ = p.op("tensor", "matmul", psB[:, 512:1024], ones[0:1, :],
                           rows[0:1, TW + n * 512: TW + (n + 1) * 512],
                           start=True, stop=True, inc="pe")
                p.wait("vector", "pe", t2_)
                t3_ = p.op("vector", "tensor_copy",
                           out=rb[:, n * 512:(n + 1) * 512],
                           in_=psB[:, 512:1024], inc="dve")
                p.wait("tensor", "dve", t3_)
            for hh in range(NHT):
                p.op("vector", "tensor_sub", acc_slice(hh, 0, TW),
                     acc_slice(hh, 0, TW), mb[:])
                p.op("vector", "tensor_mul", acc_slice(hh, 0, TW),
                     acc_slice(hh, 0, TW), rb[:], inc="dve")
            p.barrier()

    for c in range(n_chunks):
        t0 = c * TW
        kvw = (c + 1) * TW
        nwin = kvw // 512
        nqt = TW // 128

        with ExitStack() as xs_:
            xtm = [xs_.enter_context(
                nc.sbuf_tensor(f"a{c}_xtm{i}", [128, H], F32)) for i in range(2)]
            p.barrier()
            for tt in range(TW // 128):
                par = tt % 2
                p.wait("sync", "pe", p.tok("pe"))
                tl = p.dma("sync", f"qxm{par}", xtm[par][:], xv[t0 // 128 + tt])
                p.wait("tensor", f"qxm{par}", tl)
                for hh in range(NHT):
                    half = hh % 2
                    ttr = p.op("tensor", "matmul",
                               psB[:, half * 512:half * 512 + 128],
                               xtm[par][:, hh * 128:(hh + 1) * 128],
                               idf[:], start=True, stop=True, inc="pe")
                    p.wait("vector", "pe", ttr)
                    tev = p.op("vector", "tensor_copy",
                               out=acc_slice(hh, tt * 128, 128),
                               in_=psB[:, half * 512:half * 512 + 128], inc="dve")
                    p.wait("tensor", "dve", tev)
            p.barrier()

        with ExitStack() as as_:
            asb = lambda n, s, d: as_.enter_context(nc.sbuf_tensor(n, s, d))
            cq = asb(f"a{c}_cq", [128, S], BF16)
            sq = asb(f"a{c}_sq", [128, S], BF16)
            ck = asb(f"a{c}_ck", [128, S], BF16)
            sk = asb(f"a{c}_sk", [128, S], BF16)
            kbf = [asb(f"a{c}_k{j}", [128, kvw], BF16) for j in range(GH)]
            vtm = [asb(f"a{c}_v{j}", [128, kvw], BF16) for j in range(GH)]
            qbf = [asb(f"a{c}_q{j}", [128, TW], BF16) for j in range(GH)]
            ctxb = [asb(f"a{c}_c{j}", [128, TW], BF16) for j in range(GH)]
            pbf = asb(f"a{c}_pbf", [128, 4 * 2048], BF16)
            ptsb = [asb(f"a{c}_pt{i}", [128, 512], BF16) for i in range(2)]
            xwin = [asb(f"a{c}_xw{i}", [128, NHT * 512], BF16) for i in range(2)]
            wslab = [asb(f"a{c}_ws{i}", [128, NHT * 128], BF16) for i in range(2)]
            raww = [asb(f"a{c}_rw{i}", [128, 512], F32) for i in range(2)]
            rotw = asb(f"a{c}_rot", [128, 512], F32)
            tmpw = asb(f"a{c}_tmp", [128, 512], F32)
            vwin = asb(f"a{c}_vw", [128, 512], BF16)
            biasc = asb(f"a{c}_bias", [128, 8], F32)
            esum = asb(f"a{c}_esum", [128, 8], F32)
            rcol = asb(f"a{c}_rcol", [128, 8], F32)

            p.barrier()
            p.dma("sync", "qtab", cq[:], cq_d)
            t_t = p.dma("sync", "qtab", sq[:], sq_d)
            p.wait("vector", "qtab", t_t)
            p.op("vector", "tensor_scalar_mul", ck[:], cq[:], ISQ)
            t_tab = p.op("vector", "tensor_scalar_mul", sk[:], sq[:], ISQ,
                         inc="dve")
            p.wait("tensor", "dve", t_tab)

            slab_last = [0, 0]
            xw_last = [0, 0]
            slab_i = [0]

            def do_proj(widx, j, dst_kind, parw, nlo, g):
                pars = slab_i[0] % 2
                slab_i[0] += 1
                p.wait("gpsimd", "pe", slab_last[pars])
                p.dma("gpsimd", f"qws{pars}",
                      wslab[pars].rearrange("pp (hh cc) -> pp hh cc", hh=NHT),
                      wt_d[widx, g * GH + j].rearrange("hh pp cc -> pp hh cc"))
                p.wait("tensor", f"qws{pars}", p.tok(f"qws{pars}"))
                half = slab_i[0] % 2
                p.wait("tensor", "act", p.tok("act"))
                p.wait("tensor", "dve", p.tok("dve"))
                t_mm = 0
                for kk in range(NHT):
                    t_mm = p.op("tensor", "matmul",
                                psB[:, half * 512:(half + 1) * 512],
                                wslab[pars][:, kk * 128:(kk + 1) * 128],
                                xwin[parw][:, kk * 512:(kk + 1) * 512],
                                start=(kk == 0), stop=(kk == NHT - 1), inc="pe")
                slab_last[pars] = p.tok("pe")
                p.wait("scalar", "pe", t_mm)
                if dst_kind == "v":
                    t_ev = p.op("scalar", "copy", out=vwin[:],
                                in_=psB[:, half * 512:(half + 1) * 512], inc="act")
                    p.wait("tensor", "act", t_ev)
                    t_tr = 0
                    for tt4 in range(4):
                        t_tr = p.op("tensor", "transpose",
                                    psT[:, tt4 * 128:(tt4 + 1) * 128],
                                    vwin[:, tt4 * 128:(tt4 + 1) * 128],
                                    idb[:], inc="pe")
                    p.wait("scalar", "pe", t_tr)
                    p.op("scalar", "copy", out=vtm[j][:, nlo:nlo + 512],
                         in_=psT[:, 0:512], inc="act")
                else:
                    parr = slab_i[0] % 2
                    p.wait("scalar", "dve", p.tok("dve"))
                    t_ev = p.op("scalar", "copy", out=raww[parr][:],
                                in_=psB[:, half * 512:(half + 1) * 512], inc="act")
                    p.wait("vector", "act", t_ev)
                    X = raww[parr]
                    C = cq if dst_kind == "q" else ck
                    Sn = sq if dst_kind == "q" else sk
                    dst = (qbf[j][:, nlo - t0:nlo - t0 + 512] if dst_kind == "q"
                           else kbf[j][:, nlo:nlo + 512])
                    p.op("vector", "tensor_scalar_mul", rotw[0:64, :],
                         X[64:128, :], -1.0)
                    p.op("vector", "tensor_copy", out=rotw[64:128, :],
                         in_=X[0:64, :])
                    p.op("vector", "tensor_mul", tmpw[:], X[:],
                         C[:, nlo:nlo + 512])
                    p.op("vector", "tensor_mul", rotw[:], rotw[:],
                         Sn[:, nlo:nlo + 512])
                    p.op("vector", "tensor_add", dst, tmpw[:], rotw[:], inc="dve")

            for g in range(n_groups):
                for n in range(nwin):
                    nlo = n * 512
                    parw = n % 2
                    p.wait("sync", "pe", xw_last[parw])
                    t_xw = p.dma("sync", f"qxw{parw}",
                                 xwin[parw].rearrange("pp (kk tt2) -> pp kk tt2",
                                                      kk=NHT),
                                 xbv[:, :, nlo:nlo + 512].rearrange(
                                     "hh pp tt2 -> pp hh tt2"))
                    p.wait("tensor", f"qxw{parw}", t_xw)
                    for j in range(GH):
                        do_proj(1, j, "k", parw, nlo, g)
                    for j in range(GH):
                        do_proj(2, j, "v", parw, nlo, g)
                    if nlo >= t0:
                        for j in range(GH):
                            do_proj(0, j, "q", parw, nlo, g)
                    xw_last[parw] = p.tok("pe")
                    if XBAR & 4:
                        p.barrier()

                for j in range(GH):
                    for qc in range(TW // 512):
                        kmaxt = c * nqt + qc * 4 + 4
                        for jj in range(4):
                            qt = qc * 4 + jj
                            qi = c * nqt + qt
                            klen = (qi + 1) * 128
                            nch = ceil_div(klen, 512)
                            p.wait("tensor", "act", p.tok("act"))
                            p.wait("tensor", "dve", p.tok("dve"))
                            t_mm = 0
                            for nn in range(nch):
                                w = min(512, klen - nn * 512)
                                t_mm = p.op("tensor", "matmul",
                                            psA[:, nn * 512:nn * 512 + w],
                                            qbf[j][:, qt * 128:(qt + 1) * 128],
                                            kbf[j][:, nn * 512:nn * 512 + w],
                                            start=True, stop=True, inc="pe")
                            p.wait("vector", "pe", t_mm)
                            p.op("vector", "tensor_add",
                                 psA[:, qi * 128:klen],
                                 psA[:, qi * 128:klen], dmask[:])
                            p.op("vector", "reduce_max", biasc[:, qt:qt + 1],
                                 psA[:, 0:klen], axis=AX.X, negate=True)
                            t_b = p.drain_inc("vector", "dve")
                            p.wait("scalar", "dve", t_b)
                            t_e = p.op("scalar", "activation",
                                       out=pbf[:, jj * 2048:jj * 2048 + klen],
                                       in_=psA[:, 0:klen], func=AF.Exp,
                                       bias=biasc[:, qt:qt + 1], scale=1.0,
                                       accum_out=esum[:, qt:qt + 1], inc="act")
                            p.wait("vector", "act", t_e)
                            p.op("vector", "reciprocal", rcol[:, qt:qt + 1],
                                 esum[:, qt:qt + 1])
                            p.op("vector", "drain")
                            kmax = kmaxt * 128
                            if klen < kmax:
                                p.op("vector", "memset",
                                     pbf[:, jj * 2048 + klen:jj * 2048 + kmax],
                                     0.0)
                            p.op("vector", "tensor_scalar_mul",
                                 pbf[:, jj * 2048:jj * 2048 + klen],
                                 pbf[:, jj * 2048:jj * 2048 + klen],
                                 rcol[:, qt:qt + 1], inc="dve")
                        p.wait("tensor", "dve", p.tok("dve"))
                        t_cx = 0
                        for kj in range(kmaxt):
                            half = kj % 2
                            if kj >= 2:
                                p.wait("tensor", "act", p.tok("act"))
                            t_tr = 0
                            for jj in range(4):
                                t_tr = p.op("tensor", "transpose",
                                            psT[:, half * 512 + jj * 128:
                                                half * 512 + (jj + 1) * 128],
                                            pbf[:, jj * 2048 + kj * 128:
                                                jj * 2048 + (kj + 1) * 128],
                                            idb[:], inc="pe")
                            p.wait("scalar", "pe", t_tr)
                            t_ev = p.op("scalar", "copy", out=ptsb[half][:],
                                        in_=psT[:, half * 512:(half + 1) * 512],
                                        inc="act")
                            p.wait("tensor", "act", t_ev)
                            t_cx = p.op("tensor", "matmul", psC[:],
                                        vtm[j][:, kj * 128:(kj + 1) * 128],
                                        ptsb[half][:],
                                        start=(kj == 0), stop=(kj == kmaxt - 1),
                                        inc="pe")
                        p.wait("vector", "pe", t_cx)
                        t_cv = p.op("vector", "tensor_copy",
                                    out=ctxb[j][:, qc * 512:(qc + 1) * 512],
                                    in_=psC[:], inc="dve")
                        p.wait("tensor", "dve", t_cv)
                        if XBAR & 8:
                            p.barrier()

                for m in range(NHT):
                    p.wait("gpsimd", "pe", p.tok("pe"))
                    p.dma("gpsimd", "qws0",
                          wslab[0][:, 0:GH * 128].rearrange(
                              "pp (jh cc) -> pp jh cc", jh=GH),
                          wt_d[3, m, g * GH:(g + 1) * GH].rearrange(
                              "jh pp cc -> pp jh cc"))
                    p.wait("tensor", "qws0", p.tok("qws0"))
                    for n in range(TW // 512):
                        half = n % 2
                        p.wait("tensor", "dve", p.tok("dve"))
                        t_mm = 0
                        for j in range(GH):
                            t_mm = p.op("tensor", "matmul",
                                        psB[:, half * 512:(half + 1) * 512],
                                        wslab[0][:, j * 128:(j + 1) * 128],
                                        ctxb[j][:, n * 512:(n + 1) * 512],
                                        start=(j == 0), stop=(j == GH - 1),
                                        inc="pe")
                        p.wait("vector", "pe", t_mm)
                        p.op("vector", "tensor_add",
                             acc_slice(m, n * 512, 512),
                             acc_slice(m, n * 512, 512),
                             psB[:, half * 512:(half + 1) * 512], inc="dve")
                slab_last[0] = slab_last[1] = p.tok("pe")
            p.barrier()
        dump_acc(f"dbg_res1_{c}")

        layernorm()
        dump_acc(f"dbg_h1_{c}")

        with ExitStack() as fs_:
            fsb = lambda n, s, d: fs_.enter_context(nc.sbuf_tensor(n, s, d))
            h1bf = fsb(f"f{c}_h1", [128, NHT * TW], BF16)
            w_em = fsb(f"f{c}_wem", [NE, TW], F32)
            zrow = fsb(f"f{c}_z", [128, NE], F32)
            m8 = fsb(f"f{c}_m8", [128, 8], F32)
            mrow = fsb(f"f{c}_msk", [128, NE], F32)
            srow = fsb(f"f{c}_srow", [128, 4], F32)
            wrow = fsb(f"f{c}_wrow", [1, TW], F32)
            wbc = [fsb(f"f{c}_wbc{i}", [128, TW], F32) for i in range(2)]
            gslab = [fsb(f"f{c}_gs{i}", [128, NHT * 128], BF16) for i in range(2)]
            uslab = [fsb(f"f{c}_us{i}", [128, NHT * 128], BF16) for i in range(2)]
            dslab = [[fsb(f"f{c}_ds{e4}_{i}", [128, NIT * 128], BF16)
                      for i in range(2)] for e4 in range(EG)]
            t1 = [fsb(f"f{c}_t1{i}", [128, TW], F32) for i in range(2)]
            gu = [fsb(f"f{c}_gu{e4}", [128, NIT * TW], BF16) for e4 in range(EG)]

            for hh in range(NHT):
                p.op("vector", "tensor_copy",
                     out=h1bf[:, hh * TW:(hh + 1) * TW],
                     in_=acc_slice(hh, 0, TW), inc="dve")

            for tt in range(TW // 128):
                reg = (tt % 2) * NE
                p.wait("tensor", "act", p.tok("act"))
                t_lg = 0
                for kk in range(NHT):
                    t_lg = p.op("tensor", "matmul", psC[:, reg:reg + NE],
                                acc_slice(kk, tt * 128, 128),
                                gwT[:, kk * NE:(kk + 1) * NE],
                                start=(kk == 0), stop=(kk == NHT - 1), inc="pe")
                p.wait("vector", "pe", t_lg)
                p.op("vector", "reduce_max", srow[:, 0:1], psC[:, reg:reg + NE],
                     axis=AX.X, negate=True)
                t_b = p.drain_inc("vector", "dve")
                p.wait("scalar", "dve", t_b)
                t_z = p.op("scalar", "activation", out=zrow[:],
                           in_=psC[:, reg:reg + NE], func=AF.Exp,
                           bias=srow[:, 0:1], scale=1.0, inc="act")
                p.wait("vector", "act", t_z)
                p.op("vector", "max", out=m8[:], in_=zrow[:])
                p.op("vector", "drain")
                p.op("vector", "tensor_single_scalar", mrow[:], zrow[:],
                     m8[:, 7:8], OP.is_ge)
                p.op("vector", "drain")
                p.op("vector", "tensor_mul", mrow[:], mrow[:], zrow[:])
                p.op("vector", "drain")
                p.op("vector", "reduce_sum", srow[:, 1:2], mrow[:], axis=AX.X)
                p.op("vector", "drain")
                p.op("vector", "reciprocal", srow[:, 2:3], srow[:, 1:2])
                p.op("vector", "drain")
                t_w = p.op("vector", "tensor_scalar_mul", mrow[:], mrow[:],
                           srow[:, 2:3], inc="dve")
                p.wait("tensor", "dve", t_w)
                t_tr = p.op("tensor", "matmul", psB[0:NE, 0:128], mrow[:],
                            idf[:], start=True, stop=True, inc="pe")
                p.wait("vector", "pe", t_tr)
                p.op("vector", "tensor_copy",
                     out=w_em[:, tt * 128:(tt + 1) * 128],
                     in_=psB[0:NE, 0:128], inc="dve")
            p.barrier(("tensor", "vector", "scalar", "gpsimd"))

            wgv = wg_d.rearrange("e (kk pp) i -> e kk pp i", pp=128)
            wuv = wu_d.rearrange("e (kk pp) i -> e kk pp i", pp=128)
            wdv = wd_d.rearrange("e (kk pp) h -> e kk pp h", pp=128)
            gs_last = [0, 0]
            ds_last = {}
            gu_last = [0] * EG
            for egi in range(n_experts // EG):
                for e4 in range(EG):
                    e = egi * EG + e4
                    parw = e % 2
                    p.wait("gpsimd", "pe", p.tok("pe"))
                    t_wr = p.dma("gpsimd", "qwr", wrow[:], w_em[e:e + 1, :])
                    p.wait("tensor", "qwr", t_wr)
                    for n in range(TW // 512):
                        p.wait("tensor", "dve", p.tok("dve"))
                        t_bc = p.op("tensor", "matmul", psC[:], ones[0:1, :],
                                    wrow[0:1, n * 512:(n + 1) * 512],
                                    start=True, stop=True, inc="pe")
                        p.wait("vector", "pe", t_bc)
                        p.op("vector", "tensor_copy",
                             out=wbc[parw][:, n * 512:(n + 1) * 512],
                             in_=psC[:], inc="dve")
                    for m in range(NIT):
                        pars = m % 2
                        p.wait("gpsimd", "pe", gs_last[pars])
                        t_g = p.dma("gpsimd", f"qgs{pars}",
                                    gslab[pars].rearrange("pp (kk ii) -> pp kk ii",
                                                          kk=NHT),
                                    wgv[e, :, :, m * 128:(m + 1) * 128].rearrange(
                                        "kk pp ii -> pp kk ii"))
                        t_u = p.dma("gpsimd", f"qus{pars}",
                                    uslab[pars].rearrange("pp (kk ii) -> pp kk ii",
                                                          kk=NHT),
                                    wuv[e, :, :, m * 128:(m + 1) * 128].rearrange(
                                        "kk pp ii -> pp kk ii"))
                        p.wait("tensor", f"qgs{pars}", t_g)
                        p.wait("tensor", "act", p.tok("act"))
                        t_gm = 0
                        for kk in range(NHT):
                            for n in range(TW // 512):
                                t_gm = p.op("tensor", "matmul",
                                            psA[:, n * 512:(n + 1) * 512],
                                            gslab[pars][:, kk * 128:(kk + 1) * 128],
                                            h1bf[:, kk * TW + n * 512:
                                                 kk * TW + (n + 1) * 512],
                                            start=(kk == 0), stop=(kk == NHT - 1),
                                            inc="pe")
                        p.wait("scalar", "pe", t_gm)
                        part = m % 2
                        p.wait("scalar", "dve", p.tok("dve"))
                        t_si = p.op("scalar", "activation", out=t1[part][:],
                                    in_=psA[:, 0:TW], func=AF.Silu, inc="act")
                        p.wait("tensor", f"qus{pars}", t_u)
                        p.wait("tensor", "dve", p.tok("dve"))
                        t_um = 0
                        for kk in range(NHT):
                            for n in range(TW // 512):
                                t_um = p.op("tensor", "matmul",
                                            psA[:, TW + n * 512:TW + (n + 1) * 512],
                                            uslab[pars][:, kk * 128:(kk + 1) * 128],
                                            h1bf[:, kk * TW + n * 512:
                                                 kk * TW + (n + 1) * 512],
                                            start=(kk == 0), stop=(kk == NHT - 1),
                                            inc="pe")
                        gs_last[pars] = p.tok("pe")
                        p.wait("vector", "act", t_si)
                        p.wait("vector", "pe", t_um)
                        p.wait("vector", "pe", gu_last[e4])
                        p.op("vector", "tensor_mul", t1[part][:], t1[part][:],
                             wbc[parw][:])
                        p.op("vector", "tensor_mul",
                             gu[e4][:, m * TW:(m + 1) * TW],
                             t1[part][:], psA[:, TW:2 * TW], inc="dve")
                    if XBAR & 1:
                        p.barrier()
                for m2 in range(NHT):
                    par2 = m2 % 2
                    for e4 in range(EG):
                        e = egi * EG + e4
                        p.wait("gpsimd", "pe", ds_last.get((e4, par2), 0))
                        t_d = p.dma("gpsimd", f"qds{e4}_{par2}",
                                    dslab[e4][par2].rearrange(
                                        "pp (kk hh2) -> pp kk hh2", kk=NIT),
                                    wdv[e, :, :, m2 * 128:(m2 + 1) * 128].rearrange(
                                        "kk pp hh2 -> pp kk hh2"))
                        p.wait("tensor", f"qds{e4}_{par2}", t_d)
                    p.wait("tensor", "dve", p.tok("dve"))
                    for n in range(TW // 512):
                        t_dm = 0
                        for e4 in range(EG):
                            for kk in range(NIT):
                                t_dm = p.op("tensor", "matmul",
                                            psB[:, n * 512:(n + 1) * 512],
                                            dslab[e4][par2][:,
                                                            kk * 128:(kk + 1) * 128],
                                            gu[e4][:, kk * TW + n * 512:
                                                   kk * TW + (n + 1) * 512],
                                            start=(e4 == 0 and kk == 0),
                                            stop=(e4 == EG - 1 and kk == NIT - 1),
                                            inc="pe")
                        p.wait("vector", "pe", t_dm)
                        p.op("vector", "tensor_add",
                             acc_slice(m2, n * 512, 512),
                             acc_slice(m2, n * 512, 512),
                             psB[:, n * 512:(n + 1) * 512], inc="dve")
                    for e4 in range(EG):
                        ds_last[(e4, par2)] = p.tok("pe")
                    if XBAR & 2:
                        p.barrier()
                gu_last = [p.tok("pe")] * EG
            p.barrier()

        layernorm()
        dump_acc(f"dbg_h2_{c}")

        with ExitStack() as os_:
            ost = [os_.enter_context(
                nc.sbuf_tensor(f"o{c}_st{i}", [128, H], F32)) for i in range(2)]
            st_last = [0, 0]
            for tt in range(TW // 128):
                par = tt % 2
                p.wait("vector", p.qctr(f"qos{par}"), st_last[par])
                t_ev = 0
                for hh in range(NHT):
                    half = hh % 2
                    t_tr = p.op("tensor", "matmul",
                                psB[:, half * 512:half * 512 + 128],
                                acc_slice(hh, tt * 128, 128), idf[:],
                                start=True, stop=True, inc="pe")
                    p.wait("vector", "pe", t_tr)
                    t_ev = p.op("vector", "tensor_copy",
                                out=ost[par][:, hh * 128:(hh + 1) * 128],
                                in_=psB[:, half * 512:half * 512 + 128],
                                inc="dve")
                    p.wait("tensor", "dve", t_ev)
                p.wait("sync", "dve", t_ev)
                st_last[par] = p.dma("sync", f"qos{par}", ov[t0 // 128 + tt],
                                     ost[par][:])
            p.barrier()

    for qn in list(p.ctr):
        if qn.startswith("qos") or qn == "qdp":
            p.wait("sync", qn, p.tok(qn))

    with nc.Block() as block:
        for ename in Plan.ENGINES:
            getattr(block, ename)(lambda eng, q=p.q[ename]: [f(eng) for f in q])
    ctx.close()
    return nc


_ROPE_BASE = 10000.0


def _host_constants():
    half = HD // 2
    inv = (1.0 / _ROPE_BASE ** (np.arange(0, HD, 2, dtype=np.float32) / HD))
    t = np.arange(S, dtype=np.float32)
    freqs = np.einsum("i,j->ij", t, inv.astype(np.float32)).astype(np.float32)
    cos = np.cos(freqs).astype(np.float32)
    sin = np.sin(freqs).astype(np.float32)
    C = np.empty((HD, S), np.float32)
    Sn = np.empty((HD, S), np.float32)
    C[:half] = cos.T
    C[half:] = cos.T
    Sn[:half] = sin.T
    Sn[half:] = sin.T
    return {
        "ropec": C.astype(ml_dtypes.bfloat16),
        "ropes": Sn.astype(ml_dtypes.bfloat16),
        "dmask": np.triu(np.full((128, 128), NEG, np.float32), k=1),
        "idf": np.eye(128, dtype=np.float32),
        "idb": np.eye(128).astype(ml_dtypes.bfloat16),
        "ones": np.ones((128, 128), np.float32),
    }


def kernel(L0_attn_hidden_states, L0_attn_Wq, L0_attn_Wk, L0_attn_Wv, L0_attn_Wo,
           L0_ffn_gate_w, L0_ffn_w_gate, L0_ffn_w_up, L0_ffn_w_down):
    x = np.ascontiguousarray(np.asarray(L0_attn_hidden_states, dtype=np.float32))
    shared = {
        "wq": np.ascontiguousarray(np.asarray(L0_attn_Wq, np.float32)),
        "wk": np.ascontiguousarray(np.asarray(L0_attn_Wk, np.float32)),
        "wv": np.ascontiguousarray(np.asarray(L0_attn_Wv, np.float32)),
        "wo": np.ascontiguousarray(np.asarray(L0_attn_Wo, np.float32)),
        "gw": np.ascontiguousarray(np.asarray(L0_ffn_gate_w, np.float32)),
        "wgate": np.ascontiguousarray(np.asarray(L0_ffn_w_gate, np.float32)),
        "wup": np.ascontiguousarray(np.asarray(L0_ffn_w_up, np.float32)),
        "wdown": np.ascontiguousarray(np.asarray(L0_ffn_w_down, np.float32)),
        **_host_constants(),
    }
    nc = build()
    in_maps = [dict(shared, x=np.ascontiguousarray(x[b])) for b in range(8)]
    res = run_bass_kernel_spmd(nc, in_maps, core_ids=list(range(8)))
    return np.stack([r["out"] for r in res.results], axis=0)


# version 03
# speedup vs baseline: 1.0130x; 1.0130x over previous
import math
import numpy as np
import ml_dtypes
from contextlib import ExitStack

import concourse.bass as bass
import concourse.mybir as mybir
from concourse.bass_utils import run_bass_kernel_spmd

F32 = mybir.dt.float32
BF16 = mybir.dt.bfloat16
AF = mybir.ActivationFunctionType
AX = mybir.AxisListType
OP = mybir.AluOpType

S = 2048
H = 2048
NH = 16
HD = 128
NE = 32
DI = 768
NCHUNK = 2
TW = S // NCHUNK
EPS = 1e-5
ISQ = 1.0 / math.sqrt(HD)
NHT = H // 128
NIT = DI // 128
NG = 4
GH = NH // NG
EG = 4
NEG = -1e9
import os as _os
XBAR = int(_os.environ.get('KBAR', '0'))


class Plan:

    ENGINES = ("tensor", "vector", "scalar", "gpsimd", "sync")

    def __init__(self, nc, ctx):
        self.nc = nc
        self.ctx = ctx
        self.q = {e: [] for e in self.ENGINES}
        self.ctr = {}
        self.sem = {}
        self.step = {}
        self.lastw = {}
        for name in ("pe", "dve", "act", "pool"):
            self._mk(name, 1)

    def _mk(self, name, step):
        self.sem[name] = self.ctx.enter_context(self.nc.semaphore(name))
        self.ctr[name] = 0
        self.step[name] = step

    def qctr(self, name):
        if name not in self.ctr:
            self._mk(name, 16)
        return name

    def wait(self, engine, name, val=None):
        val = self.ctr[name] if val is None else val
        if val <= 0 or self.lastw.get((engine, name), -1) >= val:
            return
        self.lastw[(engine, name)] = val
        sem = self.sem[name]
        self.q[engine].append(lambda eng, s=sem, v=val: eng.wait_ge(s, v))

    def op(self, engine, method, *args, inc=None, **kwargs):
        if inc is None:
            self.q[engine].append(
                lambda eng, m=method, a=args, kw=kwargs: getattr(eng, m)(*a, **kw))
            return None
        self.ctr[inc] += self.step[inc]
        sem, k = self.sem[inc], self.step[inc]
        self.q[engine].append(
            lambda eng, m=method, a=args, kw=kwargs, sem=sem, k=k:
            getattr(eng, m)(*a, **kw).then_inc(sem, k))
        return self.ctr[inc]

    def drain_inc(self, engine, inc):
        return self.op(engine, "drain", inc=inc)

    def dma(self, engine, qname, outap, inap):
        qname = self.qctr(qname)
        self.wait(engine, qname)
        return self.op(engine, "dma_start", inc=qname, out=outap, in_=inap)

    def tok(self, name):
        return self.ctr[name]

    def barrier(self, engines=None):
        engines = engines or self.ENGINES
        for e in engines:
            for c_ in list(self.ctr):
                self.wait(e, c_)


def ceil_div(a, b):
    return (a + b - 1) // b


_uid = [0]


def _u():
    _uid[0] += 1
    return _uid[0]


def build(n_chunks=NCHUNK, n_groups=NG, n_experts=NE, debug_dumps=()):
    nc = bass.Bass("TRN2", target_bir_lowering=False, debug=False)

    x_d = nc.dram_tensor("x", [S, H], F32, kind="ExternalInput").ap()
    wq_d = nc.dram_tensor("wq", [H, H], F32, kind="ExternalInput").ap()
    wk_d = nc.dram_tensor("wk", [H, H], F32, kind="ExternalInput").ap()
    wv_d = nc.dram_tensor("wv", [H, H], F32, kind="ExternalInput").ap()
    wo_d = nc.dram_tensor("wo", [H, H], F32, kind="ExternalInput").ap()
    gw_d = nc.dram_tensor("gw", [NE, H], F32, kind="ExternalInput").ap()
    wg_d = nc.dram_tensor("wgate", [NE, H, DI], F32, kind="ExternalInput").ap()
    wu_d = nc.dram_tensor("wup", [NE, H, DI], F32, kind="ExternalInput").ap()
    wd_d = nc.dram_tensor("wdown", [NE, DI, H], F32, kind="ExternalInput").ap()
    cq_d = nc.dram_tensor("ropec", [HD, S], BF16, kind="ExternalInput").ap()
    sq_d = nc.dram_tensor("ropes", [HD, S], BF16, kind="ExternalInput").ap()
    dmask_d = nc.dram_tensor("dmask", [128, 128], F32, kind="ExternalInput").ap()
    idf_d = nc.dram_tensor("idf", [128, 128], F32, kind="ExternalInput").ap()
    idb_d = nc.dram_tensor("idb", [128, 128], BF16, kind="ExternalInput").ap()
    ones_d = nc.dram_tensor("ones", [128, 128], F32, kind="ExternalInput").ap()
    out_d = nc.dram_tensor("out", [S, H], F32, kind="ExternalOutput").ap()
    dbg = {}
    for name in debug_dumps:
        if name:
            dbg[name] = nc.dram_tensor(name, [128, NHT * TW], F32,
                                       kind="ExternalOutput").ap()

    xbf_d = nc.dram_tensor("xbf_i", [H, S], BF16, kind="Internal").ap()
    wt_d = nc.dram_tensor("wt_i", [4, NHT, NHT, 128, 128], BF16, kind="Internal").ap()

    ctx = ExitStack()
    sb = lambda n, s, d: ctx.enter_context(nc.sbuf_tensor(n, s, d))
    p = Plan(nc, ctx)

    acc = sb("acc", [128, NHT * TW], F32)
    idf = sb("idf_t", [128, 128], F32)
    idb = sb("idb_t", [128, 128], BF16)
    ones = sb("ones_t", [128, 128], F32)
    dmask = sb("dmask_t", [128, 128], F32)
    gwT = sb("gwT", [128, NHT * NE], F32)

    psA = ctx.enter_context(nc.psum_tensor("psA", [128, 2048], F32))
    psB = ctx.enter_context(nc.psum_tensor("psB", [128, 1024], F32))
    psT = ctx.enter_context(nc.psum_tensor("psT", [128, 1024], BF16))
    psC = ctx.enter_context(nc.psum_tensor("psC", [128, 512], F32))

    xv = x_d.rearrange("(tt pp) h -> tt pp h", pp=128)
    ov = out_d.rearrange("(tt pp) h -> tt pp h", pp=128)
    xbv = xbf_d.rearrange("(hh pp) t -> hh pp t", pp=128)

    p.dma("sync", "qc", idf[:], idf_d)
    p.dma("sync", "qc", idb[:], idb_d)
    p.dma("sync", "qc", ones[:], ones_d)
    t_cst = p.dma("sync", "qc", dmask[:], dmask_d)

    def acc_slice(hh, lo, w):
        return acc[:, hh * TW + lo: hh * TW + lo + w]

    with ExitStack() as ps_:
        psb_ = lambda n, s, d: ps_.enter_context(nc.sbuf_tensor(n, s, d))
        xt = [psb_(f"pr_xt{i}", [128, H], F32) for i in range(2)]
        xb = [psb_(f"pr_xb{i}", [128, H], BF16) for i in range(2)]
        stg = [psb_(f"pr_st{i}", [128, NHT * 128], BF16) for i in range(2)]
        gwsb = psb_("pr_gw", [NE, H], F32)
        p.wait("tensor", "qc", t_cst)

        t_gw = p.dma("sync", "qc", gwsb[:], gw_d)
        p.wait("tensor", "qc", t_gw)
        for kk in range(NHT):
            t_tr = p.op("tensor", "matmul",
                        psB[:, (kk % 2) * 512:(kk % 2) * 512 + 128],
                        gwsb[:, kk * 128:(kk + 1) * 128], idf[0:NE, :],
                        start=True, stop=True, inc="pe")
            p.wait("vector", "pe", t_tr)
            t_ev = p.op("vector", "tensor_copy",
                        out=gwT[:, kk * NE:(kk + 1) * NE],
                        in_=psB[:, (kk % 2) * 512:(kk % 2) * 512 + NE], inc="dve")
            p.wait("tensor", "dve", t_ev)

        def prep_one(idx, src_rows_f32, dst_view):
            par = idx % 2
            p.wait("sync", "dve", p.tok("dve"))
            p.wait("scalar", p.qctr(f"qst{par}"), p.tok(p.qctr(f"qst{par}")))
            t_load = p.dma("sync", f"qxt{par}", xt[par][:], src_rows_f32)
            p.wait("vector", f"qxt{par}", t_load)
            p.wait("vector", "pe", p.tok("pe"))
            t_cast = p.op("vector", "tensor_copy", out=xb[par][:], in_=xt[par][:],
                          inc="dve")
            p.wait("tensor", "dve", t_cast)
            t_ev = 0
            for hh in range(NHT):
                half = hh % 2
                if t_ev:
                    p.wait("tensor", "act", t_ev)
                t_tr = p.op("tensor", "transpose",
                            psT[:, half * 512:half * 512 + 128],
                            xb[par][:, hh * 128:(hh + 1) * 128], idb[:], inc="pe")
                p.wait("scalar", "pe", t_tr)
                t_ev = p.op("scalar", "copy",
                            out=stg[par][:, hh * 128:(hh + 1) * 128],
                            in_=psT[:, half * 512:half * 512 + 128], inc="act")
            p.wait("gpsimd", "act", t_ev)
            p.dma("gpsimd", f"qst{par}", dst_view.rearrange("hh pp cc -> pp hh cc"),
                  stg[par].rearrange("pp (hh cc) -> pp hh cc", hh=NHT))

        for tt in range(NHT):
            prep_one(tt, xv[tt], xbv[:, :, tt * 128:(tt + 1) * 128])
        for wi, wsrc in enumerate([wq_d, wk_d, wv_d, wo_d]):
            wvv = wsrc.rearrange("(ot pp) h -> ot pp h", pp=128)
            for ot in range(NHT):
                prep_one(wi * NHT + ot, wvv[ot], wt_d[wi, ot])
        p.barrier()

    def dump_acc(tag):
        if tag not in dbg:
            return
        with ExitStack() as ds:
            stg2 = ds.enter_context(
                nc.sbuf_tensor(f"dump{_u()}", [128, NHT * TW], F32))
            p.barrier(("vector",))
            t = p.op("vector", "tensor_copy", out=stg2[:], in_=acc[:], inc="dve")
            p.wait("sync", "dve", t)
            t2 = p.dma("sync", "qdp", dbg[tag], stg2[:])
            p.wait("sync", "qdp", t2)
            p.barrier()

    def layernorm():
        u = _u()
        with ExitStack() as ls:
            lsb = lambda n, s, d: ls.enter_context(nc.sbuf_tensor(n, s, d))
            sqs = [lsb(f"ln{u}_sq{i}", [128, TW], F32) for i in range(2)]
            rows = lsb(f"ln{u}_rows", [1, 3 * TW], F32)
            mb = lsb(f"ln{u}_mb", [128, TW], F32)
            rb = lsb(f"ln{u}_rb", [128, TW], F32)
            p.barrier()
            tmm = 0
            for hh in range(NHT):
                par = hh % 2
                p.wait("scalar", "pe", p.tok("pe"))
                t_sq = p.op("scalar", "activation", out=sqs[par][:],
                            in_=acc_slice(hh, 0, TW), func=AF.Square, inc="act")
                for n in range(TW // 512):
                    p.op("tensor", "matmul",
                         psA[0:1, n * 512:(n + 1) * 512],
                         ones[:, 0:1], acc_slice(hh, n * 512, 512),
                         start=(hh == 0), stop=(hh == NHT - 1), inc="pe")
                p.wait("tensor", "act", t_sq)
                for n in range(TW // 512):
                    tmm = p.op("tensor", "matmul",
                               psA[0:1, TW + n * 512: TW + (n + 1) * 512],
                               ones[:, 0:1], sqs[par][:, n * 512:(n + 1) * 512],
                               start=(hh == 0), stop=(hh == NHT - 1), inc="pe")
            p.wait("vector", "pe", tmm)
            p.op("vector", "tensor_scalar_mul", rows[0:1, 0:TW],
                 psA[0:1, 0:TW], 1.0 / H)
            p.op("vector", "tensor_scalar_mul", rows[0:1, TW:2 * TW],
                 psA[0:1, TW:2 * TW], 1.0 / H)
            p.op("vector", "drain")
            p.op("vector", "tensor_mul", rows[0:1, 2 * TW:3 * TW],
                 rows[0:1, 0:TW], rows[0:1, 0:TW])
            p.op("vector", "drain")
            p.op("vector", "tensor_sub", rows[0:1, TW:2 * TW],
                 rows[0:1, TW:2 * TW], rows[0:1, 2 * TW:3 * TW])
            p.op("vector", "drain")
            p.op("vector", "tensor_scalar_add", rows[0:1, TW:2 * TW],
                 rows[0:1, TW:2 * TW], EPS)
            t = p.drain_inc("vector", "dve")
            p.wait("scalar", "dve", t)
            t = p.op("scalar", "activation", out=rows[0:1, TW:2 * TW],
                     in_=rows[0:1, TW:2 * TW], func=AF.Sqrt, inc="act")
            p.wait("vector", "act", t)
            p.op("vector", "reciprocal", rows[0:1, TW:2 * TW],
                 rows[0:1, TW:2 * TW])
            t = p.drain_inc("vector", "dve")
            p.wait("tensor", "dve", t)
            for n in range(TW // 512):
                t1_ = p.op("tensor", "matmul", psB[:, 0:512], ones[0:1, :],
                           rows[0:1, n * 512:(n + 1) * 512],
                           start=True, stop=True, inc="pe")
                p.wait("vector", "pe", t1_)
                p.op("vector", "tensor_copy", out=mb[:, n * 512:(n + 1) * 512],
                     in_=psB[:, 0:512], inc="dve")
                t2_ = p.op("tensor", "matmul", psB[:, 512:1024], ones[0:1, :],
                           rows[0:1, TW + n * 512: TW + (n + 1) * 512],
                           start=True, stop=True, inc="pe")
                p.wait("vector", "pe", t2_)
                t3_ = p.op("vector", "tensor_copy",
                           out=rb[:, n * 512:(n + 1) * 512],
                           in_=psB[:, 512:1024], inc="dve")
                p.wait("tensor", "dve", t3_)
            for hh in range(NHT):
                p.op("vector", "tensor_sub", acc_slice(hh, 0, TW),
                     acc_slice(hh, 0, TW), mb[:])
                p.op("vector", "tensor_mul", acc_slice(hh, 0, TW),
                     acc_slice(hh, 0, TW), rb[:], inc="dve")
            p.barrier()

    for c in range(n_chunks):
        t0 = c * TW
        kvw = (c + 1) * TW
        nwin = kvw // 512
        nqt = TW // 128

        with ExitStack() as xs_:
            xtm = [xs_.enter_context(
                nc.sbuf_tensor(f"a{c}_xtm{i}", [128, H], F32)) for i in range(2)]
            p.barrier()
            for tt in range(TW // 128):
                par = tt % 2
                p.wait("sync", "pe", p.tok("pe"))
                tl = p.dma("sync", f"qxm{par}", xtm[par][:], xv[t0 // 128 + tt])
                p.wait("tensor", f"qxm{par}", tl)
                for hh in range(NHT):
                    half = hh % 2
                    ttr = p.op("tensor", "matmul",
                               psB[:, half * 512:half * 512 + 128],
                               xtm[par][:, hh * 128:(hh + 1) * 128],
                               idf[:], start=True, stop=True, inc="pe")
                    p.wait("vector", "pe", ttr)
                    tev = p.op("vector", "tensor_copy",
                               out=acc_slice(hh, tt * 128, 128),
                               in_=psB[:, half * 512:half * 512 + 128], inc="dve")
                    p.wait("tensor", "dve", tev)
            p.barrier()

        with ExitStack() as as_:
            asb = lambda n, s, d: as_.enter_context(nc.sbuf_tensor(n, s, d))
            cq = asb(f"a{c}_cq", [128, S], BF16)
            sq = asb(f"a{c}_sq", [128, S], BF16)
            ck = asb(f"a{c}_ck", [128, S], BF16)
            sk = asb(f"a{c}_sk", [128, S], BF16)
            kbf = [asb(f"a{c}_k{j}", [128, kvw], BF16) for j in range(GH)]
            vtm = [asb(f"a{c}_v{j}", [128, kvw], BF16) for j in range(GH)]
            qbf = [asb(f"a{c}_q{j}", [128, TW], BF16) for j in range(GH)]
            ctxb = [asb(f"a{c}_c{j}", [128, TW], BF16) for j in range(GH)]
            pbf = asb(f"a{c}_pbf", [128, 8 * 2048], BF16)
            ptsb = [asb(f"a{c}_pt{i}", [128, 512], BF16) for i in range(2)]
            xwin = [asb(f"a{c}_xw0", [128, NHT * 512], BF16)]
            wslab = [asb(f"a{c}_ws{i}", [128, NHT * 128], BF16) for i in range(2)]
            raww = [asb(f"a{c}_rw{i}", [128, 512], F32) for i in range(2)]
            rotw = asb(f"a{c}_rot", [128, 512], F32)
            tmpw = asb(f"a{c}_tmp", [128, 512], F32)
            vwin = asb(f"a{c}_vw", [128, 512], BF16)
            biasc = asb(f"a{c}_bias", [128, 8], F32)
            esum = asb(f"a{c}_esum", [128, 8], F32)
            rcol = asb(f"a{c}_rcol", [128, 8], F32)
            unit_ctr = [0]
            ctx_pe_done = [0, 0]

            p.barrier()
            p.dma("sync", "qtab", cq[:], cq_d)
            t_t = p.dma("sync", "qtab", sq[:], sq_d)
            p.wait("vector", "qtab", t_t)
            p.op("vector", "tensor_scalar_mul", ck[:], cq[:], ISQ)
            t_tab = p.op("vector", "tensor_scalar_mul", sk[:], sq[:], ISQ,
                         inc="dve")
            p.wait("tensor", "dve", t_tab)

            slab_last = [0, 0]
            xw_last = [0, 0]
            slab_i = [0]

            def do_proj(widx, j, dst_kind, parw, nlo, g):
                pars = slab_i[0] % 2
                slab_i[0] += 1
                p.wait("gpsimd", "pe", slab_last[pars])
                p.dma("gpsimd", f"qws{pars}",
                      wslab[pars].rearrange("pp (hh cc) -> pp hh cc", hh=NHT),
                      wt_d[widx, g * GH + j].rearrange("hh pp cc -> pp hh cc"))
                p.wait("tensor", f"qws{pars}", p.tok(f"qws{pars}"))
                half = slab_i[0] % 2
                p.wait("tensor", "act", p.tok("act"))
                p.wait("tensor", "dve", p.tok("dve"))
                t_mm = 0
                for kk in range(NHT):
                    t_mm = p.op("tensor", "matmul",
                                psB[:, half * 512:(half + 1) * 512],
                                wslab[pars][:, kk * 128:(kk + 1) * 128],
                                xwin[parw][:, kk * 512:(kk + 1) * 512],
                                start=(kk == 0), stop=(kk == NHT - 1), inc="pe")
                slab_last[pars] = p.tok("pe")
                p.wait("scalar", "pe", t_mm)
                if dst_kind == "v":
                    t_ev = p.op("scalar", "copy", out=vwin[:],
                                in_=psB[:, half * 512:(half + 1) * 512], inc="act")
                    p.wait("tensor", "act", t_ev)
                    t_tr = 0
                    for tt4 in range(4):
                        t_tr = p.op("tensor", "transpose",
                                    psT[:, tt4 * 128:(tt4 + 1) * 128],
                                    vwin[:, tt4 * 128:(tt4 + 1) * 128],
                                    idb[:], inc="pe")
                    p.wait("scalar", "pe", t_tr)
                    p.op("scalar", "copy", out=vtm[j][:, nlo:nlo + 512],
                         in_=psT[:, 0:512], inc="act")
                else:
                    parr = slab_i[0] % 2
                    p.wait("scalar", "dve", p.tok("dve"))
                    t_ev = p.op("scalar", "copy", out=raww[parr][:],
                                in_=psB[:, half * 512:(half + 1) * 512], inc="act")
                    p.wait("vector", "act", t_ev)
                    X = raww[parr]
                    C = cq if dst_kind == "q" else ck
                    Sn = sq if dst_kind == "q" else sk
                    dst = (qbf[j][:, nlo - t0:nlo - t0 + 512] if dst_kind == "q"
                           else kbf[j][:, nlo:nlo + 512])
                    p.op("vector", "tensor_scalar_mul", rotw[0:64, :],
                         X[64:128, :], -1.0)
                    p.op("vector", "tensor_copy", out=rotw[64:128, :],
                         in_=X[0:64, :])
                    p.op("vector", "tensor_mul", tmpw[:], X[:],
                         C[:, nlo:nlo + 512])
                    p.op("vector", "tensor_mul", rotw[:], rotw[:],
                         Sn[:, nlo:nlo + 512])
                    p.op("vector", "tensor_add", dst, tmpw[:], rotw[:], inc="dve")

            for g in range(n_groups):
                for n in range(nwin):
                    nlo = n * 512
                    parw = 0
                    p.wait("sync", "pe", xw_last[parw])
                    t_xw = p.dma("sync", "qxw0",
                                 xwin[parw].rearrange("pp (kk tt2) -> pp kk tt2",
                                                      kk=NHT),
                                 xbv[:, :, nlo:nlo + 512].rearrange(
                                     "hh pp tt2 -> pp hh tt2"))
                    p.wait("tensor", "qxw0", t_xw)
                    for j in range(GH):
                        do_proj(1, j, "k", parw, nlo, g)
                    for j in range(GH):
                        do_proj(2, j, "v", parw, nlo, g)
                    if nlo >= t0:
                        for j in range(GH):
                            do_proj(0, j, "q", parw, nlo, g)
                    xw_last[parw] = p.tok("pe")
                    if XBAR & 4:
                        p.barrier()

                def emit_sm_qt(j, qc, jj, par):
                    sl = par * 4 + jj
                    qt = qc * 4 + jj
                    qi = c * nqt + qt
                    klen = (qi + 1) * 128
                    kmaxt = c * nqt + qc * 4 + 4
                    nch = ceil_div(klen, 512)
                    p.wait("tensor", "act", p.tok("act"))
                    p.wait("tensor", "dve", p.tok("dve"))
                    t_mm = 0
                    for nn in range(nch):
                        w = min(512, klen - nn * 512)
                        t_mm = p.op("tensor", "matmul",
                                    psA[:, nn * 512:nn * 512 + w],
                                    qbf[j][:, qt * 128:(qt + 1) * 128],
                                    kbf[j][:, nn * 512:nn * 512 + w],
                                    start=True, stop=True, inc="pe")
                    p.wait("vector", "pe", t_mm)
                    p.op("vector", "tensor_add", psA[:, qi * 128:klen],
                         psA[:, qi * 128:klen], dmask[:])
                    p.op("vector", "reduce_max", biasc[:, sl:sl + 1],
                         psA[:, 0:klen], axis=AX.X, negate=True)
                    t_b = p.drain_inc("vector", "dve")
                    p.wait("scalar", "dve", t_b)
                    p.wait("scalar", "pe", ctx_pe_done[par])
                    t_e = p.op("scalar", "activation",
                               out=pbf[:, sl * 2048:sl * 2048 + klen],
                               in_=psA[:, 0:klen], func=AF.Exp,
                               bias=biasc[:, sl:sl + 1], scale=1.0,
                               accum_out=esum[:, sl:sl + 1], inc="act")
                    p.wait("vector", "act", t_e)
                    p.op("vector", "reciprocal", rcol[:, sl:sl + 1],
                         esum[:, sl:sl + 1])
                    p.op("vector", "drain")
                    kmax = kmaxt * 128
                    if klen < kmax:
                        p.op("vector", "memset",
                             pbf[:, sl * 2048 + klen:sl * 2048 + kmax], 0.0)
                    p.op("vector", "tensor_scalar_mul",
                         pbf[:, sl * 2048:sl * 2048 + klen],
                         pbf[:, sl * 2048:sl * 2048 + klen],
                         rcol[:, sl:sl + 1], inc="dve")

                def emit_ctx_portion(st, kj_from, kj_to):
                    j, qc, par, kmaxt, dve_tok = st
                    p.wait("tensor", "dve", dve_tok)
                    t_cx = 0
                    for kj in range(kj_from, kj_to):
                        half = kj % 2
                        if kj >= 2:
                            p.wait("tensor", "act", p.tok("act"))
                        t_tr = 0
                        for jj in range(4):
                            sl = par * 4 + jj
                            t_tr = p.op("tensor", "transpose",
                                        psT[:, half * 512 + jj * 128:
                                            half * 512 + (jj + 1) * 128],
                                        pbf[:, sl * 2048 + kj * 128:
                                            sl * 2048 + (kj + 1) * 128],
                                        idb[:], inc="pe")
                        p.wait("scalar", "pe", t_tr)
                        t_ev = p.op("scalar", "copy", out=ptsb[half][:],
                                    in_=psT[:, half * 512:(half + 1) * 512],
                                    inc="act")
                        p.wait("tensor", "act", t_ev)
                        t_cx = p.op("tensor", "matmul", psC[:],
                                    vtm[j][:, kj * 128:(kj + 1) * 128],
                                    ptsb[half][:],
                                    start=(kj == 0), stop=(kj == kmaxt - 1),
                                    inc="pe")
                    if kj_to == kmaxt:
                        ctx_pe_done[par] = p.tok("pe")
                        p.wait("vector", "pe", t_cx)
                        t_cv = p.op("vector", "tensor_copy",
                                    out=ctxb[j][:, qc * 512:(qc + 1) * 512],
                                    in_=psC[:], inc="dve")
                        p.wait("tensor", "dve", t_cv)

                prev = None
                for j in range(GH):
                    for qc in range(TW // 512):
                        kmaxt = c * nqt + qc * 4 + 4
                        par = unit_ctr[0] % 2
                        unit_ctr[0] += 1
                        splits = None
                        if prev is not None:
                            km = prev[3]
                            splits = [0, km // 4, km // 2, (3 * km) // 4, km]
                        for jj in range(4):
                            emit_sm_qt(j, qc, jj, par)
                            if splits is not None:
                                emit_ctx_portion(prev, splits[jj], splits[jj + 1])
                        prev = (j, qc, par, kmaxt, p.tok("dve"))
                if prev is not None:
                    emit_ctx_portion(prev, 0, prev[3])
                    prev = None

                for m in range(NHT):
                    p.wait("gpsimd", "pe", p.tok("pe"))
                    p.dma("gpsimd", "qws0",
                          wslab[0][:, 0:GH * 128].rearrange(
                              "pp (jh cc) -> pp jh cc", jh=GH),
                          wt_d[3, m, g * GH:(g + 1) * GH].rearrange(
                              "jh pp cc -> pp jh cc"))
                    p.wait("tensor", "qws0", p.tok("qws0"))
                    for n in range(TW // 512):
                        half = n % 2
                        p.wait("tensor", "dve", p.tok("dve"))
                        t_mm = 0
                        for j in range(GH):
                            t_mm = p.op("tensor", "matmul",
                                        psB[:, half * 512:(half + 1) * 512],
                                        wslab[0][:, j * 128:(j + 1) * 128],
                                        ctxb[j][:, n * 512:(n + 1) * 512],
                                        start=(j == 0), stop=(j == GH - 1),
                                        inc="pe")
                        p.wait("vector", "pe", t_mm)
                        p.op("vector", "tensor_add",
                             acc_slice(m, n * 512, 512),
                             acc_slice(m, n * 512, 512),
                             psB[:, half * 512:(half + 1) * 512], inc="dve")
                slab_last[0] = slab_last[1] = p.tok("pe")
            p.barrier()
        dump_acc(f"dbg_res1_{c}")

        layernorm()
        dump_acc(f"dbg_h1_{c}")

        with ExitStack() as fs_:
            fsb = lambda n, s, d: fs_.enter_context(nc.sbuf_tensor(n, s, d))
            h1bf = fsb(f"f{c}_h1", [128, NHT * TW], BF16)
            w_em = fsb(f"f{c}_wem", [NE, TW], F32)
            zrow = fsb(f"f{c}_z", [128, NE], F32)
            m8 = fsb(f"f{c}_m8", [128, 8], F32)
            mrow = fsb(f"f{c}_msk", [128, NE], F32)
            srow = fsb(f"f{c}_srow", [128, 4], F32)
            wrow = fsb(f"f{c}_wrow", [1, TW], F32)
            wbc = [fsb(f"f{c}_wbc{i}", [128, TW], F32) for i in range(2)]
            gslab = [fsb(f"f{c}_gs{i}", [128, NHT * 128], BF16) for i in range(2)]
            uslab = [fsb(f"f{c}_us{i}", [128, NHT * 128], BF16) for i in range(2)]
            dslab = [[fsb(f"f{c}_ds{e4}_{i}", [128, NIT * 128], BF16)
                      for i in range(2)] for e4 in range(EG)]
            t1 = [fsb(f"f{c}_t1{i}", [128, TW], F32) for i in range(2)]
            gu = [fsb(f"f{c}_gu{e4}", [128, NIT * TW], BF16) for e4 in range(EG)]

            for hh in range(NHT):
                p.op("vector", "tensor_copy",
                     out=h1bf[:, hh * TW:(hh + 1) * TW],
                     in_=acc_slice(hh, 0, TW), inc="dve")

            for tt in range(TW // 128):
                reg = (tt % 2) * NE
                p.wait("tensor", "act", p.tok("act"))
                t_lg = 0
                for kk in range(NHT):
                    t_lg = p.op("tensor", "matmul", psC[:, reg:reg + NE],
                                acc_slice(kk, tt * 128, 128),
                                gwT[:, kk * NE:(kk + 1) * NE],
                                start=(kk == 0), stop=(kk == NHT - 1), inc="pe")
                p.wait("vector", "pe", t_lg)
                p.op("vector", "reduce_max", srow[:, 0:1], psC[:, reg:reg + NE],
                     axis=AX.X, negate=True)
                t_b = p.drain_inc("vector", "dve")
                p.wait("scalar", "dve", t_b)
                t_z = p.op("scalar", "activation", out=zrow[:],
                           in_=psC[:, reg:reg + NE], func=AF.Exp,
                           bias=srow[:, 0:1], scale=1.0, inc="act")
                p.wait("vector", "act", t_z)
                p.op("vector", "max", out=m8[:], in_=zrow[:])
                p.op("vector", "drain")
                p.op("vector", "tensor_single_scalar", mrow[:], zrow[:],
                     m8[:, 7:8], OP.is_ge)
                p.op("vector", "drain")
                p.op("vector", "tensor_mul", mrow[:], mrow[:], zrow[:])
                p.op("vector", "drain")
                p.op("vector", "reduce_sum", srow[:, 1:2], mrow[:], axis=AX.X)
                p.op("vector", "drain")
                p.op("vector", "reciprocal", srow[:, 2:3], srow[:, 1:2])
                p.op("vector", "drain")
                t_w = p.op("vector", "tensor_scalar_mul", mrow[:], mrow[:],
                           srow[:, 2:3], inc="dve")
                p.wait("tensor", "dve", t_w)
                t_tr = p.op("tensor", "matmul", psB[0:NE, 0:128], mrow[:],
                            idf[:], start=True, stop=True, inc="pe")
                p.wait("vector", "pe", t_tr)
                p.op("vector", "tensor_copy",
                     out=w_em[:, tt * 128:(tt + 1) * 128],
                     in_=psB[0:NE, 0:128], inc="dve")
            p.barrier(("tensor", "vector", "scalar", "gpsimd"))

            wgv = wg_d.rearrange("e (kk pp) i -> e kk pp i", pp=128)
            wuv = wu_d.rearrange("e (kk pp) i -> e kk pp i", pp=128)
            wdv = wd_d.rearrange("e (kk pp) h -> e kk pp h", pp=128)
            gs_last = [0, 0]
            ds_last = {}
            gu_last = [0] * EG
            for egi in range(n_experts // EG):
                for e4 in range(EG):
                    e = egi * EG + e4
                    parw = e % 2
                    p.wait("gpsimd", "pe", p.tok("pe"))
                    t_wr = p.dma("gpsimd", "qwr", wrow[:], w_em[e:e + 1, :])
                    p.wait("tensor", "qwr", t_wr)
                    for n in range(TW // 512):
                        p.wait("tensor", "dve", p.tok("dve"))
                        t_bc = p.op("tensor", "matmul", psC[:], ones[0:1, :],
                                    wrow[0:1, n * 512:(n + 1) * 512],
                                    start=True, stop=True, inc="pe")
                        p.wait("vector", "pe", t_bc)
                        p.op("vector", "tensor_copy",
                             out=wbc[parw][:, n * 512:(n + 1) * 512],
                             in_=psC[:], inc="dve")
                    for m in range(NIT):
                        pars = m % 2
                        p.wait("gpsimd", "pe", gs_last[pars])
                        t_g = p.dma("gpsimd", f"qgs{pars}",
                                    gslab[pars].rearrange("pp (kk ii) -> pp kk ii",
                                                          kk=NHT),
                                    wgv[e, :, :, m * 128:(m + 1) * 128].rearrange(
                                        "kk pp ii -> pp kk ii"))
                        t_u = p.dma("gpsimd", f"qus{pars}",
                                    uslab[pars].rearrange("pp (kk ii) -> pp kk ii",
                                                          kk=NHT),
                                    wuv[e, :, :, m * 128:(m + 1) * 128].rearrange(
                                        "kk pp ii -> pp kk ii"))
                        p.wait("tensor", f"qgs{pars}", t_g)
                        p.wait("tensor", "act", p.tok("act"))
                        t_gm = 0
                        for kk in range(NHT):
                            for n in range(TW // 512):
                                t_gm = p.op("tensor", "matmul",
                                            psA[:, n * 512:(n + 1) * 512],
                                            gslab[pars][:, kk * 128:(kk + 1) * 128],
                                            h1bf[:, kk * TW + n * 512:
                                                 kk * TW + (n + 1) * 512],
                                            start=(kk == 0), stop=(kk == NHT - 1),
                                            inc="pe")
                        p.wait("scalar", "pe", t_gm)
                        part = m % 2
                        p.wait("scalar", "dve", p.tok("dve"))
                        t_si = p.op("scalar", "activation", out=t1[part][:],
                                    in_=psA[:, 0:TW], func=AF.Silu, inc="act")
                        p.wait("tensor", f"qus{pars}", t_u)
                        p.wait("tensor", "dve", p.tok("dve"))
                        t_um = 0
                        for kk in range(NHT):
                            for n in range(TW // 512):
                                t_um = p.op("tensor", "matmul",
                                            psA[:, TW + n * 512:TW + (n + 1) * 512],
                                            uslab[pars][:, kk * 128:(kk + 1) * 128],
                                            h1bf[:, kk * TW + n * 512:
                                                 kk * TW + (n + 1) * 512],
                                            start=(kk == 0), stop=(kk == NHT - 1),
                                            inc="pe")
                        gs_last[pars] = p.tok("pe")
                        p.wait("vector", "act", t_si)
                        p.wait("vector", "pe", t_um)
                        p.wait("vector", "pe", gu_last[e4])
                        p.op("vector", "tensor_mul", t1[part][:], t1[part][:],
                             wbc[parw][:])
                        p.op("vector", "tensor_mul",
                             gu[e4][:, m * TW:(m + 1) * TW],
                             t1[part][:], psA[:, TW:2 * TW], inc="dve")
                    if XBAR & 1:
                        p.barrier()
                for m2 in range(NHT):
                    par2 = m2 % 2
                    for e4 in range(EG):
                        e = egi * EG + e4
                        p.wait("gpsimd", "pe", ds_last.get((e4, par2), 0))
                        t_d = p.dma("gpsimd", f"qds{e4}_{par2}",
                                    dslab[e4][par2].rearrange(
                                        "pp (kk hh2) -> pp kk hh2", kk=NIT),
                                    wdv[e, :, :, m2 * 128:(m2 + 1) * 128].rearrange(
                                        "kk pp hh2 -> pp kk hh2"))
                        p.wait("tensor", f"qds{e4}_{par2}", t_d)
                    p.wait("tensor", "dve", p.tok("dve"))
                    for n in range(TW // 512):
                        t_dm = 0
                        for e4 in range(EG):
                            for kk in range(NIT):
                                t_dm = p.op("tensor", "matmul",
                                            psB[:, n * 512:(n + 1) * 512],
                                            dslab[e4][par2][:,
                                                            kk * 128:(kk + 1) * 128],
                                            gu[e4][:, kk * TW + n * 512:
                                                   kk * TW + (n + 1) * 512],
                                            start=(e4 == 0 and kk == 0),
                                            stop=(e4 == EG - 1 and kk == NIT - 1),
                                            inc="pe")
                        p.wait("vector", "pe", t_dm)
                        p.op("vector", "tensor_add",
                             acc_slice(m2, n * 512, 512),
                             acc_slice(m2, n * 512, 512),
                             psB[:, n * 512:(n + 1) * 512], inc="dve")
                    for e4 in range(EG):
                        ds_last[(e4, par2)] = p.tok("pe")
                    if XBAR & 2:
                        p.barrier()
                gu_last = [p.tok("pe")] * EG
            p.barrier()

        layernorm()
        dump_acc(f"dbg_h2_{c}")

        with ExitStack() as os_:
            ost = [os_.enter_context(
                nc.sbuf_tensor(f"o{c}_st{i}", [128, H], F32)) for i in range(2)]
            st_last = [0, 0]
            for tt in range(TW // 128):
                par = tt % 2
                p.wait("vector", p.qctr(f"qos{par}"), st_last[par])
                t_ev = 0
                for hh in range(NHT):
                    half = hh % 2
                    t_tr = p.op("tensor", "matmul",
                                psB[:, half * 512:half * 512 + 128],
                                acc_slice(hh, tt * 128, 128), idf[:],
                                start=True, stop=True, inc="pe")
                    p.wait("vector", "pe", t_tr)
                    t_ev = p.op("vector", "tensor_copy",
                                out=ost[par][:, hh * 128:(hh + 1) * 128],
                                in_=psB[:, half * 512:half * 512 + 128],
                                inc="dve")
                    p.wait("tensor", "dve", t_ev)
                p.wait("sync", "dve", t_ev)
                st_last[par] = p.dma("sync", f"qos{par}", ov[t0 // 128 + tt],
                                     ost[par][:])
            p.barrier()

    for qn in list(p.ctr):
        if qn.startswith("qos") or qn == "qdp":
            p.wait("sync", qn, p.tok(qn))

    with nc.Block() as block:
        for ename in Plan.ENGINES:
            getattr(block, ename)(lambda eng, q=p.q[ename]: [f(eng) for f in q])
    ctx.close()
    return nc


_ROPE_BASE = 10000.0


def _host_constants():
    half = HD // 2
    inv = (1.0 / _ROPE_BASE ** (np.arange(0, HD, 2, dtype=np.float32) / HD))
    t = np.arange(S, dtype=np.float32)
    freqs = np.einsum("i,j->ij", t, inv.astype(np.float32)).astype(np.float32)
    cos = np.cos(freqs).astype(np.float32)
    sin = np.sin(freqs).astype(np.float32)
    C = np.empty((HD, S), np.float32)
    Sn = np.empty((HD, S), np.float32)
    C[:half] = cos.T
    C[half:] = cos.T
    Sn[:half] = sin.T
    Sn[half:] = sin.T
    return {
        "ropec": C.astype(ml_dtypes.bfloat16),
        "ropes": Sn.astype(ml_dtypes.bfloat16),
        "dmask": np.triu(np.full((128, 128), NEG, np.float32), k=1),
        "idf": np.eye(128, dtype=np.float32),
        "idb": np.eye(128).astype(ml_dtypes.bfloat16),
        "ones": np.ones((128, 128), np.float32),
    }


def kernel(L0_attn_hidden_states, L0_attn_Wq, L0_attn_Wk, L0_attn_Wv, L0_attn_Wo,
           L0_ffn_gate_w, L0_ffn_w_gate, L0_ffn_w_up, L0_ffn_w_down):
    x = np.ascontiguousarray(np.asarray(L0_attn_hidden_states, dtype=np.float32))
    shared = {
        "wq": np.ascontiguousarray(np.asarray(L0_attn_Wq, np.float32)),
        "wk": np.ascontiguousarray(np.asarray(L0_attn_Wk, np.float32)),
        "wv": np.ascontiguousarray(np.asarray(L0_attn_Wv, np.float32)),
        "wo": np.ascontiguousarray(np.asarray(L0_attn_Wo, np.float32)),
        "gw": np.ascontiguousarray(np.asarray(L0_ffn_gate_w, np.float32)),
        "wgate": np.ascontiguousarray(np.asarray(L0_ffn_w_gate, np.float32)),
        "wup": np.ascontiguousarray(np.asarray(L0_ffn_w_up, np.float32)),
        "wdown": np.ascontiguousarray(np.asarray(L0_ffn_w_down, np.float32)),
        **_host_constants(),
    }
    nc = build()
    in_maps = [dict(shared, x=np.ascontiguousarray(x[b])) for b in range(8)]
    res = run_bass_kernel_spmd(nc, in_maps, core_ids=list(range(8)))
    return np.stack([r["out"] for r in res.results], axis=0)


# version 04
# speedup vs baseline: 1.0253x; 1.0121x over previous
import math
import numpy as np
import ml_dtypes
from contextlib import ExitStack

import concourse.bass as bass
import concourse.mybir as mybir
from concourse.bass_utils import run_bass_kernel_spmd

F32 = mybir.dt.float32
BF16 = mybir.dt.bfloat16
AF = mybir.ActivationFunctionType
AX = mybir.AxisListType
OP = mybir.AluOpType

S = 2048
H = 2048
NH = 16
HD = 128
NE = 32
DI = 768
NCHUNK = 2
TW = S // NCHUNK
EPS = 1e-5
ISQ = 1.0 / math.sqrt(HD)
NHT = H // 128
NIT = DI // 128
NG = 4
GH = NH // NG
EG = 4
NEG = -1e9
import os as _os
XBAR = int(_os.environ.get('KBAR', '0'))


class Plan:

    ENGINES = ("tensor", "vector", "scalar", "gpsimd", "sync")

    def __init__(self, nc, ctx):
        self.nc = nc
        self.ctx = ctx
        self.q = {e: [] for e in self.ENGINES}
        self.ctr = {}
        self.sem = {}
        self.step = {}
        self.lastw = {}
        for name in ("pe", "dve", "act", "pool"):
            self._mk(name, 1)

    def _mk(self, name, step):
        self.sem[name] = self.ctx.enter_context(self.nc.semaphore(name))
        self.ctr[name] = 0
        self.step[name] = step

    def qctr(self, name):
        if name not in self.ctr:
            self._mk(name, 16)
        return name

    def wait(self, engine, name, val=None):
        val = self.ctr[name] if val is None else val
        if val <= 0 or self.lastw.get((engine, name), -1) >= val:
            return
        self.lastw[(engine, name)] = val
        sem = self.sem[name]
        self.q[engine].append(lambda eng, s=sem, v=val: eng.wait_ge(s, v))

    def op(self, engine, method, *args, inc=None, **kwargs):
        if inc is None:
            self.q[engine].append(
                lambda eng, m=method, a=args, kw=kwargs: getattr(eng, m)(*a, **kw))
            return None
        self.ctr[inc] += self.step[inc]
        sem, k = self.sem[inc], self.step[inc]
        self.q[engine].append(
            lambda eng, m=method, a=args, kw=kwargs, sem=sem, k=k:
            getattr(eng, m)(*a, **kw).then_inc(sem, k))
        return self.ctr[inc]

    def drain_inc(self, engine, inc):
        return self.op(engine, "drain", inc=inc)

    def dma(self, engine, qname, outap, inap):
        qname = self.qctr(qname)
        self.wait(engine, qname)
        return self.op(engine, "dma_start", inc=qname, out=outap, in_=inap)

    def tok(self, name):
        return self.ctr[name]

    def barrier(self, engines=None):
        engines = engines or self.ENGINES
        for e in engines:
            for c_ in list(self.ctr):
                self.wait(e, c_)


def ceil_div(a, b):
    return (a + b - 1) // b


_uid = [0]


def _u():
    _uid[0] += 1
    return _uid[0]


def build(n_chunks=NCHUNK, n_groups=NG, n_experts=NE, debug_dumps=()):
    nc = bass.Bass("TRN2", target_bir_lowering=False, debug=False)

    x_d = nc.dram_tensor("x", [S, H], F32, kind="ExternalInput").ap()
    wq_d = nc.dram_tensor("wq", [H, H], F32, kind="ExternalInput").ap()
    wk_d = nc.dram_tensor("wk", [H, H], F32, kind="ExternalInput").ap()
    wv_d = nc.dram_tensor("wv", [H, H], F32, kind="ExternalInput").ap()
    wo_d = nc.dram_tensor("wo", [H, H], F32, kind="ExternalInput").ap()
    gw_d = nc.dram_tensor("gw", [NE, H], F32, kind="ExternalInput").ap()
    wg_d = nc.dram_tensor("wgate", [NE, H, DI], F32, kind="ExternalInput").ap()
    wu_d = nc.dram_tensor("wup", [NE, H, DI], F32, kind="ExternalInput").ap()
    wd_d = nc.dram_tensor("wdown", [NE, DI, H], F32, kind="ExternalInput").ap()
    cq_d = nc.dram_tensor("ropec", [HD, S], BF16, kind="ExternalInput").ap()
    sq_d = nc.dram_tensor("ropes", [HD, S], BF16, kind="ExternalInput").ap()
    dmask_d = nc.dram_tensor("dmask", [128, 128], F32, kind="ExternalInput").ap()
    idf_d = nc.dram_tensor("idf", [128, 128], F32, kind="ExternalInput").ap()
    idb_d = nc.dram_tensor("idb", [128, 128], BF16, kind="ExternalInput").ap()
    ones_d = nc.dram_tensor("ones", [128, 128], F32, kind="ExternalInput").ap()
    out_d = nc.dram_tensor("out", [S, H], F32, kind="ExternalOutput").ap()
    dbg = {}
    for name in debug_dumps:
        if name:
            dbg[name] = nc.dram_tensor(name, [128, NHT * TW], F32,
                                       kind="ExternalOutput").ap()

    xbf_d = nc.dram_tensor("xbf_i", [H, S], BF16, kind="Internal").ap()
    wt_d = nc.dram_tensor("wt_i", [4, NHT, NHT, 128, 128], BF16, kind="Internal").ap()

    ctx = ExitStack()
    sb = lambda n, s, d: ctx.enter_context(nc.sbuf_tensor(n, s, d))
    p = Plan(nc, ctx)

    acc = sb("acc", [128, NHT * TW], F32)
    idf = sb("idf_t", [128, 128], F32)
    idb = sb("idb_t", [128, 128], BF16)
    ones = sb("ones_t", [128, 128], F32)
    dmask = sb("dmask_t", [128, 128], F32)
    gwT = sb("gwT", [128, NHT * NE], F32)

    psA = ctx.enter_context(nc.psum_tensor("psA", [128, 2048], F32))
    psB = ctx.enter_context(nc.psum_tensor("psB", [128, 512], F32))
    psT = ctx.enter_context(nc.psum_tensor("psT", [128, 2048], BF16))
    psC = ctx.enter_context(nc.psum_tensor("psC", [128, 512], F32))

    xv = x_d.rearrange("(tt pp) h -> tt pp h", pp=128)
    ov = out_d.rearrange("(tt pp) h -> tt pp h", pp=128)
    xbv = xbf_d.rearrange("(hh pp) t -> hh pp t", pp=128)

    p.dma("sync", "qc", idf[:], idf_d)
    p.dma("sync", "qc", idb[:], idb_d)
    p.dma("sync", "qc", ones[:], ones_d)
    t_cst = p.dma("sync", "qc", dmask[:], dmask_d)

    def acc_slice(hh, lo, w):
        return acc[:, hh * TW + lo: hh * TW + lo + w]

    with ExitStack() as ps_:
        psb_ = lambda n, s, d: ps_.enter_context(nc.sbuf_tensor(n, s, d))
        xt = [psb_(f"pr_xt{i}", [128, H], F32) for i in range(2)]
        xb = [psb_(f"pr_xb{i}", [128, H], BF16) for i in range(2)]
        stg = [psb_(f"pr_st{i}", [128, NHT * 128], BF16) for i in range(2)]
        gwsb = psb_("pr_gw", [NE, H], F32)
        p.wait("tensor", "qc", t_cst)

        t_gw = p.dma("sync", "qc", gwsb[:], gw_d)
        p.wait("tensor", "qc", t_gw)
        for kk in range(NHT):
            t_tr = p.op("tensor", "matmul",
                        psB[:, 0:128],
                        gwsb[:, kk * 128:(kk + 1) * 128], idf[0:NE, :],
                        start=True, stop=True, inc="pe")
            p.wait("vector", "pe", t_tr)
            t_ev = p.op("vector", "tensor_copy",
                        out=gwT[:, kk * NE:(kk + 1) * NE],
                        in_=psB[:, 0:NE], inc="dve")
            p.wait("tensor", "dve", t_ev)

        def prep_one(idx, src_rows_f32, dst_view):
            par = idx % 2
            p.wait("sync", "dve", p.tok("dve"))
            p.wait("scalar", p.qctr(f"qst{par}"), p.tok(p.qctr(f"qst{par}")))
            t_load = p.dma("sync", f"qxt{par}", xt[par][:], src_rows_f32)
            p.wait("vector", f"qxt{par}", t_load)
            p.wait("vector", "pe", p.tok("pe"))
            t_cast = p.op("vector", "tensor_copy", out=xb[par][:], in_=xt[par][:],
                          inc="dve")
            p.wait("tensor", "dve", t_cast)
            t_ev = 0
            for hh in range(NHT):
                half = hh % 2
                if t_ev:
                    p.wait("tensor", "act", t_ev)
                t_tr = p.op("tensor", "transpose",
                            psT[:, half * 1024:half * 1024 + 128],
                            xb[par][:, hh * 128:(hh + 1) * 128], idb[:], inc="pe")
                p.wait("scalar", "pe", t_tr)
                t_ev = p.op("scalar", "copy",
                            out=stg[par][:, hh * 128:(hh + 1) * 128],
                            in_=psT[:, half * 1024:half * 1024 + 128], inc="act")
            p.wait("gpsimd", "act", t_ev)
            p.dma("gpsimd", f"qst{par}", dst_view.rearrange("hh pp cc -> pp hh cc"),
                  stg[par].rearrange("pp (hh cc) -> pp hh cc", hh=NHT))

        for tt in range(NHT):
            prep_one(tt, xv[tt], xbv[:, :, tt * 128:(tt + 1) * 128])
        for wi, wsrc in enumerate([wq_d, wk_d, wv_d, wo_d]):
            wvv = wsrc.rearrange("(ot pp) h -> ot pp h", pp=128)
            for ot in range(NHT):
                prep_one(wi * NHT + ot, wvv[ot], wt_d[wi, ot])
        p.barrier()

    def dump_acc(tag):
        if tag not in dbg:
            return
        with ExitStack() as ds:
            stg2 = ds.enter_context(
                nc.sbuf_tensor(f"dump{_u()}", [128, NHT * TW], F32))
            p.barrier(("vector",))
            t = p.op("vector", "tensor_copy", out=stg2[:], in_=acc[:], inc="dve")
            p.wait("sync", "dve", t)
            t2 = p.dma("sync", "qdp", dbg[tag], stg2[:])
            p.wait("sync", "qdp", t2)
            p.barrier()

    def layernorm():
        u = _u()
        with ExitStack() as ls:
            lsb = lambda n, s, d: ls.enter_context(nc.sbuf_tensor(n, s, d))
            sqs = [lsb(f"ln{u}_sq{i}", [128, TW], F32) for i in range(2)]
            rows = lsb(f"ln{u}_rows", [1, 3 * TW], F32)
            mb = lsb(f"ln{u}_mb", [128, TW], F32)
            rb = lsb(f"ln{u}_rb", [128, TW], F32)
            p.barrier()
            tmm = 0
            for hh in range(NHT):
                par = hh % 2
                p.wait("scalar", "pe", p.tok("pe"))
                t_sq = p.op("scalar", "activation", out=sqs[par][:],
                            in_=acc_slice(hh, 0, TW), func=AF.Square, inc="act")
                for n in range(TW // 512):
                    p.op("tensor", "matmul",
                         psA[0:1, n * 512:(n + 1) * 512],
                         ones[:, 0:1], acc_slice(hh, n * 512, 512),
                         start=(hh == 0), stop=(hh == NHT - 1), inc="pe")
                p.wait("tensor", "act", t_sq)
                for n in range(TW // 512):
                    tmm = p.op("tensor", "matmul",
                               psA[0:1, TW + n * 512: TW + (n + 1) * 512],
                               ones[:, 0:1], sqs[par][:, n * 512:(n + 1) * 512],
                               start=(hh == 0), stop=(hh == NHT - 1), inc="pe")
            p.wait("vector", "pe", tmm)
            p.op("vector", "tensor_scalar_mul", rows[0:1, 0:TW],
                 psA[0:1, 0:TW], 1.0 / H)
            p.op("vector", "tensor_scalar_mul", rows[0:1, TW:2 * TW],
                 psA[0:1, TW:2 * TW], 1.0 / H)
            p.op("vector", "drain")
            p.op("vector", "tensor_mul", rows[0:1, 2 * TW:3 * TW],
                 rows[0:1, 0:TW], rows[0:1, 0:TW])
            p.op("vector", "drain")
            p.op("vector", "tensor_sub", rows[0:1, TW:2 * TW],
                 rows[0:1, TW:2 * TW], rows[0:1, 2 * TW:3 * TW])
            p.op("vector", "drain")
            p.op("vector", "tensor_scalar_add", rows[0:1, TW:2 * TW],
                 rows[0:1, TW:2 * TW], EPS)
            t = p.drain_inc("vector", "dve")
            p.wait("scalar", "dve", t)
            t = p.op("scalar", "activation", out=rows[0:1, TW:2 * TW],
                     in_=rows[0:1, TW:2 * TW], func=AF.Sqrt, inc="act")
            p.wait("vector", "act", t)
            p.op("vector", "reciprocal", rows[0:1, TW:2 * TW],
                 rows[0:1, TW:2 * TW])
            t = p.drain_inc("vector", "dve")
            p.wait("tensor", "dve", t)
            for n in range(TW // 512):
                t1_ = p.op("tensor", "matmul", psB[:, 0:512], ones[0:1, :],
                           rows[0:1, n * 512:(n + 1) * 512],
                           start=True, stop=True, inc="pe")
                p.wait("vector", "pe", t1_)
                t3_ = p.op("vector", "tensor_copy",
                           out=mb[:, n * 512:(n + 1) * 512],
                           in_=psB[:, 0:512], inc="dve")
                p.wait("tensor", "dve", t3_)
                t2_ = p.op("tensor", "matmul", psB[:, 0:512], ones[0:1, :],
                           rows[0:1, TW + n * 512: TW + (n + 1) * 512],
                           start=True, stop=True, inc="pe")
                p.wait("vector", "pe", t2_)
                t3_ = p.op("vector", "tensor_copy",
                           out=rb[:, n * 512:(n + 1) * 512],
                           in_=psB[:, 0:512], inc="dve")
                p.wait("tensor", "dve", t3_)
            for hh in range(NHT):
                p.op("vector", "tensor_sub", acc_slice(hh, 0, TW),
                     acc_slice(hh, 0, TW), mb[:])
                p.op("vector", "tensor_mul", acc_slice(hh, 0, TW),
                     acc_slice(hh, 0, TW), rb[:], inc="dve")
            p.barrier()

    for c in range(n_chunks):
        t0 = c * TW
        kvw = (c + 1) * TW
        nwin = kvw // 512
        nqt = TW // 128

        with ExitStack() as xs_:
            xtm = [xs_.enter_context(
                nc.sbuf_tensor(f"a{c}_xtm{i}", [128, H], F32)) for i in range(2)]
            p.barrier()
            for tt in range(TW // 128):
                par = tt % 2
                p.wait("sync", "pe", p.tok("pe"))
                tl = p.dma("sync", f"qxm{par}", xtm[par][:], xv[t0 // 128 + tt])
                p.wait("tensor", f"qxm{par}", tl)
                for hh in range(NHT):
                    ttr = p.op("tensor", "matmul",
                               psB[:, 0:128],
                               xtm[par][:, hh * 128:(hh + 1) * 128],
                               idf[:], start=True, stop=True, inc="pe")
                    p.wait("vector", "pe", ttr)
                    tev = p.op("vector", "tensor_copy",
                               out=acc_slice(hh, tt * 128, 128),
                               in_=psB[:, 0:128], inc="dve")
                    p.wait("tensor", "dve", tev)
            p.barrier()

        with ExitStack() as as_:
            asb = lambda n, s, d: as_.enter_context(nc.sbuf_tensor(n, s, d))
            cq = asb(f"a{c}_cq", [128, S], BF16)
            sq = asb(f"a{c}_sq", [128, S], BF16)
            ck = asb(f"a{c}_ck", [128, S], BF16)
            sk = asb(f"a{c}_sk", [128, S], BF16)
            kbf = [asb(f"a{c}_k{j}", [128, kvw], BF16) for j in range(GH)]
            vtm = [asb(f"a{c}_v{j}", [128, kvw], BF16) for j in range(GH)]
            qbf = [asb(f"a{c}_q{j}", [128, TW], BF16) for j in range(GH)]
            ctxb = [asb(f"a{c}_c{j}", [128, TW], BF16) for j in range(GH)]
            pbf = asb(f"a{c}_pbf", [128, 8 * 2048], BF16)
            ptsb = [asb(f"a{c}_pt{i}", [128, 512], BF16) for i in range(2)]
            xwin = [asb(f"a{c}_xw0", [128, NHT * 512], BF16)]
            wslab = [asb(f"a{c}_ws{i}", [128, NHT * 128], BF16) for i in range(2)]
            raww = [asb(f"a{c}_rw{i}", [128, 512], F32) for i in range(2)]
            rotw = asb(f"a{c}_rot", [128, 512], F32)
            tmpw = asb(f"a{c}_tmp", [128, 512], F32)
            vwin = asb(f"a{c}_vw", [128, 512], BF16)
            biasc = asb(f"a{c}_bias", [128, 8], F32)
            esum = asb(f"a{c}_esum", [128, 8], F32)
            rcol = asb(f"a{c}_rcol", [128, 8], F32)
            unit_ctr = [0]
            ctx_pe_done = [0, 0]
            psa_exp_tok = [0]
            psa_mask_tok = [0]
            psT_free = [0, 0]
            psb_act_tok = [0]
            psb_dve_tok = [0]
            vpar = [0]

            p.barrier()
            p.dma("sync", "qtab", cq[:], cq_d)
            t_t = p.dma("sync", "qtab", sq[:], sq_d)
            p.wait("vector", "qtab", t_t)
            p.op("vector", "tensor_scalar_mul", ck[:], cq[:], ISQ)
            t_tab = p.op("vector", "tensor_scalar_mul", sk[:], sq[:], ISQ,
                         inc="dve")
            p.wait("tensor", "dve", t_tab)

            slab_last = [0, 0]
            xw_last = [0, 0]
            slab_i = [0]

            def do_proj(widx, j, dst_kind, parw, nlo, g):
                pars = slab_i[0] % 2
                slab_i[0] += 1
                p.wait("gpsimd", "pe", slab_last[pars])
                p.dma("gpsimd", f"qws{pars}",
                      wslab[pars].rearrange("pp (hh cc) -> pp hh cc", hh=NHT),
                      wt_d[widx, g * GH + j].rearrange("hh pp cc -> pp hh cc"))
                p.wait("tensor", f"qws{pars}", p.tok(f"qws{pars}"))
                p.wait("tensor", "act", psb_act_tok[0])
                p.wait("tensor", "dve", psb_dve_tok[0])
                t_mm = 0
                for kk in range(NHT):
                    t_mm = p.op("tensor", "matmul",
                                psB[:, 0:512],
                                wslab[pars][:, kk * 128:(kk + 1) * 128],
                                xwin[parw][:, kk * 512:(kk + 1) * 512],
                                start=(kk == 0), stop=(kk == NHT - 1), inc="pe")
                slab_last[pars] = p.tok("pe")
                p.wait("scalar", "pe", t_mm)
                if dst_kind == "v":
                    t_ev = p.op("scalar", "copy", out=vwin[:],
                                in_=psB[:, 0:512], inc="act")
                    psb_act_tok[0] = t_ev
                    p.wait("tensor", "act", t_ev)
                    hv = vpar[0] % 2
                    vpar[0] += 1
                    p.wait("tensor", "act", psT_free[hv])
                    t_tr = 0
                    for tt4 in range(4):
                        t_tr = p.op("tensor", "transpose",
                                    psT[:, hv * 1024 + tt4 * 128:
                                        hv * 1024 + (tt4 + 1) * 128],
                                    vwin[:, tt4 * 128:(tt4 + 1) * 128],
                                    idb[:], inc="pe")
                    p.wait("scalar", "pe", t_tr)
                    psT_free[hv] = p.op("scalar", "copy",
                                        out=vtm[j][:, nlo:nlo + 512],
                                        in_=psT[:, hv * 1024:hv * 1024 + 512],
                                        inc="act")
                else:
                    parr = slab_i[0] % 2
                    p.wait("scalar", "dve", p.tok("dve"))
                    t_ev = p.op("scalar", "copy", out=raww[parr][:],
                                in_=psB[:, 0:512], inc="act")
                    psb_act_tok[0] = t_ev
                    p.wait("vector", "act", t_ev)
                    X = raww[parr]
                    C = cq if dst_kind == "q" else ck
                    Sn = sq if dst_kind == "q" else sk
                    dst = (qbf[j][:, nlo - t0:nlo - t0 + 512] if dst_kind == "q"
                           else kbf[j][:, nlo:nlo + 512])
                    p.op("vector", "tensor_scalar_mul", rotw[0:64, :],
                         X[64:128, :], -1.0)
                    p.op("vector", "tensor_copy", out=rotw[64:128, :],
                         in_=X[0:64, :])
                    p.op("vector", "tensor_mul", tmpw[:], X[:],
                         C[:, nlo:nlo + 512])
                    p.op("vector", "tensor_mul", rotw[:], rotw[:],
                         Sn[:, nlo:nlo + 512])
                    p.op("vector", "tensor_add", dst, tmpw[:], rotw[:], inc="dve")

            for g in range(n_groups):
                for n in range(nwin):
                    nlo = n * 512
                    parw = 0
                    p.wait("sync", "pe", xw_last[parw])
                    t_xw = p.dma("sync", "qxw0",
                                 xwin[parw].rearrange("pp (kk tt2) -> pp kk tt2",
                                                      kk=NHT),
                                 xbv[:, :, nlo:nlo + 512].rearrange(
                                     "hh pp tt2 -> pp hh tt2"))
                    p.wait("tensor", "qxw0", t_xw)
                    for j in range(GH):
                        do_proj(1, j, "k", parw, nlo, g)
                    for j in range(GH):
                        do_proj(2, j, "v", parw, nlo, g)
                    if nlo >= t0:
                        for j in range(GH):
                            do_proj(0, j, "q", parw, nlo, g)
                    xw_last[parw] = p.tok("pe")
                    if XBAR & 4:
                        p.barrier()

                def emit_sm_qt(j, qc, jj, par):
                    sl = par * 4 + jj
                    qt = qc * 4 + jj
                    qi = c * nqt + qt
                    klen = (qi + 1) * 128
                    kmaxt = c * nqt + qc * 4 + 4
                    nch = ceil_div(klen, 512)
                    p.wait("tensor", "act", psa_exp_tok[0])
                    p.wait("tensor", "dve", psa_mask_tok[0])
                    t_mm = 0
                    for nn in range(nch):
                        w = min(512, klen - nn * 512)
                        t_mm = p.op("tensor", "matmul",
                                    psA[:, nn * 512:nn * 512 + w],
                                    qbf[j][:, qt * 128:(qt + 1) * 128],
                                    kbf[j][:, nn * 512:nn * 512 + w],
                                    start=True, stop=True, inc="pe")
                    p.wait("vector", "pe", t_mm)
                    p.op("vector", "tensor_add", psA[:, qi * 128:klen],
                         psA[:, qi * 128:klen], dmask[:])
                    p.op("vector", "reduce_max", biasc[:, sl:sl + 1],
                         psA[:, 0:klen], axis=AX.X, negate=True)
                    t_b = p.drain_inc("vector", "dve")
                    psa_mask_tok[0] = t_b
                    p.wait("scalar", "dve", t_b)
                    p.wait("scalar", "pe", ctx_pe_done[par])
                    t_e = p.op("scalar", "activation",
                               out=pbf[:, sl * 2048:sl * 2048 + klen],
                               in_=psA[:, 0:klen], func=AF.Exp,
                               bias=biasc[:, sl:sl + 1], scale=1.0,
                               accum_out=esum[:, sl:sl + 1], inc="act")
                    psa_exp_tok[0] = t_e
                    p.wait("vector", "act", t_e)
                    p.op("vector", "reciprocal", rcol[:, sl:sl + 1],
                         esum[:, sl:sl + 1])
                    p.op("vector", "drain")
                    kmax = kmaxt * 128
                    if klen < kmax:
                        p.op("vector", "memset",
                             pbf[:, sl * 2048 + klen:sl * 2048 + kmax], 0.0)
                    p.op("vector", "tensor_scalar_mul",
                         pbf[:, sl * 2048:sl * 2048 + klen],
                         pbf[:, sl * 2048:sl * 2048 + klen],
                         rcol[:, sl:sl + 1], inc="dve")

                def emit_ctx_portion(st, kj_from, kj_to):
                    j, qc, par, kmaxt, dve_tok = st
                    p.wait("tensor", "dve", dve_tok)
                    t_cx = 0
                    for kj in range(kj_from, kj_to):
                        half = kj % 2
                        p.wait("tensor", "act", psT_free[half])
                        t_tr = 0
                        for jj in range(4):
                            sl = par * 4 + jj
                            t_tr = p.op("tensor", "transpose",
                                        psT[:, half * 1024 + jj * 128:
                                            half * 1024 + (jj + 1) * 128],
                                        pbf[:, sl * 2048 + kj * 128:
                                            sl * 2048 + (kj + 1) * 128],
                                        idb[:], inc="pe")
                        p.wait("scalar", "pe", t_tr)
                        t_ev = p.op("scalar", "copy", out=ptsb[half][:],
                                    in_=psT[:, half * 1024:half * 1024 + 512],
                                    inc="act")
                        psT_free[half] = t_ev
                        p.wait("tensor", "act", t_ev)
                        t_cx = p.op("tensor", "matmul", psC[:],
                                    vtm[j][:, kj * 128:(kj + 1) * 128],
                                    ptsb[half][:],
                                    start=(kj == 0), stop=(kj == kmaxt - 1),
                                    inc="pe")
                    if kj_to == kmaxt:
                        ctx_pe_done[par] = p.tok("pe")
                        p.wait("vector", "pe", t_cx)
                        t_cv = p.op("vector", "tensor_copy",
                                    out=ctxb[j][:, qc * 512:(qc + 1) * 512],
                                    in_=psC[:], inc="dve")
                        p.wait("tensor", "dve", t_cv)

                prev = None
                for j in range(GH):
                    for qc in range(TW // 512):
                        kmaxt = c * nqt + qc * 4 + 4
                        par = unit_ctr[0] % 2
                        unit_ctr[0] += 1
                        splits = None
                        if prev is not None:
                            km = prev[3]
                            splits = [0, km // 4, km // 2, (3 * km) // 4, km]
                        for jj in range(4):
                            emit_sm_qt(j, qc, jj, par)
                            if splits is not None:
                                emit_ctx_portion(prev, splits[jj], splits[jj + 1])
                        prev = (j, qc, par, kmaxt, p.tok("dve"))
                if prev is not None:
                    emit_ctx_portion(prev, 0, prev[3])
                    prev = None

                for m in range(NHT):
                    p.wait("gpsimd", "pe", p.tok("pe"))
                    p.dma("gpsimd", "qws0",
                          wslab[0][:, 0:GH * 128].rearrange(
                              "pp (jh cc) -> pp jh cc", jh=GH),
                          wt_d[3, m, g * GH:(g + 1) * GH].rearrange(
                              "jh pp cc -> pp jh cc"))
                    p.wait("tensor", "qws0", p.tok("qws0"))
                    for n in range(TW // 512):
                        p.wait("tensor", "act", psb_act_tok[0])
                        p.wait("tensor", "dve", psb_dve_tok[0])
                        t_mm = 0
                        for j in range(GH):
                            t_mm = p.op("tensor", "matmul",
                                        psB[:, 0:512],
                                        wslab[0][:, j * 128:(j + 1) * 128],
                                        ctxb[j][:, n * 512:(n + 1) * 512],
                                        start=(j == 0), stop=(j == GH - 1),
                                        inc="pe")
                        p.wait("vector", "pe", t_mm)
                        psb_dve_tok[0] = p.op("vector", "tensor_add",
                             acc_slice(m, n * 512, 512),
                             acc_slice(m, n * 512, 512),
                             psB[:, 0:512], inc="dve")
                slab_last[0] = slab_last[1] = p.tok("pe")
            p.barrier()
        dump_acc(f"dbg_res1_{c}")

        layernorm()
        dump_acc(f"dbg_h1_{c}")

        with ExitStack() as fs_:
            fsb = lambda n, s, d: fs_.enter_context(nc.sbuf_tensor(n, s, d))
            h1bf = fsb(f"f{c}_h1", [128, NHT * TW], BF16)
            w_em = fsb(f"f{c}_wem", [NE, TW], F32)
            zrow = fsb(f"f{c}_z", [128, NE], F32)
            m8 = fsb(f"f{c}_m8", [128, 8], F32)
            mrow = fsb(f"f{c}_msk", [128, NE], F32)
            srow = fsb(f"f{c}_srow", [128, 4], F32)
            wrow = fsb(f"f{c}_wrow", [1, TW], F32)
            wbc = [fsb(f"f{c}_wbc{i}", [128, TW], F32) for i in range(2)]
            gslab = [fsb(f"f{c}_gs{i}", [128, NHT * 128], BF16) for i in range(2)]
            uslab = [fsb(f"f{c}_us{i}", [128, NHT * 128], BF16) for i in range(2)]
            dslab = [[fsb(f"f{c}_ds{e4}_{i}", [128, NIT * 128], BF16)
                      for i in range(2)] for e4 in range(EG)]
            t1 = [fsb(f"f{c}_t1{i}", [128, TW], F32) for i in range(2)]
            gu = [fsb(f"f{c}_gu{e4}", [128, NIT * TW], BF16) for e4 in range(EG)]

            for hh in range(NHT):
                p.op("vector", "tensor_copy",
                     out=h1bf[:, hh * TW:(hh + 1) * TW],
                     in_=acc_slice(hh, 0, TW), inc="dve")

            for tt in range(TW // 128):
                reg = (tt % 2) * NE
                p.wait("tensor", "act", p.tok("act"))
                t_lg = 0
                for kk in range(NHT):
                    t_lg = p.op("tensor", "matmul", psC[:, reg:reg + NE],
                                acc_slice(kk, tt * 128, 128),
                                gwT[:, kk * NE:(kk + 1) * NE],
                                start=(kk == 0), stop=(kk == NHT - 1), inc="pe")
                p.wait("vector", "pe", t_lg)
                p.op("vector", "reduce_max", srow[:, 0:1], psC[:, reg:reg + NE],
                     axis=AX.X, negate=True)
                t_b = p.drain_inc("vector", "dve")
                p.wait("scalar", "dve", t_b)
                t_z = p.op("scalar", "activation", out=zrow[:],
                           in_=psC[:, reg:reg + NE], func=AF.Exp,
                           bias=srow[:, 0:1], scale=1.0, inc="act")
                p.wait("vector", "act", t_z)
                p.op("vector", "max", out=m8[:], in_=zrow[:])
                p.op("vector", "drain")
                p.op("vector", "tensor_single_scalar", mrow[:], zrow[:],
                     m8[:, 7:8], OP.is_ge)
                p.op("vector", "drain")
                p.op("vector", "tensor_mul", mrow[:], mrow[:], zrow[:])
                p.op("vector", "drain")
                p.op("vector", "reduce_sum", srow[:, 1:2], mrow[:], axis=AX.X)
                p.op("vector", "drain")
                p.op("vector", "reciprocal", srow[:, 2:3], srow[:, 1:2])
                p.op("vector", "drain")
                t_w = p.op("vector", "tensor_scalar_mul", mrow[:], mrow[:],
                           srow[:, 2:3], inc="dve")
                p.wait("tensor", "dve", t_w)
                t_tr = p.op("tensor", "matmul", psB[0:NE, 0:128], mrow[:],
                            idf[:], start=True, stop=True, inc="pe")
                p.wait("vector", "pe", t_tr)
                p.op("vector", "tensor_copy",
                     out=w_em[:, tt * 128:(tt + 1) * 128],
                     in_=psB[0:NE, 0:128], inc="dve")
            p.barrier(("tensor", "vector", "scalar", "gpsimd"))

            wgv = wg_d.rearrange("e (kk pp) i -> e kk pp i", pp=128)
            wuv = wu_d.rearrange("e (kk pp) i -> e kk pp i", pp=128)
            wdv = wd_d.rearrange("e (kk pp) h -> e kk pp h", pp=128)
            gs_last = [0, 0]
            ds_last = {}
            gu_last = [0] * EG
            silu_tok = [0]
            gum_tok = [0]
            dacc_tok = [0, 0]
            for egi in range(n_experts // EG):
                for e4 in range(EG):
                    e = egi * EG + e4
                    parw = e % 2
                    p.wait("gpsimd", "pe", p.tok("pe"))
                    t_wr = p.dma("gpsimd", "qwr", wrow[:], w_em[e:e + 1, :])
                    p.wait("tensor", "qwr", t_wr)
                    for n in range(TW // 512):
                        t_bc = p.op("tensor", "matmul", psC[:], ones[0:1, :],
                                    wrow[0:1, n * 512:(n + 1) * 512],
                                    start=True, stop=True, inc="pe")
                        p.wait("vector", "pe", t_bc)
                        t_bcv = p.op("vector", "tensor_copy",
                             out=wbc[parw][:, n * 512:(n + 1) * 512],
                             in_=psC[:], inc="dve")
                        p.wait("tensor", "dve", t_bcv)
                    for m in range(NIT):
                        pars = m % 2
                        p.wait("gpsimd", "pe", gs_last[pars])
                        t_g = p.dma("gpsimd", f"qgs{pars}",
                                    gslab[pars].rearrange("pp (kk ii) -> pp kk ii",
                                                          kk=NHT),
                                    wgv[e, :, :, m * 128:(m + 1) * 128].rearrange(
                                        "kk pp ii -> pp kk ii"))
                        t_u = p.dma("gpsimd", f"qus{pars}",
                                    uslab[pars].rearrange("pp (kk ii) -> pp kk ii",
                                                          kk=NHT),
                                    wuv[e, :, :, m * 128:(m + 1) * 128].rearrange(
                                        "kk pp ii -> pp kk ii"))
                        p.wait("tensor", f"qgs{pars}", t_g)
                        p.wait("tensor", "act", silu_tok[0])
                        t_gm = 0
                        for kk in range(NHT):
                            for n in range(TW // 512):
                                t_gm = p.op("tensor", "matmul",
                                            psA[:, n * 512:(n + 1) * 512],
                                            gslab[pars][:, kk * 128:(kk + 1) * 128],
                                            h1bf[:, kk * TW + n * 512:
                                                 kk * TW + (n + 1) * 512],
                                            start=(kk == 0), stop=(kk == NHT - 1),
                                            inc="pe")
                        p.wait("scalar", "pe", t_gm)
                        part = m % 2
                        p.wait("scalar", "dve", p.tok("dve"))
                        t_si = p.op("scalar", "activation", out=t1[part][:],
                                    in_=psA[:, 0:TW], func=AF.Silu, inc="act")
                        silu_tok[0] = t_si
                        p.wait("tensor", f"qus{pars}", t_u)
                        p.wait("tensor", "dve", gum_tok[0])
                        t_um = 0
                        for kk in range(NHT):
                            for n in range(TW // 512):
                                t_um = p.op("tensor", "matmul",
                                            psA[:, TW + n * 512:TW + (n + 1) * 512],
                                            uslab[pars][:, kk * 128:(kk + 1) * 128],
                                            h1bf[:, kk * TW + n * 512:
                                                 kk * TW + (n + 1) * 512],
                                            start=(kk == 0), stop=(kk == NHT - 1),
                                            inc="pe")
                        gs_last[pars] = p.tok("pe")
                        p.wait("vector", "act", t_si)
                        p.wait("vector", "pe", t_um)
                        p.wait("vector", "pe", gu_last[e4])
                        p.op("vector", "tensor_mul", t1[part][:], t1[part][:],
                             wbc[parw][:])
                        gum_tok[0] = p.op("vector", "tensor_mul",
                             gu[e4][:, m * TW:(m + 1) * TW],
                             t1[part][:], psA[:, TW:2 * TW], inc="dve")
                    if XBAR & 1:
                        p.barrier()
                for m2 in range(NHT):
                    par2 = m2 % 2
                    for e4 in range(EG):
                        e = egi * EG + e4
                        p.wait("gpsimd", "pe", ds_last.get((e4, par2), 0))
                        t_d = p.dma("gpsimd", f"qds{e4}_{par2}",
                                    dslab[e4][par2].rearrange(
                                        "pp (kk hh2) -> pp kk hh2", kk=NIT),
                                    wdv[e, :, :, m2 * 128:(m2 + 1) * 128].rearrange(
                                        "kk pp hh2 -> pp kk hh2"))
                        p.wait("tensor", f"qds{e4}_{par2}", t_d)
                    p.wait("tensor", "dve", gum_tok[0])
                    for n in range(TW // 512):
                        p.wait("tensor", "dve", dacc_tok[0])
                        t_dm = 0
                        for e4 in range(EG):
                            for kk in range(NIT):
                                t_dm = p.op("tensor", "matmul",
                                            psB[:, 0:512],
                                            dslab[e4][par2][:,
                                                            kk * 128:(kk + 1) * 128],
                                            gu[e4][:, kk * TW + n * 512:
                                                   kk * TW + (n + 1) * 512],
                                            start=(e4 == 0 and kk == 0),
                                            stop=(e4 == EG - 1 and kk == NIT - 1),
                                            inc="pe")
                        p.wait("vector", "pe", t_dm)
                        dacc_tok[0] = p.op("vector", "tensor_add",
                             acc_slice(m2, n * 512, 512),
                             acc_slice(m2, n * 512, 512),
                             psB[:, 0:512], inc="dve")
                    for e4 in range(EG):
                        ds_last[(e4, par2)] = p.tok("pe")
                    if XBAR & 2:
                        p.barrier()
                gu_last = [p.tok("pe")] * EG
            p.barrier()

        layernorm()
        dump_acc(f"dbg_h2_{c}")

        with ExitStack() as os_:
            ost = [os_.enter_context(
                nc.sbuf_tensor(f"o{c}_st{i}", [128, H], F32)) for i in range(2)]
            st_last = [0, 0]
            for tt in range(TW // 128):
                par = tt % 2
                p.wait("vector", p.qctr(f"qos{par}"), st_last[par])
                t_ev = 0
                for hh in range(NHT):
                    t_tr = p.op("tensor", "matmul",
                                psB[:, 0:128],
                                acc_slice(hh, tt * 128, 128), idf[:],
                                start=True, stop=True, inc="pe")
                    p.wait("vector", "pe", t_tr)
                    t_ev = p.op("vector", "tensor_copy",
                                out=ost[par][:, hh * 128:(hh + 1) * 128],
                                in_=psB[:, 0:128],
                                inc="dve")
                    p.wait("tensor", "dve", t_ev)
                p.wait("sync", "dve", t_ev)
                st_last[par] = p.dma("sync", f"qos{par}", ov[t0 // 128 + tt],
                                     ost[par][:])
            p.barrier()

    for qn in list(p.ctr):
        if qn.startswith("qos") or qn == "qdp":
            p.wait("sync", qn, p.tok(qn))

    with nc.Block() as block:
        for ename in Plan.ENGINES:
            getattr(block, ename)(lambda eng, q=p.q[ename]: [f(eng) for f in q])
    ctx.close()
    return nc


_ROPE_BASE = 10000.0


def _host_constants():
    half = HD // 2
    inv = (1.0 / _ROPE_BASE ** (np.arange(0, HD, 2, dtype=np.float32) / HD))
    t = np.arange(S, dtype=np.float32)
    freqs = np.einsum("i,j->ij", t, inv.astype(np.float32)).astype(np.float32)
    cos = np.cos(freqs).astype(np.float32)
    sin = np.sin(freqs).astype(np.float32)
    C = np.empty((HD, S), np.float32)
    Sn = np.empty((HD, S), np.float32)
    C[:half] = cos.T
    C[half:] = cos.T
    Sn[:half] = sin.T
    Sn[half:] = sin.T
    return {
        "ropec": C.astype(ml_dtypes.bfloat16),
        "ropes": Sn.astype(ml_dtypes.bfloat16),
        "dmask": np.triu(np.full((128, 128), NEG, np.float32), k=1),
        "idf": np.eye(128, dtype=np.float32),
        "idb": np.eye(128).astype(ml_dtypes.bfloat16),
        "ones": np.ones((128, 128), np.float32),
    }


def kernel(L0_attn_hidden_states, L0_attn_Wq, L0_attn_Wk, L0_attn_Wv, L0_attn_Wo,
           L0_ffn_gate_w, L0_ffn_w_gate, L0_ffn_w_up, L0_ffn_w_down):
    x = np.ascontiguousarray(np.asarray(L0_attn_hidden_states, dtype=np.float32))
    shared = {
        "wq": np.ascontiguousarray(np.asarray(L0_attn_Wq, np.float32)),
        "wk": np.ascontiguousarray(np.asarray(L0_attn_Wk, np.float32)),
        "wv": np.ascontiguousarray(np.asarray(L0_attn_Wv, np.float32)),
        "wo": np.ascontiguousarray(np.asarray(L0_attn_Wo, np.float32)),
        "gw": np.ascontiguousarray(np.asarray(L0_ffn_gate_w, np.float32)),
        "wgate": np.ascontiguousarray(np.asarray(L0_ffn_w_gate, np.float32)),
        "wup": np.ascontiguousarray(np.asarray(L0_ffn_w_up, np.float32)),
        "wdown": np.ascontiguousarray(np.asarray(L0_ffn_w_down, np.float32)),
        **_host_constants(),
    }
    nc = build()
    in_maps = [dict(shared, x=np.ascontiguousarray(x[b])) for b in range(8)]
    res = run_bass_kernel_spmd(nc, in_maps, core_ids=list(range(8)))
    return np.stack([r["out"] for r in res.results], axis=0)


# version 05
# speedup vs baseline: 1.0873x; 1.0605x over previous
import math
import numpy as np
import ml_dtypes
from contextlib import ExitStack

import concourse.bass as bass
import concourse.mybir as mybir
from concourse.bass_utils import run_bass_kernel_spmd

F32 = mybir.dt.float32
BF16 = mybir.dt.bfloat16
AF = mybir.ActivationFunctionType
AX = mybir.AxisListType
OP = mybir.AluOpType

S = 2048
H = 2048
NH = 16
HD = 128
NE = 32
DI = 768
NCHUNK = 2
TW = S // NCHUNK
EPS = 1e-5
ISQ = 1.0 / math.sqrt(HD)
NHT = H // 128
NIT = DI // 128
NG = 4
GH = NH // NG
EG = 4
NEG = -1e9
import os as _os
XBAR = int(_os.environ.get('KBAR', '0'))


class Plan:

    ENGINES = ("tensor", "vector", "scalar", "gpsimd", "sync")

    def __init__(self, nc, ctx):
        self.nc = nc
        self.ctx = ctx
        self.q = {e: [] for e in self.ENGINES}
        self.ctr = {}
        self.sem = {}
        self.step = {}
        self.lastw = {}
        for name in ("pe", "dve", "act", "pool"):
            self._mk(name, 1)

    def _mk(self, name, step):
        self.sem[name] = self.ctx.enter_context(self.nc.semaphore(name))
        self.ctr[name] = 0
        self.step[name] = step

    def qctr(self, name):
        if name not in self.ctr:
            self._mk(name, 16)
        return name

    def wait(self, engine, name, val=None):
        val = self.ctr[name] if val is None else val
        if val <= 0 or self.lastw.get((engine, name), -1) >= val:
            return
        self.lastw[(engine, name)] = val
        sem = self.sem[name]
        self.q[engine].append(lambda eng, s=sem, v=val: eng.wait_ge(s, v))

    def op(self, engine, method, *args, inc=None, **kwargs):
        if inc is None:
            self.q[engine].append(
                lambda eng, m=method, a=args, kw=kwargs: getattr(eng, m)(*a, **kw))
            return None
        self.ctr[inc] += self.step[inc]
        sem, k = self.sem[inc], self.step[inc]
        self.q[engine].append(
            lambda eng, m=method, a=args, kw=kwargs, sem=sem, k=k:
            getattr(eng, m)(*a, **kw).then_inc(sem, k))
        return self.ctr[inc]

    def drain_inc(self, engine, inc):
        return self.op(engine, "drain", inc=inc)

    def dma(self, engine, qname, outap, inap):
        qname = self.qctr(qname)
        self.wait(engine, qname)
        return self.op(engine, "dma_start", inc=qname, out=outap, in_=inap)

    def tok(self, name):
        return self.ctr[name]

    def barrier(self, engines=None):
        engines = engines or self.ENGINES
        for e in engines:
            for c_ in list(self.ctr):
                self.wait(e, c_)


def ceil_div(a, b):
    return (a + b - 1) // b


_uid = [0]


def _u():
    _uid[0] += 1
    return _uid[0]


def build(n_chunks=NCHUNK, n_groups=NG, n_experts=NE, debug_dumps=()):
    nc = bass.Bass("TRN2", target_bir_lowering=False, debug=False)

    x_d = nc.dram_tensor("x", [S, H], F32, kind="ExternalInput").ap()
    wq_d = nc.dram_tensor("wq", [H, H], F32, kind="ExternalInput").ap()
    wk_d = nc.dram_tensor("wk", [H, H], F32, kind="ExternalInput").ap()
    wv_d = nc.dram_tensor("wv", [H, H], F32, kind="ExternalInput").ap()
    wo_d = nc.dram_tensor("wo", [H, H], F32, kind="ExternalInput").ap()
    gw_d = nc.dram_tensor("gw", [NE, H], F32, kind="ExternalInput").ap()
    wg_d = nc.dram_tensor("wgate", [NE, H, DI], F32, kind="ExternalInput").ap()
    wu_d = nc.dram_tensor("wup", [NE, H, DI], F32, kind="ExternalInput").ap()
    wd_d = nc.dram_tensor("wdown", [NE, DI, H], F32, kind="ExternalInput").ap()
    cq_d = nc.dram_tensor("ropec", [HD, S], BF16, kind="ExternalInput").ap()
    sq_d = nc.dram_tensor("ropes", [HD, S], BF16, kind="ExternalInput").ap()
    dmask_d = nc.dram_tensor("dmask", [128, 128], F32, kind="ExternalInput").ap()
    dmaskb_d = nc.dram_tensor("dmaskb", [128, 128], BF16, kind="ExternalInput").ap()
    idf_d = nc.dram_tensor("idf", [128, 128], F32, kind="ExternalInput").ap()
    idb_d = nc.dram_tensor("idb", [128, 128], BF16, kind="ExternalInput").ap()
    ones_d = nc.dram_tensor("ones", [128, 128], F32, kind="ExternalInput").ap()
    out_d = nc.dram_tensor("out", [S, H], F32, kind="ExternalOutput").ap()
    dbg = {}
    for name in debug_dumps:
        if name:
            dbg[name] = nc.dram_tensor(name, [128, NHT * TW], F32,
                                       kind="ExternalOutput").ap()

    xbf_d = nc.dram_tensor("xbf_i", [H, S], BF16, kind="Internal").ap()
    wt_d = nc.dram_tensor("wt_i", [4, NHT, NHT, 128, 128], BF16, kind="Internal").ap()

    ctx = ExitStack()
    sb = lambda n, s, d: ctx.enter_context(nc.sbuf_tensor(n, s, d))
    p = Plan(nc, ctx)

    acc = sb("acc", [128, NHT * TW], F32)
    idf = sb("idf_t", [128, 128], F32)
    idb = sb("idb_t", [128, 128], BF16)
    ones = sb("ones_t", [128, 128], F32)
    dmask = sb("dmask_t", [128, 128], F32)
    dmaskb = sb("dmaskb_t", [128, 128], BF16)
    gwT = sb("gwT", [128, NHT * NE], F32)

    psA = ctx.enter_context(nc.psum_tensor("psA", [128, 2048], F32))
    psB = ctx.enter_context(nc.psum_tensor("psB", [128, 512], F32))
    psT = ctx.enter_context(nc.psum_tensor("psT", [128, 2048], BF16))
    psC = ctx.enter_context(nc.psum_tensor("psC", [128, 512], F32))

    xv = x_d.rearrange("(tt pp) h -> tt pp h", pp=128)
    ov = out_d.rearrange("(tt pp) h -> tt pp h", pp=128)
    xbv = xbf_d.rearrange("(hh pp) t -> hh pp t", pp=128)

    p.dma("sync", "qc", idf[:], idf_d)
    p.dma("sync", "qc", idb[:], idb_d)
    p.dma("sync", "qc", ones[:], ones_d)
    p.dma("sync", "qc", dmask[:], dmask_d)
    t_cst = p.dma("sync", "qc", dmaskb[:], dmaskb_d)

    def acc_slice(hh, lo, w):
        return acc[:, hh * TW + lo: hh * TW + lo + w]

    with ExitStack() as ps_:
        psb_ = lambda n, s, d: ps_.enter_context(nc.sbuf_tensor(n, s, d))
        xt = [psb_(f"pr_xt{i}", [128, H], F32) for i in range(2)]
        xb = [psb_(f"pr_xb{i}", [128, H], BF16) for i in range(2)]
        stg = [psb_(f"pr_st{i}", [128, NHT * 128], BF16) for i in range(2)]
        gwsb = psb_("pr_gw", [NE, H], F32)
        p.wait("tensor", "qc", t_cst)

        t_gw = p.dma("sync", "qc", gwsb[:], gw_d)
        p.wait("tensor", "qc", t_gw)
        for kk in range(NHT):
            t_tr = p.op("tensor", "matmul",
                        psB[:, 0:128],
                        gwsb[:, kk * 128:(kk + 1) * 128], idf[0:NE, :],
                        start=True, stop=True, inc="pe")
            p.wait("vector", "pe", t_tr)
            t_ev = p.op("vector", "tensor_copy",
                        out=gwT[:, kk * NE:(kk + 1) * NE],
                        in_=psB[:, 0:NE], inc="dve")
            p.wait("tensor", "dve", t_ev)

        def prep_one(idx, src_rows_f32, dst_view):
            par = idx % 2
            p.wait("sync", "dve", p.tok("dve"))
            p.wait("scalar", p.qctr(f"qst{par}"), p.tok(p.qctr(f"qst{par}")))
            t_load = p.dma("sync", f"qxt{par}", xt[par][:], src_rows_f32)
            p.wait("vector", f"qxt{par}", t_load)
            p.wait("vector", "pe", p.tok("pe"))
            t_cast = p.op("vector", "tensor_copy", out=xb[par][:], in_=xt[par][:],
                          inc="dve")
            p.wait("tensor", "dve", t_cast)
            t_ev = [0, 0]
            for b in range(4):
                half = b % 2
                p.wait("tensor", "act", t_ev[half])
                t_tr = 0
                for i4 in range(4):
                    hh = b * 4 + i4
                    t_tr = p.op("tensor", "transpose",
                                psT[:, half * 1024 + i4 * 128:
                                    half * 1024 + (i4 + 1) * 128],
                                xb[par][:, hh * 128:(hh + 1) * 128], idb[:],
                                inc="pe")
                p.wait("scalar", "pe", t_tr)
                t_ev[half] = p.op("scalar", "copy",
                                  out=stg[par][:, b * 512:(b + 1) * 512],
                                  in_=psT[:, half * 1024:half * 1024 + 512],
                                  inc="act")
            p.wait("gpsimd", "act", max(t_ev))
            p.dma("gpsimd", f"qst{par}", dst_view.rearrange("hh pp cc -> pp hh cc"),
                  stg[par].rearrange("pp (hh cc) -> pp hh cc", hh=NHT))

        for tt in range(NHT):
            prep_one(tt, xv[tt], xbv[:, :, tt * 128:(tt + 1) * 128])
        for wi, wsrc in enumerate([wq_d, wk_d, wv_d, wo_d]):
            wvv = wsrc.rearrange("(ot pp) h -> ot pp h", pp=128)
            for ot in range(NHT):
                prep_one(wi * NHT + ot, wvv[ot], wt_d[wi, ot])
        p.barrier()

    def dump_acc(tag):
        if tag not in dbg:
            return
        with ExitStack() as ds:
            stg2 = ds.enter_context(
                nc.sbuf_tensor(f"dump{_u()}", [128, NHT * TW], F32))
            p.barrier(("vector",))
            t = p.op("vector", "tensor_copy", out=stg2[:], in_=acc[:], inc="dve")
            p.wait("sync", "dve", t)
            t2 = p.dma("sync", "qdp", dbg[tag], stg2[:])
            p.wait("sync", "qdp", t2)
            p.barrier()

    def layernorm():
        u = _u()
        with ExitStack() as ls:
            lsb = lambda n, s, d: ls.enter_context(nc.sbuf_tensor(n, s, d))
            sqs = [lsb(f"ln{u}_sq{i}", [128, TW], F32) for i in range(2)]
            rows = lsb(f"ln{u}_rows", [1, 3 * TW], F32)
            mb = lsb(f"ln{u}_mb", [128, TW], F32)
            rb = lsb(f"ln{u}_rb", [128, TW], F32)
            p.barrier()
            tmm = 0
            for hh in range(NHT):
                par = hh % 2
                p.wait("scalar", "pe", p.tok("pe"))
                t_sq = p.op("scalar", "activation", out=sqs[par][:],
                            in_=acc_slice(hh, 0, TW), func=AF.Square, inc="act")
                for n in range(TW // 512):
                    p.op("tensor", "matmul",
                         psA[0:1, n * 512:(n + 1) * 512],
                         ones[:, 0:1], acc_slice(hh, n * 512, 512),
                         start=(hh == 0), stop=(hh == NHT - 1), inc="pe")
                p.wait("tensor", "act", t_sq)
                for n in range(TW // 512):
                    tmm = p.op("tensor", "matmul",
                               psA[0:1, TW + n * 512: TW + (n + 1) * 512],
                               ones[:, 0:1], sqs[par][:, n * 512:(n + 1) * 512],
                               start=(hh == 0), stop=(hh == NHT - 1), inc="pe")
            p.wait("vector", "pe", tmm)
            p.op("vector", "tensor_scalar_mul", rows[0:1, 0:TW],
                 psA[0:1, 0:TW], 1.0 / H)
            p.op("vector", "tensor_scalar_mul", rows[0:1, TW:2 * TW],
                 psA[0:1, TW:2 * TW], 1.0 / H)
            p.op("vector", "drain")
            p.op("vector", "tensor_mul", rows[0:1, 2 * TW:3 * TW],
                 rows[0:1, 0:TW], rows[0:1, 0:TW])
            p.op("vector", "drain")
            p.op("vector", "tensor_sub", rows[0:1, TW:2 * TW],
                 rows[0:1, TW:2 * TW], rows[0:1, 2 * TW:3 * TW])
            p.op("vector", "drain")
            p.op("vector", "tensor_scalar_add", rows[0:1, TW:2 * TW],
                 rows[0:1, TW:2 * TW], EPS)
            t = p.drain_inc("vector", "dve")
            p.wait("scalar", "dve", t)
            t = p.op("scalar", "activation", out=rows[0:1, TW:2 * TW],
                     in_=rows[0:1, TW:2 * TW], func=AF.Sqrt, inc="act")
            p.wait("vector", "act", t)
            p.op("vector", "reciprocal", rows[0:1, TW:2 * TW],
                 rows[0:1, TW:2 * TW])
            t = p.drain_inc("vector", "dve")
            p.wait("tensor", "dve", t)
            for n in range(TW // 512):
                t1_ = p.op("tensor", "matmul", psB[:, 0:512], ones[0:1, :],
                           rows[0:1, n * 512:(n + 1) * 512],
                           start=True, stop=True, inc="pe")
                p.wait("vector", "pe", t1_)
                t3_ = p.op("vector", "tensor_copy",
                           out=mb[:, n * 512:(n + 1) * 512],
                           in_=psB[:, 0:512], inc="dve")
                p.wait("tensor", "dve", t3_)
                t2_ = p.op("tensor", "matmul", psB[:, 0:512], ones[0:1, :],
                           rows[0:1, TW + n * 512: TW + (n + 1) * 512],
                           start=True, stop=True, inc="pe")
                p.wait("vector", "pe", t2_)
                t3_ = p.op("vector", "tensor_copy",
                           out=rb[:, n * 512:(n + 1) * 512],
                           in_=psB[:, 0:512], inc="dve")
                p.wait("tensor", "dve", t3_)
            for hh in range(NHT):
                p.op("vector", "tensor_sub", acc_slice(hh, 0, TW),
                     acc_slice(hh, 0, TW), mb[:])
                p.op("vector", "tensor_mul", acc_slice(hh, 0, TW),
                     acc_slice(hh, 0, TW), rb[:], inc="dve")
            p.barrier()

    for c in range(n_chunks):
        t0 = c * TW
        kvw = (c + 1) * TW
        nwin = kvw // 512
        nqt = TW // 128

        with ExitStack() as xs_:
            xtm = [xs_.enter_context(
                nc.sbuf_tensor(f"a{c}_xtm{i}", [128, H], F32)) for i in range(2)]
            p.barrier()
            for tt in range(TW // 128):
                par = tt % 2
                p.wait("sync", "pe", p.tok("pe"))
                tl = p.dma("sync", f"qxm{par}", xtm[par][:], xv[t0 // 128 + tt])
                p.wait("tensor", f"qxm{par}", tl)
                for b in range(4):
                    ttr = 0
                    for i4 in range(4):
                        hh = b * 4 + i4
                        ttr = p.op("tensor", "matmul",
                                   psB[:, i4 * 128:(i4 + 1) * 128],
                                   xtm[par][:, hh * 128:(hh + 1) * 128],
                                   idf[:], start=True, stop=True, inc="pe")
                    p.wait("vector", "pe", ttr)
                    tev = 0
                    for i4 in range(4):
                        hh = b * 4 + i4
                        tev = p.op("vector", "tensor_copy",
                                   out=acc_slice(hh, tt * 128, 128),
                                   in_=psB[:, i4 * 128:(i4 + 1) * 128], inc="dve")
                    p.wait("tensor", "dve", tev)
            p.barrier()

        with ExitStack() as as_:
            asb = lambda n, s, d: as_.enter_context(nc.sbuf_tensor(n, s, d))
            cq = asb(f"a{c}_cq", [128, S], BF16)
            sq = asb(f"a{c}_sq", [128, S], BF16)
            ck = asb(f"a{c}_ck", [128, S], BF16)
            sk = asb(f"a{c}_sk", [128, S], BF16)
            kbf = [asb(f"a{c}_k{j}", [128, kvw], BF16) for j in range(GH)]
            vtm = [asb(f"a{c}_v{j}", [128, kvw], BF16) for j in range(GH)]
            qbf = [asb(f"a{c}_q{j}", [128, TW], BF16) for j in range(GH)]
            ctxb = [asb(f"a{c}_c{j}", [128, TW], BF16) for j in range(GH)]
            pbf = asb(f"a{c}_pbf", [128, 8 * 2048], BF16)
            ptsb = [asb(f"a{c}_pt{i}", [128, 512], BF16) for i in range(2)]
            xwin = [asb(f"a{c}_xw0", [128, NHT * 512], BF16)]
            wslab = [asb(f"a{c}_ws{i}", [128, NHT * 128], BF16) for i in range(2)]
            raww = [asb(f"a{c}_rw{i}", [128, 512], F32) for i in range(2)]
            rotw = asb(f"a{c}_rot", [128, 512], F32)
            tmpw = asb(f"a{c}_tmp", [128, 512], F32)
            vwin = asb(f"a{c}_vw", [128, 512], BF16)
            biasc = asb(f"a{c}_bias", [128, 8], F32)
            esum = asb(f"a{c}_esum", [128, 8], F32)
            rcol = asb(f"a{c}_rcol", [128, 8], F32)
            unit_ctr = [0]
            ctx_pe_done = [0, 0]
            psa_exp_tok = [0]
            psa_mask_tok = [0]
            psT_free = [0, 0]
            psb_act_tok = [0]
            psb_dve_tok = [0]
            vpar = [0]

            p.barrier()
            p.dma("sync", "qtab", cq[:], cq_d)
            t_t = p.dma("sync", "qtab", sq[:], sq_d)
            p.wait("vector", "qtab", t_t)
            p.op("vector", "tensor_scalar_mul", ck[:], cq[:], ISQ)
            t_tab = p.op("vector", "tensor_scalar_mul", sk[:], sq[:], ISQ,
                         inc="dve")
            p.wait("tensor", "dve", t_tab)

            slab_last = [0, 0]
            xw_last = [0, 0]
            slab_i = [0]

            def do_proj(widx, j, dst_kind, parw, nlo, g):
                pars = slab_i[0] % 2
                slab_i[0] += 1
                p.wait("gpsimd", "pe", slab_last[pars])
                p.dma("gpsimd", f"qws{pars}",
                      wslab[pars].rearrange("pp (hh cc) -> pp hh cc", hh=NHT),
                      wt_d[widx, g * GH + j].rearrange("hh pp cc -> pp hh cc"))
                p.wait("tensor", f"qws{pars}", p.tok(f"qws{pars}"))
                p.wait("tensor", "act", psb_act_tok[0])
                p.wait("tensor", "dve", psb_dve_tok[0])
                t_mm = 0
                for kk in range(NHT):
                    t_mm = p.op("tensor", "matmul",
                                psB[:, 0:512],
                                wslab[pars][:, kk * 128:(kk + 1) * 128],
                                xwin[parw][:, kk * 512:(kk + 1) * 512],
                                start=(kk == 0), stop=(kk == NHT - 1), inc="pe")
                slab_last[pars] = p.tok("pe")
                p.wait("scalar", "pe", t_mm)
                if dst_kind == "v":
                    t_ev = p.op("scalar", "copy", out=vwin[:],
                                in_=psB[:, 0:512], inc="act")
                    psb_act_tok[0] = t_ev
                    p.wait("tensor", "act", t_ev)
                    hv = vpar[0] % 2
                    vpar[0] += 1
                    p.wait("tensor", "act", psT_free[hv])
                    t_tr = 0
                    for tt4 in range(4):
                        t_tr = p.op("tensor", "transpose",
                                    psT[:, hv * 1024 + tt4 * 128:
                                        hv * 1024 + (tt4 + 1) * 128],
                                    vwin[:, tt4 * 128:(tt4 + 1) * 128],
                                    idb[:], inc="pe")
                    p.wait("scalar", "pe", t_tr)
                    psT_free[hv] = p.op("scalar", "copy",
                                        out=vtm[j][:, nlo:nlo + 512],
                                        in_=psT[:, hv * 1024:hv * 1024 + 512],
                                        inc="act")
                else:
                    parr = slab_i[0] % 2
                    p.wait("scalar", "dve", p.tok("dve"))
                    t_ev = p.op("scalar", "copy", out=raww[parr][:],
                                in_=psB[:, 0:512], inc="act")
                    psb_act_tok[0] = t_ev
                    p.wait("vector", "act", t_ev)
                    X = raww[parr]
                    C = cq if dst_kind == "q" else ck
                    Sn = sq if dst_kind == "q" else sk
                    dst = (qbf[j][:, nlo - t0:nlo - t0 + 512] if dst_kind == "q"
                           else kbf[j][:, nlo:nlo + 512])
                    p.op("vector", "tensor_scalar_mul", rotw[0:64, :],
                         X[64:128, :], -1.0)
                    p.op("vector", "tensor_copy", out=rotw[64:128, :],
                         in_=X[0:64, :])
                    p.op("vector", "tensor_mul", tmpw[:], X[:],
                         C[:, nlo:nlo + 512])
                    p.op("vector", "tensor_mul", rotw[:], rotw[:],
                         Sn[:, nlo:nlo + 512])
                    p.op("vector", "tensor_add", dst, tmpw[:], rotw[:], inc="dve")

            for g in range(n_groups):
                for n in range(nwin):
                    nlo = n * 512
                    parw = 0
                    p.wait("sync", "pe", xw_last[parw])
                    t_xw = p.dma("sync", "qxw0",
                                 xwin[parw].rearrange("pp (kk tt2) -> pp kk tt2",
                                                      kk=NHT),
                                 xbv[:, :, nlo:nlo + 512].rearrange(
                                     "hh pp tt2 -> pp hh tt2"))
                    p.wait("tensor", "qxw0", t_xw)
                    for j in range(GH):
                        do_proj(1, j, "k", parw, nlo, g)
                    for j in range(GH):
                        do_proj(2, j, "v", parw, nlo, g)
                    if nlo >= t0:
                        for j in range(GH):
                            do_proj(0, j, "q", parw, nlo, g)
                    xw_last[parw] = p.tok("pe")
                    if XBAR & 4:
                        p.barrier()

                def emit_sm_qt(j, qc, jj, par):
                    sl = par * 4 + jj
                    qt = qc * 4 + jj
                    qi = c * nqt + qt
                    klen = (qi + 1) * 128
                    kmaxt = c * nqt + qc * 4 + 4
                    nch = ceil_div(klen, 512)
                    p.wait("tensor", "act", psa_exp_tok[0])
                    p.wait("tensor", "dve", psa_mask_tok[0])
                    t_mm = 0
                    for nn in range(nch):
                        w = min(512, klen - nn * 512)
                        p.op("tensor", "matmul",
                             psA[:, nn * 512:nn * 512 + w],
                             qbf[j][:, qt * 128:(qt + 1) * 128],
                             kbf[j][:, nn * 512:nn * 512 + w],
                             start=True, stop=(nn < nch - 1), inc="pe")
                    t_mm = p.op("tensor", "matmul", psA[:, qi * 128:klen],
                                idb[:], dmaskb[:], start=False, stop=True,
                                inc="pe")
                    p.wait("vector", "pe", t_mm)
                    p.op("vector", "reduce_max", biasc[:, sl:sl + 1],
                         psA[:, 0:klen], axis=AX.X, negate=True)
                    t_b = p.drain_inc("vector", "dve")
                    psa_mask_tok[0] = t_b
                    p.wait("scalar", "dve", t_b)
                    p.wait("scalar", "pe", ctx_pe_done[par])
                    t_e = p.op("scalar", "activation",
                               out=pbf[:, sl * 2048:sl * 2048 + klen],
                               in_=psA[:, 0:klen], func=AF.Exp,
                               bias=biasc[:, sl:sl + 1], scale=1.0,
                               accum_out=esum[:, sl:sl + 1], inc="act")
                    psa_exp_tok[0] = t_e
                    p.wait("vector", "act", t_e)
                    p.op("vector", "reciprocal", rcol[:, sl:sl + 1],
                         esum[:, sl:sl + 1])
                    p.op("vector", "drain")
                    kmax = kmaxt * 128
                    if klen < kmax:
                        p.op("vector", "memset",
                             pbf[:, sl * 2048 + klen:sl * 2048 + kmax], 0.0)
                    p.op("vector", "tensor_scalar_mul",
                         pbf[:, sl * 2048:sl * 2048 + klen],
                         pbf[:, sl * 2048:sl * 2048 + klen],
                         rcol[:, sl:sl + 1], inc="dve")

                def emit_ctx_portion(st, kj_from, kj_to):
                    j, qc, par, kmaxt, dve_tok = st
                    p.wait("tensor", "dve", dve_tok)
                    t_cx = 0
                    for kj in range(kj_from, kj_to):
                        half = kj % 2
                        p.wait("tensor", "act", psT_free[half])
                        t_tr = 0
                        for jj in range(4):
                            sl = par * 4 + jj
                            t_tr = p.op("tensor", "transpose",
                                        psT[:, half * 1024 + jj * 128:
                                            half * 1024 + (jj + 1) * 128],
                                        pbf[:, sl * 2048 + kj * 128:
                                            sl * 2048 + (kj + 1) * 128],
                                        idb[:], inc="pe")
                        p.wait("scalar", "pe", t_tr)
                        t_ev = p.op("scalar", "copy", out=ptsb[half][:],
                                    in_=psT[:, half * 1024:half * 1024 + 512],
                                    inc="act")
                        psT_free[half] = t_ev
                        p.wait("tensor", "act", t_ev)
                        t_cx = p.op("tensor", "matmul", psC[:],
                                    vtm[j][:, kj * 128:(kj + 1) * 128],
                                    ptsb[half][:],
                                    start=(kj == 0), stop=(kj == kmaxt - 1),
                                    inc="pe")
                    if kj_to == kmaxt:
                        ctx_pe_done[par] = p.tok("pe")
                        p.wait("vector", "pe", t_cx)
                        t_cv = p.op("vector", "tensor_copy",
                                    out=ctxb[j][:, qc * 512:(qc + 1) * 512],
                                    in_=psC[:], inc="dve")
                        p.wait("tensor", "dve", t_cv)

                prev = None
                for j in range(GH):
                    for qc in range(TW // 512):
                        kmaxt = c * nqt + qc * 4 + 4
                        par = unit_ctr[0] % 2
                        unit_ctr[0] += 1
                        splits = None
                        if prev is not None:
                            km = prev[3]
                            splits = [0, km // 4, km // 2, (3 * km) // 4, km]
                        for jj in range(4):
                            emit_sm_qt(j, qc, jj, par)
                            if splits is not None:
                                emit_ctx_portion(prev, splits[jj], splits[jj + 1])
                        prev = (j, qc, par, kmaxt, p.tok("dve"))
                if prev is not None:
                    emit_ctx_portion(prev, 0, prev[3])
                    prev = None

                for m in range(NHT):
                    p.wait("gpsimd", "pe", p.tok("pe"))
                    p.dma("gpsimd", "qws0",
                          wslab[0][:, 0:GH * 128].rearrange(
                              "pp (jh cc) -> pp jh cc", jh=GH),
                          wt_d[3, m, g * GH:(g + 1) * GH].rearrange(
                              "jh pp cc -> pp jh cc"))
                    p.wait("tensor", "qws0", p.tok("qws0"))
                    for n in range(TW // 512):
                        p.wait("tensor", "act", psb_act_tok[0])
                        p.wait("tensor", "dve", psb_dve_tok[0])
                        t_mm = 0
                        for j in range(GH):
                            t_mm = p.op("tensor", "matmul",
                                        psB[:, 0:512],
                                        wslab[0][:, j * 128:(j + 1) * 128],
                                        ctxb[j][:, n * 512:(n + 1) * 512],
                                        start=(j == 0), stop=(j == GH - 1),
                                        inc="pe")
                        p.wait("vector", "pe", t_mm)
                        psb_dve_tok[0] = p.op("vector", "tensor_add",
                             acc_slice(m, n * 512, 512),
                             acc_slice(m, n * 512, 512),
                             psB[:, 0:512], inc="dve")
                slab_last[0] = slab_last[1] = p.tok("pe")
            p.barrier()
        dump_acc(f"dbg_res1_{c}")

        layernorm()
        dump_acc(f"dbg_h1_{c}")

        with ExitStack() as fs_:
            fsb = lambda n, s, d: fs_.enter_context(nc.sbuf_tensor(n, s, d))
            h1bf = fsb(f"f{c}_h1", [128, NHT * TW], BF16)
            w_em = fsb(f"f{c}_wem", [NE, TW], F32)
            zrow = fsb(f"f{c}_z", [128, NE], F32)
            m8 = fsb(f"f{c}_m8", [128, 8], F32)
            mrow = fsb(f"f{c}_msk", [128, NE], F32)
            srow = fsb(f"f{c}_srow", [128, 4], F32)
            wrow = fsb(f"f{c}_wrow", [1, TW], F32)
            wbc = [fsb(f"f{c}_wbc{i}", [128, TW], F32) for i in range(2)]
            gslab = [fsb(f"f{c}_gs{i}", [128, NHT * 128], BF16) for i in range(2)]
            uslab = [fsb(f"f{c}_us{i}", [128, NHT * 128], BF16) for i in range(2)]
            dslab = [[fsb(f"f{c}_ds{e4}_{i}", [128, NIT * 128], BF16)
                      for i in range(2)] for e4 in range(EG)]
            t1 = [fsb(f"f{c}_t1{i}", [128, TW], F32) for i in range(2)]
            gu = [fsb(f"f{c}_gu{e4}", [128, NIT * TW], BF16) for e4 in range(EG)]

            for hh in range(NHT):
                p.op("vector", "tensor_copy",
                     out=h1bf[:, hh * TW:(hh + 1) * TW],
                     in_=acc_slice(hh, 0, TW), inc="dve")

            for tt in range(TW // 128):
                reg = (tt % 2) * NE
                p.wait("tensor", "act", p.tok("act"))
                t_lg = 0
                for kk in range(NHT):
                    t_lg = p.op("tensor", "matmul", psC[:, reg:reg + NE],
                                acc_slice(kk, tt * 128, 128),
                                gwT[:, kk * NE:(kk + 1) * NE],
                                start=(kk == 0), stop=(kk == NHT - 1), inc="pe")
                p.wait("vector", "pe", t_lg)
                p.op("vector", "reduce_max", srow[:, 0:1], psC[:, reg:reg + NE],
                     axis=AX.X, negate=True)
                t_b = p.drain_inc("vector", "dve")
                p.wait("scalar", "dve", t_b)
                t_z = p.op("scalar", "activation", out=zrow[:],
                           in_=psC[:, reg:reg + NE], func=AF.Exp,
                           bias=srow[:, 0:1], scale=1.0, inc="act")
                p.wait("vector", "act", t_z)
                p.op("vector", "max", out=m8[:], in_=zrow[:])
                p.op("vector", "drain")
                p.op("vector", "tensor_single_scalar", mrow[:], zrow[:],
                     m8[:, 7:8], OP.is_ge)
                p.op("vector", "drain")
                p.op("vector", "tensor_mul", mrow[:], mrow[:], zrow[:])
                p.op("vector", "drain")
                p.op("vector", "reduce_sum", srow[:, 1:2], mrow[:], axis=AX.X)
                p.op("vector", "drain")
                p.op("vector", "reciprocal", srow[:, 2:3], srow[:, 1:2])
                p.op("vector", "drain")
                t_w = p.op("vector", "tensor_scalar_mul", mrow[:], mrow[:],
                           srow[:, 2:3], inc="dve")
                p.wait("tensor", "dve", t_w)
                t_tr = p.op("tensor", "matmul", psB[0:NE, 0:128], mrow[:],
                            idf[:], start=True, stop=True, inc="pe")
                p.wait("vector", "pe", t_tr)
                p.op("vector", "tensor_copy",
                     out=w_em[:, tt * 128:(tt + 1) * 128],
                     in_=psB[0:NE, 0:128], inc="dve")
            p.barrier(("tensor", "vector", "scalar", "gpsimd"))

            wgv = wg_d.rearrange("e (kk pp) i -> e kk pp i", pp=128)
            wuv = wu_d.rearrange("e (kk pp) i -> e kk pp i", pp=128)
            wdv = wd_d.rearrange("e (kk pp) h -> e kk pp h", pp=128)
            gs_last = [0, 0]
            ds_last = {}
            gu_last = [0] * EG
            silu_tok = [0]
            gum_tok = [0]
            dacc_tok = [0, 0]
            for egi in range(n_experts // EG):
                for e4 in range(EG):
                    e = egi * EG + e4
                    parw = e % 2
                    p.wait("gpsimd", "pe", p.tok("pe"))
                    t_wr = p.dma("gpsimd", "qwr", wrow[:], w_em[e:e + 1, :])
                    p.wait("tensor", "qwr", t_wr)
                    for n in range(TW // 512):
                        t_bc = p.op("tensor", "matmul", psC[:], ones[0:1, :],
                                    wrow[0:1, n * 512:(n + 1) * 512],
                                    start=True, stop=True, inc="pe")
                        p.wait("vector", "pe", t_bc)
                        t_bcv = p.op("vector", "tensor_copy",
                             out=wbc[parw][:, n * 512:(n + 1) * 512],
                             in_=psC[:], inc="dve")
                        p.wait("tensor", "dve", t_bcv)
                    for m in range(NIT):
                        pars = m % 2
                        p.wait("gpsimd", "pe", gs_last[pars])
                        t_g = p.dma("gpsimd", f"qgs{pars}",
                                    gslab[pars].rearrange("pp (kk ii) -> pp kk ii",
                                                          kk=NHT),
                                    wgv[e, :, :, m * 128:(m + 1) * 128].rearrange(
                                        "kk pp ii -> pp kk ii"))
                        t_u = p.dma("gpsimd", f"qus{pars}",
                                    uslab[pars].rearrange("pp (kk ii) -> pp kk ii",
                                                          kk=NHT),
                                    wuv[e, :, :, m * 128:(m + 1) * 128].rearrange(
                                        "kk pp ii -> pp kk ii"))
                        p.wait("tensor", f"qgs{pars}", t_g)
                        p.wait("tensor", "act", silu_tok[0])
                        t_gm = 0
                        for kk in range(NHT):
                            for n in range(TW // 512):
                                t_gm = p.op("tensor", "matmul",
                                            psA[:, n * 512:(n + 1) * 512],
                                            gslab[pars][:, kk * 128:(kk + 1) * 128],
                                            h1bf[:, kk * TW + n * 512:
                                                 kk * TW + (n + 1) * 512],
                                            start=(kk == 0), stop=(kk == NHT - 1),
                                            inc="pe")
                        p.wait("scalar", "pe", t_gm)
                        part = m % 2
                        p.wait("scalar", "dve", p.tok("dve"))
                        t_si = p.op("scalar", "activation", out=t1[part][:],
                                    in_=psA[:, 0:TW], func=AF.Silu, inc="act")
                        silu_tok[0] = t_si
                        p.wait("tensor", f"qus{pars}", t_u)
                        p.wait("tensor", "dve", gum_tok[0])
                        t_um = 0
                        for kk in range(NHT):
                            for n in range(TW // 512):
                                t_um = p.op("tensor", "matmul",
                                            psA[:, TW + n * 512:TW + (n + 1) * 512],
                                            uslab[pars][:, kk * 128:(kk + 1) * 128],
                                            h1bf[:, kk * TW + n * 512:
                                                 kk * TW + (n + 1) * 512],
                                            start=(kk == 0), stop=(kk == NHT - 1),
                                            inc="pe")
                        gs_last[pars] = p.tok("pe")
                        p.wait("vector", "act", t_si)
                        p.wait("vector", "pe", t_um)
                        p.wait("vector", "pe", gu_last[e4])
                        p.op("vector", "tensor_mul", t1[part][:], t1[part][:],
                             wbc[parw][:])
                        gum_tok[0] = p.op("vector", "tensor_mul",
                             gu[e4][:, m * TW:(m + 1) * TW],
                             t1[part][:], psA[:, TW:2 * TW], inc="dve")
                    if XBAR & 1:
                        p.barrier()
                for m2 in range(NHT):
                    par2 = m2 % 2
                    for e4 in range(EG):
                        e = egi * EG + e4
                        p.wait("gpsimd", "pe", ds_last.get((e4, par2), 0))
                        t_d = p.dma("gpsimd", f"qds{e4}_{par2}",
                                    dslab[e4][par2].rearrange(
                                        "pp (kk hh2) -> pp kk hh2", kk=NIT),
                                    wdv[e, :, :, m2 * 128:(m2 + 1) * 128].rearrange(
                                        "kk pp hh2 -> pp kk hh2"))
                        p.wait("tensor", f"qds{e4}_{par2}", t_d)
                    p.wait("tensor", "dve", gum_tok[0])
                    for n in range(TW // 512):
                        p.wait("tensor", "dve", dacc_tok[0])
                        t_dm = 0
                        for e4 in range(EG):
                            for kk in range(NIT):
                                t_dm = p.op("tensor", "matmul",
                                            psB[:, 0:512],
                                            dslab[e4][par2][:,
                                                            kk * 128:(kk + 1) * 128],
                                            gu[e4][:, kk * TW + n * 512:
                                                   kk * TW + (n + 1) * 512],
                                            start=(e4 == 0 and kk == 0),
                                            stop=(e4 == EG - 1 and kk == NIT - 1),
                                            inc="pe")
                        p.wait("vector", "pe", t_dm)
                        dacc_tok[0] = p.op("vector", "tensor_add",
                             acc_slice(m2, n * 512, 512),
                             acc_slice(m2, n * 512, 512),
                             psB[:, 0:512], inc="dve")
                    for e4 in range(EG):
                        ds_last[(e4, par2)] = p.tok("pe")
                    if XBAR & 2:
                        p.barrier()
                gu_last = [p.tok("pe")] * EG
            p.barrier()

        layernorm()
        dump_acc(f"dbg_h2_{c}")

        with ExitStack() as os_:
            ost = [os_.enter_context(
                nc.sbuf_tensor(f"o{c}_st{i}", [128, H], F32)) for i in range(2)]
            st_last = [0, 0]
            for tt in range(TW // 128):
                par = tt % 2
                p.wait("vector", p.qctr(f"qos{par}"), st_last[par])
                t_ev = 0
                for b in range(4):
                    t_tr = 0
                    for i4 in range(4):
                        hh = b * 4 + i4
                        t_tr = p.op("tensor", "matmul",
                                    psB[:, i4 * 128:(i4 + 1) * 128],
                                    acc_slice(hh, tt * 128, 128), idf[:],
                                    start=True, stop=True, inc="pe")
                    p.wait("vector", "pe", t_tr)
                    for i4 in range(4):
                        hh = b * 4 + i4
                        t_ev = p.op("vector", "tensor_copy",
                                    out=ost[par][:, hh * 128:(hh + 1) * 128],
                                    in_=psB[:, i4 * 128:(i4 + 1) * 128],
                                    inc="dve")
                    p.wait("tensor", "dve", t_ev)
                p.wait("sync", "dve", t_ev)
                st_last[par] = p.dma("sync", f"qos{par}", ov[t0 // 128 + tt],
                                     ost[par][:])
            p.barrier()

    for qn in list(p.ctr):
        if qn.startswith("qos") or qn == "qdp":
            p.wait("sync", qn, p.tok(qn))

    with nc.Block() as block:
        for ename in Plan.ENGINES:
            getattr(block, ename)(lambda eng, q=p.q[ename]: [f(eng) for f in q])
    ctx.close()
    return nc


_ROPE_BASE = 10000.0


def _host_constants():
    half = HD // 2
    inv = (1.0 / _ROPE_BASE ** (np.arange(0, HD, 2, dtype=np.float32) / HD))
    t = np.arange(S, dtype=np.float32)
    freqs = np.einsum("i,j->ij", t, inv.astype(np.float32)).astype(np.float32)
    cos = np.cos(freqs).astype(np.float32)
    sin = np.sin(freqs).astype(np.float32)
    C = np.empty((HD, S), np.float32)
    Sn = np.empty((HD, S), np.float32)
    C[:half] = cos.T
    C[half:] = cos.T
    Sn[:half] = sin.T
    Sn[half:] = sin.T
    return {
        "ropec": C.astype(ml_dtypes.bfloat16),
        "ropes": Sn.astype(ml_dtypes.bfloat16),
        "dmask": np.triu(np.full((128, 128), NEG, np.float32), k=1),
        "dmaskb": np.triu(np.full((128, 128), NEG, np.float32), k=1).astype(
            ml_dtypes.bfloat16),
        "idf": np.eye(128, dtype=np.float32),
        "idb": np.eye(128).astype(ml_dtypes.bfloat16),
        "ones": np.ones((128, 128), np.float32),
    }


def kernel(L0_attn_hidden_states, L0_attn_Wq, L0_attn_Wk, L0_attn_Wv, L0_attn_Wo,
           L0_ffn_gate_w, L0_ffn_w_gate, L0_ffn_w_up, L0_ffn_w_down):
    x = np.ascontiguousarray(np.asarray(L0_attn_hidden_states, dtype=np.float32))
    shared = {
        "wq": np.ascontiguousarray(np.asarray(L0_attn_Wq, np.float32)),
        "wk": np.ascontiguousarray(np.asarray(L0_attn_Wk, np.float32)),
        "wv": np.ascontiguousarray(np.asarray(L0_attn_Wv, np.float32)),
        "wo": np.ascontiguousarray(np.asarray(L0_attn_Wo, np.float32)),
        "gw": np.ascontiguousarray(np.asarray(L0_ffn_gate_w, np.float32)),
        "wgate": np.ascontiguousarray(np.asarray(L0_ffn_w_gate, np.float32)),
        "wup": np.ascontiguousarray(np.asarray(L0_ffn_w_up, np.float32)),
        "wdown": np.ascontiguousarray(np.asarray(L0_ffn_w_down, np.float32)),
        **_host_constants(),
    }
    nc = build()
    in_maps = [dict(shared, x=np.ascontiguousarray(x[b])) for b in range(8)]
    res = run_bass_kernel_spmd(nc, in_maps, core_ids=list(range(8)))
    return np.stack([r["out"] for r in res.results], axis=0)


# version 06 (final)
# speedup vs baseline: 1.1749x; 1.0805x over previous
import math
import numpy as np
import ml_dtypes
from contextlib import ExitStack

import concourse.bass as bass
import concourse.mybir as mybir
from concourse.bass_utils import run_bass_kernel_spmd

F32 = mybir.dt.float32
BF16 = mybir.dt.bfloat16
AF = mybir.ActivationFunctionType
AX = mybir.AxisListType
OP = mybir.AluOpType

S = 2048
H = 2048
NH = 16
HD = 128
NE = 32
DI = 768
NCHUNK = 2
TW = S // NCHUNK
EPS = 1e-5
ISQ = 1.0 / math.sqrt(HD)
NHT = H // 128
NIT = DI // 128
NG = 4
GH = NH // NG
EG = 4
NEG = -1e9
import os as _os
XBAR = int(_os.environ.get('KBAR', '0'))


class Plan:

    ENGINES = ("tensor", "vector", "scalar", "gpsimd", "sync")

    def __init__(self, nc, ctx):
        self.nc = nc
        self.ctx = ctx
        self.q = {e: [] for e in self.ENGINES}
        self.ctr = {}
        self.sem = {}
        self.step = {}
        self.lastw = {}
        for name in ("pe", "dve", "act", "pool"):
            self._mk(name, 1)

    def _mk(self, name, step):
        self.sem[name] = self.ctx.enter_context(self.nc.semaphore(name))
        self.ctr[name] = 0
        self.step[name] = step

    def qctr(self, name):
        if name not in self.ctr:
            self._mk(name, 16)
        return name

    def wait(self, engine, name, val=None):
        val = self.ctr[name] if val is None else val
        if val <= 0 or self.lastw.get((engine, name), -1) >= val:
            return
        self.lastw[(engine, name)] = val
        sem = self.sem[name]
        self.q[engine].append(lambda eng, s=sem, v=val: eng.wait_ge(s, v))

    def op(self, engine, method, *args, inc=None, **kwargs):
        if inc is None:
            self.q[engine].append(
                lambda eng, m=method, a=args, kw=kwargs: getattr(eng, m)(*a, **kw))
            return None
        self.ctr[inc] += self.step[inc]
        sem, k = self.sem[inc], self.step[inc]
        self.q[engine].append(
            lambda eng, m=method, a=args, kw=kwargs, sem=sem, k=k:
            getattr(eng, m)(*a, **kw).then_inc(sem, k))
        return self.ctr[inc]

    def drain_inc(self, engine, inc):
        return self.op(engine, "drain", inc=inc)

    def dma(self, engine, qname, outap, inap):
        qname = self.qctr(qname)
        self.wait(engine, qname)
        return self.op(engine, "dma_start", inc=qname, out=outap, in_=inap)

    def tok(self, name):
        return self.ctr[name]

    def barrier(self, engines=None):
        engines = engines or self.ENGINES
        for e in engines:
            for c_ in list(self.ctr):
                self.wait(e, c_)


def ceil_div(a, b):
    return (a + b - 1) // b


_uid = [0]


def _u():
    _uid[0] += 1
    return _uid[0]


def build(n_chunks=NCHUNK, n_groups=NG, n_experts=NE, debug_dumps=()):
    nc = bass.Bass("TRN2", target_bir_lowering=False, debug=False)

    x_d = nc.dram_tensor("x", [S, H], F32, kind="ExternalInput").ap()
    wq_d = nc.dram_tensor("wq", [H, H], F32, kind="ExternalInput").ap()
    wk_d = nc.dram_tensor("wk", [H, H], F32, kind="ExternalInput").ap()
    wv_d = nc.dram_tensor("wv", [H, H], F32, kind="ExternalInput").ap()
    wo_d = nc.dram_tensor("wo", [H, H], F32, kind="ExternalInput").ap()
    gw_d = nc.dram_tensor("gw", [NE, H], F32, kind="ExternalInput").ap()
    wg_d = nc.dram_tensor("wgate", [NE, H, DI], F32, kind="ExternalInput").ap()
    wu_d = nc.dram_tensor("wup", [NE, H, DI], F32, kind="ExternalInput").ap()
    wd_d = nc.dram_tensor("wdown", [NE, DI, H], F32, kind="ExternalInput").ap()
    cq_d = nc.dram_tensor("ropec", [HD, S], BF16, kind="ExternalInput").ap()
    sq_d = nc.dram_tensor("ropes", [HD, S], BF16, kind="ExternalInput").ap()
    dmask_d = nc.dram_tensor("dmask", [128, 128], F32, kind="ExternalInput").ap()
    dmaskb_d = nc.dram_tensor("dmaskb", [128, 128], BF16, kind="ExternalInput").ap()
    idf_d = nc.dram_tensor("idf", [128, 128], F32, kind="ExternalInput").ap()
    idb_d = nc.dram_tensor("idb", [128, 128], BF16, kind="ExternalInput").ap()
    ones_d = nc.dram_tensor("ones", [128, 128], F32, kind="ExternalInput").ap()
    out_d = nc.dram_tensor("out", [S, H], F32, kind="ExternalOutput").ap()
    dbg = {}
    for name in debug_dumps:
        if name:
            dbg[name] = nc.dram_tensor(name, [128, NHT * TW], F32,
                                       kind="ExternalOutput").ap()

    xbf_d = nc.dram_tensor("xbf_i", [H, S], BF16, kind="Internal").ap()
    wt_d = nc.dram_tensor("wt_i", [4, NHT, NHT, 128, 128], BF16, kind="Internal").ap()

    ctx = ExitStack()
    sb = lambda n, s, d: ctx.enter_context(nc.sbuf_tensor(n, s, d))
    p = Plan(nc, ctx)

    acc = sb("acc", [128, NHT * TW], F32)
    idf = sb("idf_t", [128, 128], F32)
    idb = sb("idb_t", [128, 128], BF16)
    ones = sb("ones_t", [128, 128], F32)
    dmask = sb("dmask_t", [128, 128], F32)
    dmaskb = sb("dmaskb_t", [128, 128], BF16)
    gwT = sb("gwT", [128, NHT * NE], F32)

    psA = ctx.enter_context(nc.psum_tensor("psA", [128, 2048], F32))
    psB = ctx.enter_context(nc.psum_tensor("psB", [128, 512], F32))
    psT = ctx.enter_context(nc.psum_tensor("psT", [128, 2048], BF16))
    psC = ctx.enter_context(nc.psum_tensor("psC", [128, 512], F32))

    xv = x_d.rearrange("(tt pp) h -> tt pp h", pp=128)
    ov = out_d.rearrange("(tt pp) h -> tt pp h", pp=128)
    xbv = xbf_d.rearrange("(hh pp) t -> hh pp t", pp=128)

    p.dma("sync", "qc", idf[:], idf_d)
    p.dma("sync", "qc", idb[:], idb_d)
    p.dma("sync", "qc", ones[:], ones_d)
    p.dma("sync", "qc", dmask[:], dmask_d)
    t_cst = p.dma("sync", "qc", dmaskb[:], dmaskb_d)

    def acc_slice(hh, lo, w):
        return acc[:, hh * TW + lo: hh * TW + lo + w]

    with ExitStack() as ps_:
        psb_ = lambda n, s, d: ps_.enter_context(nc.sbuf_tensor(n, s, d))
        xt = [psb_(f"pr_xt{i}", [128, H], F32) for i in range(2)]
        xb = [psb_(f"pr_xb{i}", [128, H], BF16) for i in range(2)]
        stg = [psb_(f"pr_st{i}", [128, NHT * 128], BF16) for i in range(2)]
        gwsb = psb_("pr_gw", [NE, H], F32)
        p.wait("tensor", "qc", t_cst)

        t_gw = p.dma("sync", "qc", gwsb[:], gw_d)
        p.wait("tensor", "qc", t_gw)
        for kk in range(NHT):
            t_tr = p.op("tensor", "matmul",
                        psB[:, 0:128],
                        gwsb[:, kk * 128:(kk + 1) * 128], idf[0:NE, :],
                        start=True, stop=True, inc="pe")
            p.wait("vector", "pe", t_tr)
            t_ev = p.op("vector", "tensor_copy",
                        out=gwT[:, kk * NE:(kk + 1) * NE],
                        in_=psB[:, 0:NE], inc="dve")
            p.wait("tensor", "dve", t_ev)

        def prep_one(idx, src_rows_f32, dst_view):
            par = idx % 2
            p.wait("sync", "dve", p.tok("dve"))
            p.wait("scalar", p.qctr(f"qst{par}"), p.tok(p.qctr(f"qst{par}")))
            t_load = p.dma("sync", f"qxt{par}", xt[par][:], src_rows_f32)
            p.wait("vector", f"qxt{par}", t_load)
            p.wait("vector", "pe", p.tok("pe"))
            t_cast = p.op("vector", "tensor_copy", out=xb[par][:], in_=xt[par][:],
                          inc="dve")
            p.wait("tensor", "dve", t_cast)
            t_ev = [0, 0]
            for b in range(4):
                half = b % 2
                p.wait("tensor", "act", t_ev[half])
                t_tr = 0
                for i4 in range(4):
                    hh = b * 4 + i4
                    t_tr = p.op("tensor", "transpose",
                                psT[:, half * 1024 + i4 * 128:
                                    half * 1024 + (i4 + 1) * 128],
                                xb[par][:, hh * 128:(hh + 1) * 128], idb[:],
                                inc="pe")
                p.wait("scalar", "pe", t_tr)
                t_ev[half] = p.op("scalar", "copy",
                                  out=stg[par][:, b * 512:(b + 1) * 512],
                                  in_=psT[:, half * 1024:half * 1024 + 512],
                                  inc="act")
            p.wait("gpsimd", "act", max(t_ev))
            p.dma("gpsimd", f"qst{par}", dst_view.rearrange("hh pp cc -> pp hh cc"),
                  stg[par].rearrange("pp (hh cc) -> pp hh cc", hh=NHT))

        for tt in range(NHT):
            prep_one(tt, xv[tt], xbv[:, :, tt * 128:(tt + 1) * 128])
        for wi, wsrc in enumerate([wq_d, wk_d, wv_d, wo_d]):
            wvv = wsrc.rearrange("(ot pp) h -> ot pp h", pp=128)
            for ot in range(NHT):
                prep_one(wi * NHT + ot, wvv[ot], wt_d[wi, ot])
        p.barrier()

    def dump_acc(tag):
        if tag not in dbg:
            return
        with ExitStack() as ds:
            stg2 = ds.enter_context(
                nc.sbuf_tensor(f"dump{_u()}", [128, NHT * TW], F32))
            p.barrier(("vector",))
            t = p.op("vector", "tensor_copy", out=stg2[:], in_=acc[:], inc="dve")
            p.wait("sync", "dve", t)
            t2 = p.dma("sync", "qdp", dbg[tag], stg2[:])
            p.wait("sync", "qdp", t2)
            p.barrier()

    def layernorm():
        u = _u()
        with ExitStack() as ls:
            lsb = lambda n, s, d: ls.enter_context(nc.sbuf_tensor(n, s, d))
            sqs = [lsb(f"ln{u}_sq{i}", [128, TW], F32) for i in range(2)]
            rows = lsb(f"ln{u}_rows", [1, 3 * TW], F32)
            mb = lsb(f"ln{u}_mb", [128, TW], F32)
            rb = lsb(f"ln{u}_rb", [128, TW], F32)
            p.barrier()
            tmm = 0
            for hh in range(NHT):
                par = hh % 2
                p.wait("scalar", "pe", p.tok("pe"))
                t_sq = p.op("scalar", "activation", out=sqs[par][:],
                            in_=acc_slice(hh, 0, TW), func=AF.Square, inc="act")
                for n in range(TW // 512):
                    p.op("tensor", "matmul",
                         psA[0:1, n * 512:(n + 1) * 512],
                         ones[:, 0:1], acc_slice(hh, n * 512, 512),
                         start=(hh == 0), stop=(hh == NHT - 1), inc="pe")
                p.wait("tensor", "act", t_sq)
                for n in range(TW // 512):
                    tmm = p.op("tensor", "matmul",
                               psA[0:1, TW + n * 512: TW + (n + 1) * 512],
                               ones[:, 0:1], sqs[par][:, n * 512:(n + 1) * 512],
                               start=(hh == 0), stop=(hh == NHT - 1), inc="pe")
            p.wait("vector", "pe", tmm)
            p.op("vector", "tensor_scalar_mul", rows[0:1, 0:TW],
                 psA[0:1, 0:TW], 1.0 / H)
            p.op("vector", "tensor_scalar_mul", rows[0:1, TW:2 * TW],
                 psA[0:1, TW:2 * TW], 1.0 / H)
            p.op("vector", "drain")
            p.op("vector", "tensor_mul", rows[0:1, 2 * TW:3 * TW],
                 rows[0:1, 0:TW], rows[0:1, 0:TW])
            p.op("vector", "drain")
            p.op("vector", "tensor_sub", rows[0:1, TW:2 * TW],
                 rows[0:1, TW:2 * TW], rows[0:1, 2 * TW:3 * TW])
            p.op("vector", "drain")
            p.op("vector", "tensor_scalar_add", rows[0:1, TW:2 * TW],
                 rows[0:1, TW:2 * TW], EPS)
            t = p.drain_inc("vector", "dve")
            p.wait("scalar", "dve", t)
            t = p.op("scalar", "activation", out=rows[0:1, TW:2 * TW],
                     in_=rows[0:1, TW:2 * TW], func=AF.Sqrt, inc="act")
            p.wait("vector", "act", t)
            p.op("vector", "reciprocal", rows[0:1, TW:2 * TW],
                 rows[0:1, TW:2 * TW])
            t = p.drain_inc("vector", "dve")
            p.wait("tensor", "dve", t)
            for n in range(TW // 512):
                t1_ = p.op("tensor", "matmul", psB[:, 0:512], ones[0:1, :],
                           rows[0:1, n * 512:(n + 1) * 512],
                           start=True, stop=True, inc="pe")
                p.wait("vector", "pe", t1_)
                t3_ = p.op("vector", "tensor_copy",
                           out=mb[:, n * 512:(n + 1) * 512],
                           in_=psB[:, 0:512], inc="dve")
                p.wait("tensor", "dve", t3_)
                t2_ = p.op("tensor", "matmul", psB[:, 0:512], ones[0:1, :],
                           rows[0:1, TW + n * 512: TW + (n + 1) * 512],
                           start=True, stop=True, inc="pe")
                p.wait("vector", "pe", t2_)
                t3_ = p.op("vector", "tensor_copy",
                           out=rb[:, n * 512:(n + 1) * 512],
                           in_=psB[:, 0:512], inc="dve")
                p.wait("tensor", "dve", t3_)
            for hh in range(NHT):
                p.op("vector", "tensor_sub", acc_slice(hh, 0, TW),
                     acc_slice(hh, 0, TW), mb[:])
                p.op("vector", "tensor_mul", acc_slice(hh, 0, TW),
                     acc_slice(hh, 0, TW), rb[:], inc="dve")
            p.barrier()

    for c in range(n_chunks):
        t0 = c * TW
        kvw = (c + 1) * TW
        nwin = kvw // 512
        nqt = TW // 128

        with ExitStack() as xs_:
            xtm = [xs_.enter_context(
                nc.sbuf_tensor(f"a{c}_xtm{i}", [128, H], F32)) for i in range(2)]
            p.barrier()
            for tt in range(TW // 128):
                par = tt % 2
                p.wait("sync", "pe", p.tok("pe"))
                tl = p.dma("sync", f"qxm{par}", xtm[par][:], xv[t0 // 128 + tt])
                p.wait("tensor", f"qxm{par}", tl)
                for b in range(4):
                    ttr = 0
                    for i4 in range(4):
                        hh = b * 4 + i4
                        ttr = p.op("tensor", "matmul",
                                   psB[:, i4 * 128:(i4 + 1) * 128],
                                   xtm[par][:, hh * 128:(hh + 1) * 128],
                                   idf[:], start=True, stop=True, inc="pe")
                    p.wait("vector", "pe", ttr)
                    tev = 0
                    for i4 in range(4):
                        hh = b * 4 + i4
                        tev = p.op("vector", "tensor_copy",
                                   out=acc_slice(hh, tt * 128, 128),
                                   in_=psB[:, i4 * 128:(i4 + 1) * 128], inc="dve")
                    p.wait("tensor", "dve", tev)
            p.barrier()

        with ExitStack() as as_:
            asb = lambda n, s, d: as_.enter_context(nc.sbuf_tensor(n, s, d))
            cq = asb(f"a{c}_cq", [128, S], BF16)
            sq = asb(f"a{c}_sq", [128, S], BF16)
            ck = asb(f"a{c}_ck", [128, S], BF16)
            sk = asb(f"a{c}_sk", [128, S], BF16)
            kbf = [asb(f"a{c}_k{j}", [128, kvw], BF16) for j in range(GH)]
            vtm = [asb(f"a{c}_v{j}", [128, kvw], BF16) for j in range(GH)]
            qbf = [asb(f"a{c}_q{j}", [128, TW], BF16) for j in range(GH)]
            ctxb = [asb(f"a{c}_c{j}", [128, TW], BF16) for j in range(GH)]
            pbf = asb(f"a{c}_pbf", [128, 8 * 2048], BF16)
            ptsb = [asb(f"a{c}_pt{i}", [128, 512], BF16) for i in range(2)]
            xwin = [asb(f"a{c}_xw0", [128, NHT * 512], BF16)]
            wslab = [asb(f"a{c}_ws{i}", [128, NHT * 128], BF16) for i in range(2)]
            raww = [asb(f"a{c}_rw{i}", [128, 512], F32) for i in range(2)]
            rotw = asb(f"a{c}_rot", [128, 512], F32)
            tmpw = asb(f"a{c}_tmp", [128, 512], F32)
            vwin = asb(f"a{c}_vw", [128, 512], BF16)
            biasc = asb(f"a{c}_bias", [128, 8], F32)
            esum = asb(f"a{c}_esum", [128, 8], F32)
            rcol = asb(f"a{c}_rcol", [128, 8], F32)
            unit_ctr = [0]
            ctx_pe_done = [0, 0]
            psa_exp_tok = [0]
            psa_mask_tok = [0]
            psT_free = [0, 0]
            psb_act_tok = [0]
            psb_dve_tok = [0]
            vpar = [0]

            p.barrier()
            p.dma("sync", "qtab", cq[:], cq_d)
            t_t = p.dma("sync", "qtab", sq[:], sq_d)
            p.wait("vector", "qtab", t_t)
            p.op("vector", "tensor_scalar_mul", ck[:], cq[:], ISQ)
            t_tab = p.op("vector", "tensor_scalar_mul", sk[:], sq[:], ISQ,
                         inc="dve")
            p.wait("tensor", "dve", t_tab)

            slab_last = [0, 0]
            xw_last = [0, 0]
            slab_i = [0]

            def do_proj(widx, j, dst_kind, parw, nlo, g):
                pars = slab_i[0] % 2
                slab_i[0] += 1
                p.wait("gpsimd", "pe", slab_last[pars])
                p.dma("gpsimd", f"qws{pars}",
                      wslab[pars].rearrange("pp (hh cc) -> pp hh cc", hh=NHT),
                      wt_d[widx, g * GH + j].rearrange("hh pp cc -> pp hh cc"))
                p.wait("tensor", f"qws{pars}", p.tok(f"qws{pars}"))
                p.wait("tensor", "act", psb_act_tok[0])
                p.wait("tensor", "dve", psb_dve_tok[0])
                t_mm = 0
                for kk in range(NHT):
                    t_mm = p.op("tensor", "matmul",
                                psB[:, 0:512],
                                wslab[pars][:, kk * 128:(kk + 1) * 128],
                                xwin[parw][:, kk * 512:(kk + 1) * 512],
                                start=(kk == 0), stop=(kk == NHT - 1), inc="pe")
                slab_last[pars] = p.tok("pe")
                p.wait("scalar", "pe", t_mm)
                if dst_kind == "v":
                    t_ev = p.op("scalar", "copy", out=vwin[:],
                                in_=psB[:, 0:512], inc="act")
                    psb_act_tok[0] = t_ev
                    p.wait("tensor", "act", t_ev)
                    hv = vpar[0] % 2
                    vpar[0] += 1
                    p.wait("tensor", "act", psT_free[hv])
                    t_tr = 0
                    for tt4 in range(4):
                        t_tr = p.op("tensor", "transpose",
                                    psT[:, hv * 1024 + tt4 * 128:
                                        hv * 1024 + (tt4 + 1) * 128],
                                    vwin[:, tt4 * 128:(tt4 + 1) * 128],
                                    idb[:], inc="pe")
                    p.wait("scalar", "pe", t_tr)
                    psT_free[hv] = p.op("scalar", "copy",
                                        out=vtm[j][:, nlo:nlo + 512],
                                        in_=psT[:, hv * 1024:hv * 1024 + 512],
                                        inc="act")
                else:
                    parr = slab_i[0] % 2
                    p.wait("scalar", "dve", p.tok("dve"))
                    t_ev = p.op("scalar", "copy", out=raww[parr][:],
                                in_=psB[:, 0:512], inc="act")
                    psb_act_tok[0] = t_ev
                    p.wait("vector", "act", t_ev)
                    X = raww[parr]
                    C = cq if dst_kind == "q" else ck
                    Sn = sq if dst_kind == "q" else sk
                    dst = (qbf[j][:, nlo - t0:nlo - t0 + 512] if dst_kind == "q"
                           else kbf[j][:, nlo:nlo + 512])
                    p.op("vector", "tensor_scalar_mul", rotw[0:64, :],
                         X[64:128, :], -1.0)
                    p.op("vector", "tensor_copy", out=rotw[64:128, :],
                         in_=X[0:64, :])
                    p.op("vector", "tensor_mul", tmpw[:], X[:],
                         C[:, nlo:nlo + 512])
                    p.op("vector", "tensor_mul", rotw[:], rotw[:],
                         Sn[:, nlo:nlo + 512])
                    p.op("vector", "tensor_add", dst, tmpw[:], rotw[:], inc="dve")

            for g in range(n_groups):
                for n in range(nwin):
                    nlo = n * 512
                    parw = 0
                    p.wait("sync", "pe", xw_last[parw])
                    t_xw = p.dma("sync", "qxw0",
                                 xwin[parw].rearrange("pp (kk tt2) -> pp kk tt2",
                                                      kk=NHT),
                                 xbv[:, :, nlo:nlo + 512].rearrange(
                                     "hh pp tt2 -> pp hh tt2"))
                    p.wait("tensor", "qxw0", t_xw)
                    for j in range(GH):
                        do_proj(1, j, "k", parw, nlo, g)
                    for j in range(GH):
                        do_proj(2, j, "v", parw, nlo, g)
                    if nlo >= t0:
                        for j in range(GH):
                            do_proj(0, j, "q", parw, nlo, g)
                    xw_last[parw] = p.tok("pe")
                    if XBAR & 4:
                        p.barrier()

                def emit_sm_qt(j, qc, jj, par):
                    sl = par * 4 + jj
                    qt = qc * 4 + jj
                    qi = c * nqt + qt
                    klen = (qi + 1) * 128
                    kmaxt = c * nqt + qc * 4 + 4
                    nch = ceil_div(klen, 512)
                    p.wait("tensor", "act", psa_exp_tok[0])
                    p.wait("tensor", "dve", psa_mask_tok[0])
                    t_mm = 0
                    for nn in range(nch):
                        w = min(512, klen - nn * 512)
                        p.op("tensor", "matmul",
                             psA[:, nn * 512:nn * 512 + w],
                             qbf[j][:, qt * 128:(qt + 1) * 128],
                             kbf[j][:, nn * 512:nn * 512 + w],
                             start=True, stop=(nn < nch - 1), inc="pe")
                    t_mm = p.op("tensor", "matmul", psA[:, qi * 128:klen],
                                idb[:], dmaskb[:], start=False, stop=True,
                                inc="pe")
                    p.wait("vector", "pe", t_mm)
                    p.op("vector", "reduce_max", biasc[:, sl:sl + 1],
                         psA[:, 0:klen], axis=AX.X, negate=True)
                    t_b = p.drain_inc("vector", "dve")
                    psa_mask_tok[0] = t_b
                    p.wait("scalar", "dve", t_b)
                    p.wait("scalar", "pe", ctx_pe_done[par])
                    t_e = p.op("scalar", "activation",
                               out=pbf[:, sl * 2048:sl * 2048 + klen],
                               in_=psA[:, 0:klen], func=AF.Exp,
                               bias=biasc[:, sl:sl + 1], scale=1.0,
                               accum_out=esum[:, sl:sl + 1], inc="act")
                    psa_exp_tok[0] = t_e
                    p.wait("vector", "act", t_e)
                    p.op("vector", "reciprocal", rcol[:, sl:sl + 1],
                         esum[:, sl:sl + 1])
                    p.op("vector", "drain")
                    kmax = kmaxt * 128
                    if klen < kmax:
                        p.op("vector", "memset",
                             pbf[:, sl * 2048 + klen:sl * 2048 + kmax], 0.0)
                    p.op("vector", "tensor_scalar_mul",
                         pbf[:, sl * 2048:sl * 2048 + klen],
                         pbf[:, sl * 2048:sl * 2048 + klen],
                         rcol[:, sl:sl + 1], inc="dve")

                def emit_ctx_portion(st, kj_from, kj_to):
                    j, qc, par, kmaxt, dve_tok = st
                    p.wait("tensor", "dve", dve_tok)
                    t_cx = 0
                    for kj in range(kj_from, kj_to):
                        half = kj % 2
                        p.wait("tensor", "act", psT_free[half])
                        t_tr = 0
                        for jj in range(4):
                            sl = par * 4 + jj
                            t_tr = p.op("tensor", "transpose",
                                        psT[:, half * 1024 + jj * 128:
                                            half * 1024 + (jj + 1) * 128],
                                        pbf[:, sl * 2048 + kj * 128:
                                            sl * 2048 + (kj + 1) * 128],
                                        idb[:], inc="pe")
                        p.wait("scalar", "pe", t_tr)
                        t_ev = p.op("scalar", "copy", out=ptsb[half][:],
                                    in_=psT[:, half * 1024:half * 1024 + 512],
                                    inc="act")
                        psT_free[half] = t_ev
                        p.wait("tensor", "act", t_ev)
                        t_cx = p.op("tensor", "matmul", psC[:],
                                    vtm[j][:, kj * 128:(kj + 1) * 128],
                                    ptsb[half][:],
                                    start=(kj == 0), stop=(kj == kmaxt - 1),
                                    inc="pe")
                    if kj_to == kmaxt:
                        ctx_pe_done[par] = p.tok("pe")
                        p.wait("vector", "pe", t_cx)
                        t_cv = p.op("vector", "tensor_copy",
                                    out=ctxb[j][:, qc * 512:(qc + 1) * 512],
                                    in_=psC[:], inc="dve")
                        p.wait("tensor", "dve", t_cv)

                prev = None
                for j in range(GH):
                    for qc in range(TW // 512):
                        kmaxt = c * nqt + qc * 4 + 4
                        par = unit_ctr[0] % 2
                        unit_ctr[0] += 1
                        splits = None
                        if prev is not None:
                            km = prev[3]
                            splits = [0, km // 4, km // 2, (3 * km) // 4, km]
                        for jj in range(4):
                            emit_sm_qt(j, qc, jj, par)
                            if splits is not None:
                                emit_ctx_portion(prev, splits[jj], splits[jj + 1])
                        prev = (j, qc, par, kmaxt, p.tok("dve"))
                if prev is not None:
                    emit_ctx_portion(prev, 0, prev[3])
                    prev = None

                for m in range(NHT):
                    p.wait("gpsimd", "pe", p.tok("pe"))
                    p.dma("gpsimd", "qws0",
                          wslab[0][:, 0:GH * 128].rearrange(
                              "pp (jh cc) -> pp jh cc", jh=GH),
                          wt_d[3, m, g * GH:(g + 1) * GH].rearrange(
                              "jh pp cc -> pp jh cc"))
                    p.wait("tensor", "qws0", p.tok("qws0"))
                    for n in range(TW // 512):
                        p.wait("tensor", "act", psb_act_tok[0])
                        p.wait("tensor", "dve", psb_dve_tok[0])
                        t_mm = 0
                        for j in range(GH):
                            t_mm = p.op("tensor", "matmul",
                                        psB[:, 0:512],
                                        wslab[0][:, j * 128:(j + 1) * 128],
                                        ctxb[j][:, n * 512:(n + 1) * 512],
                                        start=(j == 0), stop=(j == GH - 1),
                                        inc="pe")
                        p.wait("vector", "pe", t_mm)
                        psb_dve_tok[0] = p.op("vector", "tensor_add",
                             acc_slice(m, n * 512, 512),
                             acc_slice(m, n * 512, 512),
                             psB[:, 0:512], inc="dve")
                slab_last[0] = slab_last[1] = p.tok("pe")
            p.barrier()
        dump_acc(f"dbg_res1_{c}")

        layernorm()
        dump_acc(f"dbg_h1_{c}")

        with ExitStack() as fs_:
            fsb = lambda n, s, d: fs_.enter_context(nc.sbuf_tensor(n, s, d))
            h1bf = fsb(f"f{c}_h1", [128, NHT * TW], BF16)
            w_em = fsb(f"f{c}_wem", [NE, TW], F32)
            zrow = fsb(f"f{c}_z", [128, NE], F32)
            m8 = fsb(f"f{c}_m8", [128, 8], F32)
            mrow = fsb(f"f{c}_msk", [128, NE], F32)
            srow = fsb(f"f{c}_srow", [128, 4], F32)
            wrow = fsb(f"f{c}_wrow", [1, TW], F32)
            wbc = [fsb(f"f{c}_wbc{i}", [128, TW], F32) for i in range(2)]
            gslab = [fsb(f"f{c}_gs{i}", [128, NHT * 128], BF16) for i in range(2)]
            uslab = [fsb(f"f{c}_us{i}", [128, NHT * 128], BF16) for i in range(2)]
            dslab = [[fsb(f"f{c}_ds{e4}_{i}", [128, NIT * 128], BF16)
                      for i in range(2)] for e4 in range(EG)]
            t1 = [fsb(f"f{c}_t1{i}", [128, TW], F32) for i in range(2)]
            gu = [fsb(f"f{c}_gu{e4}", [128, NIT * TW], BF16) for e4 in range(EG)]

            for hh in range(NHT):
                p.op("vector", "tensor_copy",
                     out=h1bf[:, hh * TW:(hh + 1) * TW],
                     in_=acc_slice(hh, 0, TW), inc="dve")

            for tt in range(TW // 128):
                reg = (tt % 2) * NE
                p.wait("tensor", "act", p.tok("act"))
                t_lg = 0
                for kk in range(NHT):
                    t_lg = p.op("tensor", "matmul", psC[:, reg:reg + NE],
                                acc_slice(kk, tt * 128, 128),
                                gwT[:, kk * NE:(kk + 1) * NE],
                                start=(kk == 0), stop=(kk == NHT - 1), inc="pe")
                p.wait("vector", "pe", t_lg)
                p.op("vector", "reduce_max", srow[:, 0:1], psC[:, reg:reg + NE],
                     axis=AX.X, negate=True)
                t_b = p.drain_inc("vector", "dve")
                p.wait("scalar", "dve", t_b)
                t_z = p.op("scalar", "activation", out=zrow[:],
                           in_=psC[:, reg:reg + NE], func=AF.Exp,
                           bias=srow[:, 0:1], scale=1.0, inc="act")
                p.wait("vector", "act", t_z)
                p.op("vector", "max", out=m8[:], in_=zrow[:])
                p.op("vector", "drain")
                p.op("vector", "tensor_single_scalar", mrow[:], zrow[:],
                     m8[:, 7:8], OP.is_ge)
                p.op("vector", "drain")
                p.op("vector", "tensor_mul", mrow[:], mrow[:], zrow[:])
                p.op("vector", "drain")
                p.op("vector", "reduce_sum", srow[:, 1:2], mrow[:], axis=AX.X)
                p.op("vector", "drain")
                p.op("vector", "reciprocal", srow[:, 2:3], srow[:, 1:2])
                p.op("vector", "drain")
                t_w = p.op("vector", "tensor_scalar_mul", mrow[:], mrow[:],
                           srow[:, 2:3], inc="dve")
                p.wait("tensor", "dve", t_w)
                t_tr = p.op("tensor", "matmul", psB[0:NE, 0:128], mrow[:],
                            idf[:], start=True, stop=True, inc="pe")
                p.wait("vector", "pe", t_tr)
                p.op("vector", "tensor_copy",
                     out=w_em[:, tt * 128:(tt + 1) * 128],
                     in_=psB[0:NE, 0:128], inc="dve")
            p.barrier(("tensor", "vector", "scalar", "gpsimd"))

            wgv = wg_d.rearrange("e (kk pp) i -> e kk pp i", pp=128)
            wuv = wu_d.rearrange("e (kk pp) i -> e kk pp i", pp=128)
            wdv = wd_d.rearrange("e (kk pp) h -> e kk pp h", pp=128)
            gs_last = [0, 0]
            ds_last = {}
            gu_last = [0] * EG
            silu_tok = [0]
            gum_tok = [0]
            dacc_tok = [0, 0]
            for egi in range(n_experts // EG):
                for e4 in range(EG):
                    e = egi * EG + e4
                    parw = e % 2
                    p.wait("gpsimd", "pe", p.tok("pe"))
                    t_wr = p.dma("gpsimd", "qwr", wrow[:], w_em[e:e + 1, :])
                    p.wait("tensor", "qwr", t_wr)
                    for n in range(TW // 512):
                        p.wait("tensor", "dve", dacc_tok[1])
                        t_bc = p.op("tensor", "matmul", psC[:], ones[0:1, :],
                                    wrow[0:1, n * 512:(n + 1) * 512],
                                    start=True, stop=True, inc="pe")
                        p.wait("vector", "pe", t_bc)
                        t_bcv = p.op("vector", "tensor_copy",
                             out=wbc[parw][:, n * 512:(n + 1) * 512],
                             in_=psC[:], inc="dve")
                        p.wait("tensor", "dve", t_bcv)
                    for m in range(NIT):
                        pars = m % 2
                        p.wait("gpsimd", "pe", gs_last[pars])
                        t_g = p.dma("gpsimd", f"qgs{pars}",
                                    gslab[pars].rearrange("pp (kk ii) -> pp kk ii",
                                                          kk=NHT),
                                    wgv[e, :, :, m * 128:(m + 1) * 128].rearrange(
                                        "kk pp ii -> pp kk ii"))
                        t_u = p.dma("gpsimd", f"qus{pars}",
                                    uslab[pars].rearrange("pp (kk ii) -> pp kk ii",
                                                          kk=NHT),
                                    wuv[e, :, :, m * 128:(m + 1) * 128].rearrange(
                                        "kk pp ii -> pp kk ii"))
                        p.wait("tensor", f"qgs{pars}", t_g)
                        p.wait("tensor", "act", silu_tok[0])
                        t_gm = 0
                        for kk in range(NHT):
                            for n in range(TW // 512):
                                t_gm = p.op("tensor", "matmul",
                                            psA[:, n * 512:(n + 1) * 512],
                                            gslab[pars][:, kk * 128:(kk + 1) * 128],
                                            h1bf[:, kk * TW + n * 512:
                                                 kk * TW + (n + 1) * 512],
                                            start=(kk == 0), stop=(kk == NHT - 1),
                                            inc="pe")
                        p.wait("scalar", "pe", t_gm)
                        part = m % 2
                        p.wait("scalar", "dve", p.tok("dve"))
                        t_si = p.op("scalar", "activation", out=t1[part][:],
                                    in_=psA[:, 0:TW], func=AF.Silu, inc="act")
                        silu_tok[0] = t_si
                        p.wait("tensor", f"qus{pars}", t_u)
                        p.wait("tensor", "dve", gum_tok[0])
                        t_um = 0
                        for kk in range(NHT):
                            for n in range(TW // 512):
                                t_um = p.op("tensor", "matmul",
                                            psA[:, TW + n * 512:TW + (n + 1) * 512],
                                            uslab[pars][:, kk * 128:(kk + 1) * 128],
                                            h1bf[:, kk * TW + n * 512:
                                                 kk * TW + (n + 1) * 512],
                                            start=(kk == 0), stop=(kk == NHT - 1),
                                            inc="pe")
                        gs_last[pars] = p.tok("pe")
                        p.wait("vector", "act", t_si)
                        p.wait("vector", "pe", t_um)
                        p.wait("vector", "pe", gu_last[e4])
                        p.op("vector", "tensor_mul", t1[part][:], t1[part][:],
                             wbc[parw][:])
                        gum_tok[0] = p.op("vector", "tensor_mul",
                             gu[e4][:, m * TW:(m + 1) * TW],
                             t1[part][:], psA[:, TW:2 * TW], inc="dve")
                    if XBAR & 1:
                        p.barrier()
                for m2 in range(NHT):
                    par2 = m2 % 2
                    for e4 in range(EG):
                        e = egi * EG + e4
                        p.wait("gpsimd", "pe", ds_last.get((e4, par2), 0))
                        t_d = p.dma("gpsimd", f"qds{e4}_{par2}",
                                    dslab[e4][par2].rearrange(
                                        "pp (kk hh2) -> pp kk hh2", kk=NIT),
                                    wdv[e, :, :, m2 * 128:(m2 + 1) * 128].rearrange(
                                        "kk pp hh2 -> pp kk hh2"))
                        p.wait("tensor", f"qds{e4}_{par2}", t_d)
                    p.wait("tensor", "dve", gum_tok[0])
                    for n in range(TW // 512):
                        p.wait("tensor", "dve", dacc_tok[n])
                        t_dm = 0
                        for e4 in range(EG):
                            for kk in range(NIT):
                                t_dm = p.op("tensor", "matmul",
                                            psB[:, 0:512] if n == 0 else psC[:],
                                            dslab[e4][par2][:,
                                                            kk * 128:(kk + 1) * 128],
                                            gu[e4][:, kk * TW + n * 512:
                                                   kk * TW + (n + 1) * 512],
                                            start=(e4 == 0 and kk == 0),
                                            stop=(e4 == EG - 1 and kk == NIT - 1),
                                            inc="pe")
                        p.wait("vector", "pe", t_dm)
                        dacc_tok[n] = p.op("vector", "tensor_add",
                             acc_slice(m2, n * 512, 512),
                             acc_slice(m2, n * 512, 512),
                             psB[:, 0:512] if n == 0 else psC[:], inc="dve")
                    for e4 in range(EG):
                        ds_last[(e4, par2)] = p.tok("pe")
                    if XBAR & 2:
                        p.barrier()
                gu_last = [p.tok("pe")] * EG
            p.barrier()

        layernorm()
        dump_acc(f"dbg_h2_{c}")

        with ExitStack() as os_:
            ost = [os_.enter_context(
                nc.sbuf_tensor(f"o{c}_st{i}", [128, H], F32)) for i in range(2)]
            st_last = [0, 0]
            for tt in range(TW // 128):
                par = tt % 2
                p.wait("vector", p.qctr(f"qos{par}"), st_last[par])
                t_ev = 0
                for b in range(4):
                    t_tr = 0
                    for i4 in range(4):
                        hh = b * 4 + i4
                        t_tr = p.op("tensor", "matmul",
                                    psB[:, i4 * 128:(i4 + 1) * 128],
                                    acc_slice(hh, tt * 128, 128), idf[:],
                                    start=True, stop=True, inc="pe")
                    p.wait("vector", "pe", t_tr)
                    for i4 in range(4):
                        hh = b * 4 + i4
                        t_ev = p.op("vector", "tensor_copy",
                                    out=ost[par][:, hh * 128:(hh + 1) * 128],
                                    in_=psB[:, i4 * 128:(i4 + 1) * 128],
                                    inc="dve")
                    p.wait("tensor", "dve", t_ev)
                p.wait("sync", "dve", t_ev)
                st_last[par] = p.dma("sync", f"qos{par}", ov[t0 // 128 + tt],
                                     ost[par][:])
            p.barrier()

    for qn in list(p.ctr):
        if qn.startswith("qos") or qn == "qdp":
            p.wait("sync", qn, p.tok(qn))

    with nc.Block() as block:
        for ename in Plan.ENGINES:
            getattr(block, ename)(lambda eng, q=p.q[ename]: [f(eng) for f in q])
    ctx.close()
    return nc


_ROPE_BASE = 10000.0


def _host_constants():
    half = HD // 2
    inv = (1.0 / _ROPE_BASE ** (np.arange(0, HD, 2, dtype=np.float32) / HD))
    t = np.arange(S, dtype=np.float32)
    freqs = np.einsum("i,j->ij", t, inv.astype(np.float32)).astype(np.float32)
    cos = np.cos(freqs).astype(np.float32)
    sin = np.sin(freqs).astype(np.float32)
    C = np.empty((HD, S), np.float32)
    Sn = np.empty((HD, S), np.float32)
    C[:half] = cos.T
    C[half:] = cos.T
    Sn[:half] = sin.T
    Sn[half:] = sin.T
    return {
        "ropec": C.astype(ml_dtypes.bfloat16),
        "ropes": Sn.astype(ml_dtypes.bfloat16),
        "dmask": np.triu(np.full((128, 128), NEG, np.float32), k=1),
        "dmaskb": np.triu(np.full((128, 128), NEG, np.float32), k=1).astype(
            ml_dtypes.bfloat16),
        "idf": np.eye(128, dtype=np.float32),
        "idb": np.eye(128).astype(ml_dtypes.bfloat16),
        "ones": np.ones((128, 128), np.float32),
    }


def kernel(L0_attn_hidden_states, L0_attn_Wq, L0_attn_Wk, L0_attn_Wv, L0_attn_Wo,
           L0_ffn_gate_w, L0_ffn_w_gate, L0_ffn_w_up, L0_ffn_w_down):
    x = np.ascontiguousarray(np.asarray(L0_attn_hidden_states, dtype=np.float32))
    shared = {
        "wq": np.ascontiguousarray(np.asarray(L0_attn_Wq, np.float32)),
        "wk": np.ascontiguousarray(np.asarray(L0_attn_Wk, np.float32)),
        "wv": np.ascontiguousarray(np.asarray(L0_attn_Wv, np.float32)),
        "wo": np.ascontiguousarray(np.asarray(L0_attn_Wo, np.float32)),
        "gw": np.ascontiguousarray(np.asarray(L0_ffn_gate_w, np.float32)),
        "wgate": np.ascontiguousarray(np.asarray(L0_ffn_w_gate, np.float32)),
        "wup": np.ascontiguousarray(np.asarray(L0_ffn_w_up, np.float32)),
        "wdown": np.ascontiguousarray(np.asarray(L0_ffn_w_down, np.float32)),
        **_host_constants(),
    }
    nc = build()
    in_maps = [dict(shared, x=np.ascontiguousarray(x[b])) for b in range(8)]
    res = run_bass_kernel_spmd(nc, in_maps, core_ids=list(range(8)))
    return np.stack([r["out"] for r in res.results], axis=0)
